# version 05
# baseline (speedup 1.0000x reference)
import contextlib
import numpy as np
import ml_dtypes
import concourse.bass as bass
import concourse.mybir as mybir
from concourse.bass_utils import run_bass_kernel_spmd

F32 = mybir.dt.float32
BF16 = mybir.dt.bfloat16
AF = mybir.ActivationFunctionType
ALU = mybir.AluOpType
AX = mybir.AxisListType
BFDT = ml_dtypes.bfloat16

DEBUG = False
SERIAL = False
NCORE = 8
D = 1024
NCH = D // 128
CH = 256
RG = [list(range(NCORE))]
f32 = np.float32


class Sched:
    ENGS = ("sync", "gpsimd", "vector", "scalar", "tensor")
    ESEM = {"sync": "sy", "gpsimd": "gp", "vector": "ve", "scalar": "sc", "tensor": "pe"}
    DSEM = {"sync": "dsy", "gpsimd": "dgp", "vector": "dve", "scalar": "dsc"}

    def __init__(self):
        self.ops = []
        self.keys = {}

    def op(self, eng, key, fn, waits=()):
        self.ops.append((eng, fn, tuple(w for w in waits if w), key, 1, self.ESEM[eng]))

    def dma(self, q, key, out, in_, waits=()):
        self.ops.append((q, (lambda e, o=out, i=in_: e.dma_start(out=o, in_=i)),
                         tuple(w for w in waits if w), key, 16, self.DSEM[q]))

    def cc(self, key, fn, waits=()):
        self.ops.append(("gpsimd", fn, tuple(w for w in waits if w), key, 1, "cc"))

    def finalize(self, nc, stack, serial=False):
        if serial:
            ops2 = []
            prev = None
            for n, (eng, fn, waits, key, inc, sem) in enumerate(self.ops):
                if key is None:
                    key = f"__auto{n}"
                waits = tuple(waits) + ((prev,) if prev else ())
                ops2.append((eng, fn, waits, key, inc, sem))
                prev = key
            self.ops = ops2
        sems = {s: stack.enter_context(nc.semaphore("s_" + s))
                for s in set(list(self.ESEM.values()) + list(self.DSEM.values()) + ["cc"])}
        ctr = {s: 0 for s in sems}
        chained = []
        for eng, fn, waits, key, inc, sem in self.ops:
            cw = None
            if key is not None:
                if sem not in self.ESEM.values() and ctr[sem] > 0:
                    cw = (sem, ctr[sem])
                ctr[sem] += inc
                assert key not in self.keys, f"dup key {key}"
                self.keys[key] = (sem, ctr[sem])
            chained.append(cw)
        self.chainw = chained
        for eng2, fn2, waits2, key2, inc2, sem2 in self.ops:
            for w in waits2:
                assert w in self.keys, f"wait on undefined key {w}"
        block = stack.enter_context(nc.Block())

        def mk_stream(eng):
            myops = [o for o in self.ops if o[0] == eng]
            mysem = self.ESEM[eng]

            mychain = [self.chainw[n] for n, o in enumerate(self.ops) if o[0] == eng]

            def stream(e):
                hw = {s: 0 for s in sems}
                for (_, fn, waits, key, inc, sem), cw in zip(myops, mychain):
                    for w in waits:
                        ws, wv = self.keys[w]
                        if wv <= hw[ws]:
                            continue
                        e.wait_ge(sems[ws], wv)
                        hw[ws] = wv
                    if cw is not None and cw[1] > hw[cw[0]]:
                        e.wait_ge(sems[cw[0]], cw[1])
                        hw[cw[0]] = cw[1]
                    instr = fn(e)
                    if key is not None:
                        instr.then_inc(sems[sem], inc)
            return stream

        for eng in self.ENGS:
            getattr(block, eng)(mk_stream(eng))


def build(NU):
    nc = bass.Bass("TRN2", target_bir_lowering=False)
    NT = 2 * NU

    def din(name, shape, dt):
        return nc.dram_tensor(name, shape, dt, kind="ExternalInput").ap()

    x_own = din("x_own", [NT, CH, D], F32)
    WNAMES = ("kT", "qT", "vT", "fT", "iT", "cT", "ufT", "uiT", "ucT", "oT", "uoT")
    wT = {m: din("w_" + m, [D, D], F32) for m in WNAMES}
    bkq_in = din("bkq", [128, 2 * NCH], F32)
    brows_in = din("brows", [5, D], F32)
    incrow_in = din("incrow", [1, D], F32)
    incol_in = din("incol", [128, NCH], F32)
    masks_in = din("masks", [16, 128, CH], BF16)
    ident_in = din("ident", [128, 128], BF16)
    ocol_in = din("ocol", [128, 1], BF16)
    orf_in = din("orow_f", [1, 128], F32)
    orb_in = din("orow_b", [1, 128], BF16)

    y_own = nc.dram_tensor("y_own", [NT, CH, D], F32, kind="ExternalOutput").ap()
    if DEBUG:
        d_h = nc.dram_tensor("d_h", [NT, CH, D], F32, kind="ExternalOutput").ap()
        d_acc = nc.dram_tensor("d_acc", [NT, CH, D], F32, kind="ExternalOutput").ap()
        d_xnT = nc.dram_tensor("d_xnT", [NT, NCH, 128, CH], BF16, kind="ExternalOutput").ap()
        d_l = nc.dram_tensor("d_l", [NT, 1, CH], F32, kind="ExternalOutput").ap()
        d_negc = nc.dram_tensor("d_negc", [128, 32], F32, kind="ExternalOutput").ap()
        d_stats = nc.dram_tensor("d_stats", [128, 64], F32, kind="ExternalOutput").ap()
        d_rows = nc.dram_tensor("d_rows", [1, 176], F32, kind="ExternalOutput").ap()
        d_kout = nc.dram_tensor("d_kout", [NT, NCORE * NCH, 128, CH], BF16, kind="ExternalOutput").ap()
        d_vout = nc.dram_tensor("d_vout", [NT, NCORE * 2, 128, D], BF16, kind="ExternalOutput").ap()

    xnT_scr = nc.dram_tensor("xnT_scr", [NT, NCH, 128, CH], BF16, kind="Internal").ap()
    q_scr = nc.dram_tensor("q_scr", [NT, NCH, 128, CH], BF16, kind="Internal").ap()
    h_scr = nc.dram_tensor("h_scr", [NT, CH, D], F32, kind="Internal").ap()
    kin = [nc.dram_tensor(f"kin_{t}", [NCH, 128, CH], BF16, kind="Internal").ap() for t in range(NT)]
    kout = [nc.dram_tensor(f"kout_{t}", [NCORE * NCH, 128, CH], BF16, kind="Internal").ap() for t in range(NT)]
    vin = [nc.dram_tensor(f"vin_{t}", [2, 128, D], BF16, kind="Internal").ap() for t in range(NT)]
    vout = [nc.dram_tensor(f"vout_{t}", [NCORE * 2, 128, D], BF16, kind="Internal").ap() for t in range(NT)]
    mxin = nc.dram_tensor("mxin", [1, 16], F32, kind="Internal").ap()
    mxout = nc.dram_tensor("mxout", [1, 16], F32, kind="Internal").ap()

    with contextlib.ExitStack() as stack:
        E = stack.enter_context
        WCOL = 8 * NCH * 1024
        pbf = E(nc.sbuf_tensor("pbf", [128, WCOL + 11264], BF16))
        pf = E(nc.sbuf_tensor("pf", [128, 11520], F32))
        masks = E(nc.sbuf_tensor("masks_sb", [128, 16 * CH], BF16))
        stats = E(nc.sbuf_tensor("stats", [128, 64], F32))
        negc = E(nc.sbuf_tensor("negc", [128, 32], F32))
        linv = E(nc.sbuf_tensor("linv", [128, 4], F32))
        bkq = E(nc.sbuf_tensor("bkq_sb", [128, 2 * NCH], F32))
        incol = E(nc.sbuf_tensor("incol_sb", [128, NCH], F32))
        incb = E(nc.sbuf_tensor("incb", [128, NCH], BF16))
        ident = E(nc.sbuf_tensor("ident_sb", [128, 128], BF16))
        ocol = E(nc.sbuf_tensor("ocol_sb", [128, 1], BF16))
        orf = E(nc.sbuf_tensor("orf", [1, 128], F32))
        orb = E(nc.sbuf_tensor("orb", [1, 128], BF16))
        psA = E(nc.psum_tensor("psA", [128, 1024], F32))
        psB = E(nc.psum_tensor("psB", [128, 1024], F32))
        psC = E(nc.psum_tensor("psC", [128, 1024], F32))
        psX = E(nc.psum_tensor("psX", [128, 512], F32))
        psY = E(nc.psum_tensor("psY", [128, 512], F32))

        def wslot(i):
            return pbf[:, i * NCH * 1024:(i + 1) * NCH * 1024]
        WK, WQ, WV = wslot(0), wslot(1), wslot(2)
        FW = {"fT": wslot(0), "iT": wslot(1), "cT": wslot(2), "ufT": wslot(3),
              "uiT": wslot(4), "ucT": wslot(5), "oT": wslot(6), "uoT": wslot(7)}
        qbuf = pbf[:, 0:2048]
        ktile = [pbf[:, 4096 + k * 2048: 4096 + (k + 1) * 2048] for k in range(4)]
        vtile = [pbf[:, 12288 + k * 2048: 12288 + (k + 1) * 2048] for k in range(4)]
        pb = [pbf[:, 20480 + k * CH: 20480 + (k + 1) * CH] for k in range(2)]
        xn0T = pbf[:, WCOL + 0: WCOL + 2048]
        xn1T = pbf[:, WCOL + 2048: WCOL + 4096]
        xn_tok = xn1T
        sqb = xn1T
        c1T = pbf[:, WCOL + 4096: WCOL + 5120]
        c2T = pbf[:, WCOL + 5120: WCOL + 6144]
        c1bf = pbf[:, WCOL + 6144: WCOL + 7168]
        keb = [pbf[:, WCOL + 4096 + i * CH: WCOL + 4096 + (i + 1) * CH] for i in range(2)]
        veb = [pbf[:, WCOL + 4608 + i * 512: WCOL + 4608 + (i + 1) * 512] for i in range(2)]
        sqpp = [pbf[:, WCOL + 5632 + i * 256: WCOL + 5632 + (i + 1) * 256] for i in range(2)]
        bvrow = pbf[0:1, WCOL + 7168: WCOL + 8192]
        bprow = {"f": pbf[0:1, WCOL + 7168: WCOL + 8192],
                 "i": pbf[0:1, WCOL + 8192: WCOL + 9216],
                 "c": pbf[0:1, WCOL + 9216: WCOL + 10240]}
        borow = pbf[0:1, WCOL + 10240: WCOL + 11264]

        xa = pf[:, 0:2048]
        xa2 = pf[:, 7168:9216]
        gate = [pf[:, 0:1024], pf[:, 1024:2048], pf[:, 2048:3072]]
        sqd = pf[:, 3072:4096]
        c1f = pf[:, 3072:4096]
        c2f = pf[:, 4096:5120]
        stg = pf[:, 5120:6144]
        stgs = [pf[:, 5120:6144], pf[:, 4096:5120]]
        c0b = pf[:, 6144:7168]
        h0f = pf[:, 7168:8192]
        h1f = pf[:, 8192:9216]
        increg = pf[0:1, 9216:10240]
        brow_f = pf[0:1, 10240:11264]
        accv = pf[:, 0:2048]
        xrt = pf[:, 2048:4096]
        hrt = pf[:, 4096:6144]
        lrow = pf[0:1, 6144:6400]
        mxsb = pf[0:1, 11264:11280]
        garow = pf[0:1, 11280:11296]
        sqga = pf[0:1, 11296:11312]
        gaexp = pf[0:1, 11312:11344]
        prodr = pf[0:1, 11344:11376]
        negrow = pf[0:1, 11376:11408]
        sq1r = pf[0:1, 11408:11440]

        S = Sched()
        war = {}

        def MM(key, out, lhsT, rhs, start, stop, waits=()):
            S.op("tensor", key,
                 lambda e, o=out, l=lhsT, r=rhs, a=start, b=stop:
                 e.matmul(o, l, r, start=a, stop=b, skip_group_check=True), waits)

        def ACT(eng_key, out, in_, func, bias=None, scale=None, accum_out=None, waits=()):
            kw = {}
            if bias is not None:
                kw["bias"] = bias
            if scale is not None:
                kw["scale"] = scale
            if accum_out is not None:
                kw["accum_out"] = accum_out
            S.op("scalar", eng_key,
                 lambda e, o=out, i=in_, f=func, k=kw: e.activation(out=o, in_=i, func=f, **k), waits)

        def CP(key, out, in_, waits=()):
            S.op("scalar", key, lambda e, o=out, i=in_: e.copy(o, i), waits)

        def VCP(key, out, in_, waits=()):
            S.op("vector", key, lambda e, o=out, i=in_: e.tensor_copy(o, i), waits)

        def TT(key, out, in0, in1, op, waits=()):
            S.op("vector", key, lambda e, o=out, a=in0, b=in1, p=op: e.tensor_tensor(out=o, in0=a, in1=b, op=p), waits)

        def TS(key, out, in0, s1, s2, op0, op1=None, waits=()):
            s1 = float(s1) if isinstance(s1, np.floating) else s1
            s2 = float(s2) if isinstance(s2, np.floating) else s2
            if op1 is None:
                S.op("vector", key, lambda e, o=out, i=in0, a=s1: e.tensor_scalar(out=o, in0=i, scalar1=a, scalar2=None, op0=ALU.mult), waits)
            else:
                S.op("vector", key, lambda e, o=out, i=in0, a=s1, b=s2, p0=op0, p1=op1: e.tensor_scalar(out=o, in0=i, scalar1=a, scalar2=b, op0=p0, op1=p1), waits)

        S.dma("sync", "c_ident", ident[:], ident_in)
        S.dma("sync", "c_ocol", ocol[:], ocol_in)
        S.dma("sync", "c_orf", orf[:], orf_in)
        S.dma("sync", "c_orb", orb[:], orb_in)
        S.dma("sync", "c_masks", masks[:].rearrange("p (t q) -> p t q", t=16),
              masks_in.rearrange("t p q -> p t q"))
        S.dma("sync", "c_bkq", bkq[:], bkq_in)
        S.dma("sync", "c_incol", incol[:], incol_in)
        S.dma("sync", "c_bvf", brow_f, brows_in[0:1, :])
        VCP("c_bv", bvrow, brow_f, waits=["c_bvf"])
        prev_cast = None
        for m in ("kT", "qT", "vT"):
            slot = {"kT": WK, "qT": WQ, "vT": WV}[m]
            for ic in range(NCH):
                S.dma("gpsimd", f"wl_{m}_{ic}", stg[:, 0:1024], wT[m][ic * 128:(ic + 1) * 128, :],
                      waits=[prev_cast])
                CP(f"wc_{m}_{ic}", slot[:, ic * 1024:(ic + 1) * 1024], stg[:, 0:1024],
                   waits=[f"wl_{m}_{ic}"])
                prev_cast = f"wc_{m}_{ic}"

        tp_ps = [psX, psY]
        prev_cc = None
        for t in range(NT):
            b, u = divmod(t, NU)
            xau = [xa, xa2][t % 2]
            S.dma("sync", f"x_{t}", xau[:].rearrange("p (s d) -> p s d", s=2),
                  x_own[t].rearrange("(s p) d -> p s d", s=2),
                  waits=[war.get(f"xa_sc_{t % 2}"), war.get(f"xa_ve_{t % 2}")])
            for st in range(2):
                base = (t % 2) * 16 + st * 8
                s_sum, s_sq = stats[:, base:base + 1], stats[:, base + 1:base + 2]
                s_m, s_2d = stats[:, base + 2:base + 3], stats[:, base + 3:base + 4]
                s_nv, s_rs = stats[:, base + 4:base + 5], stats[:, base + 5:base + 6]
                s_mn = stats[:, base + 6:base + 7]
                xs = xau[:, st * 1024:(st + 1) * 1024]
                S.op("vector", f"lns_{t}_{st}",
                     lambda e, o=s_sum, i=xs: e.reduce_sum(o, i, axis=AX.X),
                     waits=[f"x_{t}", war.get(f"st_{(t % 2)}_{st}")])
                ACT(f"lnq_{t}_{st}", sqd, xs, AF.Square, accum_out=s_sq,
                    waits=[f"x_{t}", war.get(f"st_{(t % 2)}_{st}")])
                TS(f"lnm_{t}_{st}", s_m, s_sum, f32(1.0 / D), None, ALU.mult,
                   waits=[f"lns_{t}_{st}"])
                TS(f"ln2_{t}_{st}", s_2d, s_sq, f32(1.0 / D), None, ALU.mult,
                   waits=[f"lnq_{t}_{st}"])
                S.op("vector", f"lnv_{t}_{st}",
                     lambda e, o=s_nv, a=s_m, c=s_2d:
                     e.scalar_tensor_tensor(out=o, in0=a, scalar=a, in1=c, op0=ALU.mult, op1=ALU.subtract),
                     waits=[f"lnm_{t}_{st}", f"ln2_{t}_{st}"])
                s_sd = stats[:, base + 7:base + 8]
                ACT(f"lsd_{t}_{st}", s_sd, s_nv, AF.Sqrt, bias=f32(1e-5), scale=f32(-1.0),
                    waits=[f"lnv_{t}_{st}"])
                S.op("vector", f"lnr_{t}_{st}",
                     lambda e, o=s_rs, i=s_sd: e.reciprocal(o, i),
                     waits=[f"lsd_{t}_{st}"])
                TS(f"lnn_{t}_{st}", s_mn, s_m, s_rs, None, ALU.mult,
                   waits=[f"lnr_{t}_{st}"])
                S.op("vector", f"xn_{t}_{st}",
                     lambda e, o=xn_tok[:, st * 1024:(st + 1) * 1024], i=xs, a=s_rs, b=s_mn:
                     e.tensor_scalar(out=o, in0=i, scalar1=a, scalar2=b, op0=ALU.mult, op1=ALU.subtract),
                     waits=[f"lnn_{t}_{st}", war.get("xntok")])
                war[f"st_{(t % 2)}_{st}"] = f"xn_{t}_{st}"
            war[f"xa_sc_{t % 2}"] = f"xn_{t}_{1}"
            war[f"xa_ve_{t % 2}"] = f"lns_{t}_{1}"
            for idx in range(16):
                st, ic = divmod(idx, NCH)
                ps = tp_ps[idx % 2]
                MM(f"tp_{t}_{idx}", ps[:, 0:128], xn_tok[:, st * 1024 + ic * 128: st * 1024 + ic * 128 + 128],
                   ident[:], True, True,
                   waits=[f"xn_{t}_{st}", "c_ident", war.get(f"ps_{idx % 2}")])
                CP(f"tpe_{t}_{idx}", xn0T[:, ic * 256 + st * 128: ic * 256 + st * 128 + 128],
                   ps[:, 0:128], waits=[f"tp_{t}_{idx}", war.get("xnTu"), war.get("xnTu_spill")])
                war[f"ps_{idx % 2}"] = f"tpe_{t}_{idx}"
            war["xntok_pe"] = f"tp_{t}_{15}"
            war["sqb_pe"] = f"tp_{t}_{15}"
            S.dma("sync", f"xnTs_{t}", xnT_scr[t].rearrange("c p q -> p c q"),
                  xn0T[:].rearrange("p (c q) -> p c q", c=NCH),
                  waits=[f"tpe_{t}_{15}"])
            war["xnTu_spill"] = f"xnTs_{t}"
            for pj, (W, bcol, wkey) in enumerate(((WK, 0, "kT"), (WQ, 1, "qT"))):
                pn = "k" if pj == 0 else "q"
                for oc in range(NCH):
                    pi = oc % 2
                    ps = tp_ps[pi]
                    for ic in range(NCH):
                        MM(f"{pn}mm_{t}_{oc}" if ic == NCH - 1 else None,
                           ps[:, 0:256], W[:, ic * 1024 + oc * 128: ic * 1024 + oc * 128 + 128],
                           xn0T[:, ic * 256:(ic + 1) * 256], ic == 0, ic == NCH - 1,
                           waits=([f"tpe_{t}_{15}", f"wc_{wkey}_{NCH-1}", war.get(f"ps_{pi}")] if ic == 0 else []))
                    ACT(f"{pn}ev_{t}_{oc}", keb[pi], ps[:, 0:256], AF.Identity,
                        bias=bkq[:, bcol * NCH + oc: bcol * NCH + oc + 1],
                        waits=[f"{pn}mm_{t}_{oc}", "c_bkq", war.get(f"keb_{pi}")])
                    war[f"ps_{pi}"] = f"{pn}ev_{t}_{oc}"
                    dst = kin[t][oc] if pj == 0 else q_scr[t][oc]
                    S.dma("sync", f"{pn}sp_{t}_{oc}", dst, keb[pi][:, 0:256],
                          waits=[f"{pn}ev_{t}_{oc}"])
                    ACT(f"{pn}sq_{t}_{oc}", sqpp[pi], keb[pi], AF.Square,
                        waits=[f"{pn}ev_{t}_{oc}", war.get(f"sqpp_{pi}")])
                    war[f"keb_{pi}"] = f"{pn}sp_{t}_{oc}"
                    MM(f"{pn}nmm_{t}" if oc == NCH - 1 else f"{pn}nm_{t}_{oc}",
                       psC[0:1, 0:256], ocol[:], sqpp[pi],
                       oc == 0, oc == NCH - 1,
                       waits=[f"{pn}sq_{t}_{oc}"] + (["c_ocol", war.get("psC")] if oc == 0 else []))
                    war[f"sqpp_{pi}"] = f"{pn}nmm_{t}" if oc == NCH - 1 else f"{pn}nm_{t}_{oc}"
                scol = (32 + t) if pj == 1 else (48 + t)
                S.op("vector", f"{pn}nmx_{t}",
                     lambda e, o=stats[0:1, scol:scol + 1], i=psC[0:1, 0:256]:
                     e.reduce_max(o, i, axis=AX.X),
                     waits=[f"{pn}nmm_{t}"])
                war["psC"] = f"{pn}nmx_{t}"
                war["sqb_pe"] = f"{pn}nmm_{t}"
            war["xntok"] = f"tp_{t}_{15}"
            for st in range(2):
                for oh in range(2):
                    pi = (st * 2 + oh) % 2
                    ps = tp_ps[pi]
                    MM(None, ps[:, 0:512], orb[:], bvrow[0:1, oh * 512:(oh + 1) * 512],
                       True, False, waits=["c_bv", "c_orb", war.get(f"ps_{pi}")])
                    for ic in range(NCH):
                        MM(f"vmm_{t}_{st}_{oh}" if ic == NCH - 1 else None,
                           ps[:, 0:512], xn0T[:, ic * 256 + st * 128: ic * 256 + st * 128 + 128],
                           WV[:, ic * 1024 + oh * 512: ic * 1024 + oh * 512 + 512],
                           False, ic == NCH - 1,
                           waits=([f"wc_vT_{NCH-1}"] if ic == 0 else []))
                    CP(f"vev_{t}_{st}_{oh}", veb[pi][:, 0:512], ps[:, 0:512],
                       waits=[f"vmm_{t}_{st}_{oh}", war.get(f"veb_{pi}")])
                    war[f"ps_{pi}"] = f"vev_{t}_{st}_{oh}"
                    S.dma("sync", f"vsp_{t}_{st}_{oh}", vin[t][st][:, oh * 512:(oh + 1) * 512],
                          veb[pi][:, 0:512], waits=[f"vev_{t}_{st}_{oh}"])
                    war[f"veb_{pi}"] = f"vsp_{t}_{st}_{oh}"
            war["xnTu"] = f"vmm_{t}_{1}_{1}"
            S.cc(f"ccK_{t}", lambda e, a=kin[t], o=kout[t]: e.collective_compute(
                "AllGather", ALU.bypass, replica_groups=RG, ins=[a.opt()], outs=[o.opt()]),
                waits=[f"ksp_{t}_{NCH-1}", prev_cc])
            S.cc(f"ccV_{t}", lambda e, a=vin[t], o=vout[t]: e.collective_compute(
                "AllGather", ALU.bypass, replica_groups=RG, ins=[a.opt()], outs=[o.opt()]),
                waits=[f"vsp_{t}_{1}_{1}", f"ccK_{t}"])
            prev_cc = f"ccV_{t}"

        a_last = NT - 1
        S.op("vector", "mx0", lambda e: e.memset(mxsb, 0.0))
        for b in range(2):
            S.op("vector", f"mxb_{b}",
                 lambda e, o=mxsb[0:1, b:b + 1], i=stats[0:1, 48 + b * NU: 48 + (b + 1) * NU]:
                 e.reduce_max(o, i, axis=AX.X),
                 waits=["mx0", f"knmx_{b * NU + NU - 1}"])
        S.dma("sync", "mxsp", mxin, mxsb, waits=["mxb_0", "mxb_1"])
        S.cc("ccAR", lambda e: e.collective_compute(
            "AllReduce", ALU.max, replica_groups=RG, ins=[mxin.opt()], outs=[mxout.opt()]),
            waits=["mxsp", prev_cc])
        S.dma("sync", "gald", garow, mxout, waits=["ccAR"])

        FORDER = ("ufT", "uiT", "ucT", "fT", "iT", "cT", "oT", "uoT")
        a_pe_done = {"fT": f"kmm_{a_last}_{NCH-1}", "iT": f"qmm_{a_last}_{NCH-1}",
                     "cT": f"vmm_{a_last}_{1}_{1}"}
        prev_fcast = {0: "wc_vT_%d" % (NCH - 1), 1: None}
        nchunk = 0
        last_odd_cast = None
        for m in FORDER:
            for ic in range(NCH):
                par = (nchunk % 2) if nchunk < 40 else 0
                S.dma("sync", f"fwl_{m}_{ic}", stgs[par][:, 0:1024], wT[m][ic * 128:(ic + 1) * 128, :],
                      waits=[prev_fcast[par]])
                VCP(f"fwc_{m}_{ic}", FW[m][:, ic * 1024:(ic + 1) * 1024], stgs[par][:, 0:1024],
                    waits=[f"fwl_{m}_{ic}"] + ([a_pe_done[m]] if (m in a_pe_done and ic == 0) else []))
                prev_fcast[par] = f"fwc_{m}_{ic}"
                if par == 1:
                    last_odd_cast = f"fwc_{m}_{ic}"
                nchunk += 1
        CP("incb", incb[:], incol[:], waits=["c_incol"])
        S.dma("sync", "incld", increg, incrow_in)
        for mi, m in enumerate(("f", "i", "c")):
            um = ("ufT", "uiT", "ucT")[mi]
            S.dma("sync", f"brl_{m}", brow_f, brows_in[1 + mi:2 + mi, :],
                  waits=["c_bv" if mi == 0 else f"bp_{('f','i','c')[mi-1]}_{1}"])
            for oh in range(2):
                for ic in range(NCH):
                    MM(f"zmm_{m}_{oh}" if ic == NCH - 1 else None,
                       psX[0:1, 0:512], incb[:, ic:ic + 1],
                       FW[um][:, ic * 1024 + oh * 512: ic * 1024 + oh * 512 + 512],
                       ic == 0, ic == NCH - 1,
                       waits=(["incb", f"fwc_{um}_{NCH-1}", war.get("ps_0"),
                               (f"bp_{m}_{0}" if oh == 1 else None)] if ic == 0 else []))
                TT(f"bp_{m}_{oh}", bprow[m][0:1, oh * 512:(oh + 1) * 512],
                   psX[0:1, 0:512], brow_f[0:1, oh * 512:(oh + 1) * 512], ALU.add,
                   waits=[f"zmm_{m}_{oh}", f"brl_{m}",
                          (war.get("bvrow_pe") if m == "f" and oh == 0 else None)])
            war["ps_0"] = f"bp_{m}_{1}"
        S.dma("sync", "brl_o", brow_f, brows_in[4:5, :], waits=[f"bp_c_{1}"])
        VCP("c_bo", borow, brow_f, waits=["brl_o"])
        for oh in range(2):
            MM(f"c0mm_{oh}", psY[:, 0:512], orf[:], increg[0:1, oh * 512:(oh + 1) * 512],
               True, True, waits=["incld", "c_orf", war.get("ps_1"), (f"c0b_{0}" if oh else None)])
            CP(f"c0b_{oh}", c0b[:, oh * 512:(oh + 1) * 512], psY[:, 0:512], waits=[f"c0mm_{oh}"])
            war["ps_1"] = f"c0b_{oh}"
        war["increg"] = f"c0mm_{1}"

        GPS = [psA, psB, psC]
        a_sc_done = f"qsq_{a_last}_{NCH-1}"
        a_ve_done = f"qnmx_{a_last}"
        war["h0f"] = a_sc_done
        war["c1T_pe"] = f"qsp_{a_last}_{NCH-1}"
        war["c2T_pe"] = f"vsp_{a_last}_{1}_{1}"
        war["bvrow_pe"] = f"vmm_{a_last}_{1}_{1}"
        war["xn1T_war"] = f"qnmm_{a_last}"
        for u in range(NU):
            S.dma("sync", f"fx0_{u}", xn0T[:].rearrange("p (c q) -> p c q", c=NCH),
                  xnT_scr[u].rearrange("c p q -> p c q"),
                  waits=[war["xnTu"], f"xnTs_{u}"])
            S.dma("sync", f"fx1_{u}", xn1T[:].rearrange("p (c q) -> p c q", c=NCH),
                  xnT_scr[NU + u].rearrange("c p q -> p c q"),
                  waits=[war["xn1T_war"], a_sc_done, f"xnTs_{NU + u}"])
            for st in range(2):
                sb = f"{u}_{st}"
                for g in range(3):
                    for oh in range(2):
                        MM(None, GPS[g][:, oh * 512:(oh + 1) * 512], orb[:],
                           bprow["fic"[g]][0:1, oh * 512:(oh + 1) * 512], True, False,
                           waits=[f"bp_{'fic'[g]}_{oh}", war.get(f"gps_{g}"),
                                  (a_ve_done if (u == 0 and st == 0 and g == 2) else None)])
                for ic in range(NCH):
                    lh = xn0T[:, ic * 256 + st * 128: ic * 256 + st * 128 + 128]
                    for g in range(3):
                        wm = FW[("fT", "iT", "cT")[g]]
                        for oh in range(2):
                            MM(f"g1m_{sb}_{g}" if (ic == NCH - 1 and oh == 1) else None,
                               GPS[g][:, oh * 512:(oh + 1) * 512], lh,
                               wm[:, ic * 1024 + oh * 512: ic * 1024 + oh * 512 + 512],
                               False, ic == NCH - 1,
                               waits=([f"fx0_{u}", f"fwc_{('fT','iT','cT')[g]}_{NCH-1}"]
                                      if (ic == 0 and oh == 0) else []))
                for g in range(3):
                    ACT(f"g1e_{sb}_{g}", gate[g], GPS[g][:],
                        AF.Sigmoid if g < 2 else AF.Tanh,
                        waits=[f"g1m_{sb}_{g}", war.get(f"gate_{g}"),
                               (a_ve_done if (u == 0 and st == 0) else None)])
                    war[f"gps_{g}"] = f"g1e_{sb}_{g}"
                TT(f"ct1_{sb}", c1f, gate[0], c0b, ALU.mult,
                   waits=[f"g1e_{sb}_{0}", f"c0b_{1}", war.get("c1f"), (a_sc_done if (u == 0 and st == 0) else None)])
                TT(f"ct2_{sb}", h0f, gate[1], gate[2], ALU.mult,
                   waits=[f"g1e_{sb}_{1}", f"g1e_{sb}_{2}", war.get("h0f")])
                TT(f"c1_{sb}", c1f, c1f, h0f, ALU.add, waits=[f"ct1_{sb}", f"ct2_{sb}"])
                CP(f"c1b_{sb}", c1bf[:], c1f, waits=[f"c1_{sb}", war.get("c1bf")])
                for ic in range(NCH):
                    ps = tp_ps[ic % 2]
                    MM(f"c1tp_{sb}_{ic}", ps[:, 0:128], c1bf[:, ic * 128:(ic + 1) * 128],
                       ident[:], True, True, waits=[f"c1b_{sb}", war.get(f"ps_{ic % 2}")])
                    CP(f"c1T_{sb}_{ic}", c1T[:, ic * 128:(ic + 1) * 128], ps[:, 0:128],
                       waits=[f"c1tp_{sb}_{ic}", war.get("c1T_pe")])
                    war[f"ps_{ic % 2}"] = f"c1T_{sb}_{ic}"
                for g in range(3):
                    for oh in range(2):
                        MM(None, GPS[g][:, oh * 512:(oh + 1) * 512], orb[:],
                           bprow["fic"[g]][0:1, oh * 512:(oh + 1) * 512], True, False,
                           waits=[war.get(f"gps_{g}")])
                for ic in range(NCH):
                    lh = xn1T[:, ic * 256 + st * 128: ic * 256 + st * 128 + 128]
                    for g in range(3):
                        wm = FW[("fT", "iT", "cT")[g]]
                        for oh in range(2):
                            MM(None, GPS[g][:, oh * 512:(oh + 1) * 512], lh,
                               wm[:, ic * 1024 + oh * 512: ic * 1024 + oh * 512 + 512],
                               False, False,
                               waits=([f"fx1_{u}"] if (ic == 0 and g == 0 and oh == 0) else []))
                for mc in range(NCH):
                    lh = c1T[:, mc * 128:(mc + 1) * 128]
                    for g in range(3):
                        um = FW[("ufT", "uiT", "ucT")[g]]
                        for oh in range(2):
                            MM(f"g2m_{sb}_{g}" if (mc == NCH - 1 and oh == 1) else None,
                               GPS[g][:, oh * 512:(oh + 1) * 512], lh,
                               um[:, mc * 1024 + oh * 512: mc * 1024 + oh * 512 + 512],
                               False, mc == NCH - 1,
                               waits=([f"c1T_{sb}_{NCH-1}"] if (mc == 0 and g == 0 and oh == 0) else []))
                for g in range(3):
                    ACT(f"g2e_{sb}_{g}", gate[g], GPS[g][:],
                        AF.Sigmoid if g < 2 else AF.Tanh,
                        waits=[f"g2m_{sb}_{g}", f"c1_{sb}" if g == 0 else None,
                               f"ct2_{sb}" if g > 0 else None])
                    war[f"gps_{g}"] = f"g2e_{sb}_{g}"
                TT(f"dt1_{sb}", c2f, gate[0], c1f, ALU.mult,
                   waits=[f"g2e_{sb}_{0}", war.get("c2f"),
                          (last_odd_cast if (u == 0 and st == 0) else None)])
                TT(f"dt2_{sb}", h1f, gate[1], gate[2], ALU.mult,
                   waits=[f"g2e_{sb}_{1}", f"g2e_{sb}_{2}", war.get("h1f")])
                TT(f"c2_{sb}", c2f, c2f, h1f, ALU.add, waits=[f"dt1_{sb}", f"dt2_{sb}"])
                war["c1f"] = f"dt1_{sb}"
                CP(f"c2b_{sb}", c1bf[:], c2f, waits=[f"c2_{sb}", f"c1tp_{sb}_{NCH-1}"])
                war["c1bf"] = f"c2b_{sb}"
                for ic in range(NCH):
                    ps = tp_ps[ic % 2]
                    MM(f"c2tp_{sb}_{ic}", ps[:, 0:128], c1bf[:, ic * 128:(ic + 1) * 128],
                       ident[:], True, True, waits=[f"c2b_{sb}", war.get(f"ps_{ic % 2}")])
                    CP(f"c2T_{sb}_{ic}", c2T[:, ic * 128:(ic + 1) * 128], ps[:, 0:128],
                       waits=[f"c2tp_{sb}_{ic}", war.get("c2T_pe")])
                    war[f"ps_{ic % 2}"] = f"c2T_{sb}_{ic}"
                war["c1bf"] = f"c2tp_{sb}_{NCH-1}"
                war["c1T_pe"] = f"g2m_{sb}_{2}"
                for hb in range(2):
                    ps = GPS[hb]
                    xnsrc = xn0T if hb == 0 else xn1T
                    hslot = h0f if hb == 0 else h1f
                    for oh in range(2):
                        MM(None, ps[:, oh * 512:(oh + 1) * 512], orb[:],
                           borow[0:1, oh * 512:(oh + 1) * 512], True, False,
                           waits=["c_bo", war.get(f"gps_{hb}")])
                        for ic in range(NCH):
                            MM(None, ps[:, oh * 512:(oh + 1) * 512],
                               xnsrc[:, ic * 256 + st * 128: ic * 256 + st * 128 + 128],
                               FW["oT"][:, ic * 1024 + oh * 512: ic * 1024 + oh * 512 + 512],
                               False, False,
                               waits=([f"fwc_oT_{NCH-1}"] if (oh == 0 and ic == 0) else []))
                        for mc in range(NCH):
                            MM(f"hm_{sb}_{hb}" if (oh == 1 and mc == NCH - 1) else None,
                               ps[:, oh * 512:(oh + 1) * 512], c2T[:, mc * 128:(mc + 1) * 128],
                               FW["uoT"][:, mc * 1024 + oh * 512: mc * 1024 + oh * 512 + 512],
                               False, mc == NCH - 1,
                               waits=([f"c2T_{sb}_{NCH-1}", f"fwc_uoT_{NCH-1}"] if (oh == 0 and mc == 0) else []))
                    ACT(f"he_{sb}_{hb}", hslot, ps[:], AF.Tanh,
                        waits=[f"hm_{sb}_{hb}", (f"c2_{sb}" if hb == 1 else f"c1_{sb}"),
                               war.get("h0sp" if hb == 0 else "h1sp")])
                    war[f"gps_{hb}"] = f"he_{sb}_{hb}"
                    S.dma("sync", f"hsp_{sb}_{hb}", h_scr[hb * NU + u][st * 128:(st + 1) * 128, :],
                          hslot, waits=[f"he_{sb}_{hb}"])
                    war["h0sp" if hb == 0 else "h1sp"] = f"hsp_{sb}_{hb}"
                war["h0f"] = f"hsp_{sb}_{0}"
                war["h1f"] = f"hsp_{sb}_{1}"
                war["c2f"] = f"hm_{sb}_{1}"
                war["c2T_pe"] = f"hm_{sb}_{1}"
                war["gate_0"] = f"dt1_{sb}"
                war["gate_1"] = f"dt2_{sb}"
                war["gate_2"] = f"dt2_{sb}"
            war["xnTu"] = f"hm_{u}_{1}_{0}"
            war["xn1T_war"] = f"hm_{u}_{1}_{1}"
        ffn_pe_done = f"hm_{NU-1}_{1}_{1}"
        ffn_sc_done = f"he_{NU-1}_{1}_{1}"
        ffn_ve_done = f"c2_{NU-1}_{1}"

        ACT("sq1", sq1r[0:1, 0:NT], stats[0:1, 32:32 + NT], AF.Sqrt,
            waits=[f"qnmx_{a_last}"])
        ACT("sqga", sqga[0:1, 0:2], garow[0:1, 0:2], AF.Sqrt, waits=["gald"])
        for b in range(2):
            S.op("vector", f"gax_{b}",
                 lambda e, o=gaexp[0:1, b * NU:(b + 1) * NU], i=orf[0:1, 0:NU], s=sqga[0:1, b:b + 1]:
                 e.tensor_scalar(out=o, in0=i, scalar1=s, scalar2=None, op0=ALU.mult),
                 waits=["sqga", "c_orf"])
        TT("prod", prodr[0:1, 0:NT], sq1r[0:1, 0:NT], gaexp[0:1, 0:NT], ALU.mult,
           waits=["gax_0", "gax_1", "sq1"])
        TS("negrow", negrow[0:1, 0:NT], prodr[0:1, 0:NT], f32(-1.0 / 32.0), None, ALU.mult,
           waits=["prod"])
        for t in range(NT):
            MM(f"ncb_{t}", psY[:, 0:1], orf[:], negrow[0:1, t:t + 1], True, True,
               waits=["negrow", war.get("ps_1"), ffn_pe_done if t == 0 else None])
            CP(f"ncc_{t}", negc[:, t:t + 1], psY[:, 0:1], waits=[f"ncb_{t}"])
            war["ps_1"] = f"ncc_{t}"

        war["stp_0"] = war.get("ps_0")
        war["stp_1"] = war.get("ps_0")
        war["pvacc"] = ffn_sc_done
        war["psC"] = war.get("gps_2")
        for _k in range(4):
            war[f"kt_{_k}"] = ffn_pe_done
        for _k in range(4):
            war[f"vt_{_k}"] = ffn_pe_done
        war["pb_0"] = ffn_pe_done
        war["pb_1"] = ffn_pe_done
        pbkey = {}
        for t in range(NT):
            b, u = divmod(t, NU)
            nk = (u + 1) * 16
            S.dma("sync", f"aq_{t}", qbuf[:].rearrange("p (c q) -> p c q", c=NCH),
                  q_scr[t].rearrange("c p q -> p c q"),
                  waits=[war.get("qbuf") or ffn_pe_done])
            S.op("vector", f"az_{t}", lambda e: e.memset(accv, 0.0),
                 waits=[war.get("accv") or ffn_ve_done,
                        (ffn_sc_done if t == 0 else None)])
            def emit_lpv(kc, t=t, u=u, nk=nk):
                sh = kc % 2
                kci = (kc // 2) % 4
                pk0, pk1 = pbkey[(t, kc)]
                for qs in range(2):
                    if qs == 1:
                        MM(f"lm_{t}" if kc == nk - 1 else None, psC[0:1, 0:256], ocol[:],
                           pb[sh], kc == 0, kc == nk - 1,
                           waits=[pk1] + ([war.get("psC"), "c_ocol"] if kc == 0 else []))
                    for oh in range(2):
                        MM(f"pv_{t}_{kc}" if (qs == 1 and oh == 1) else None,
                           GPS[qs][:, oh * 512:(oh + 1) * 512],
                           pb[sh][:, qs * 128:(qs + 1) * 128],
                           vtile[kci][:, (kc % 2) * 1024 + oh * 512:(kc % 2) * 1024 + oh * 512 + 512],
                           kc % 4 == 0, kc % 4 == 3,
                           waits=([pk0 if qs == 0 else pk1, f"avt_{t}_{kc // 2}"] if oh == 0 else []) +
                                 ([war.get("pvacc")] if (qs == 0 and oh == 0 and kc % 4 == 0) else []))
                war[f"pb_{sh}"] = f"pv_{t}_{kc}"
                war[f"vt_{kci}"] = f"pv_{t}_{kc}"
                if kc % 4 == 3:
                    blk = kc // 4
                    for seg in range(4):
                        TT(f"aa_{t}_{blk}" if seg == 3 else None,
                           accv[:, seg * 512:(seg + 1) * 512], accv[:, seg * 512:(seg + 1) * 512],
                           GPS[seg // 2][:, (seg % 2) * 512:(seg % 2 + 1) * 512], ALU.add,
                           waits=([f"pv_{t}_{kc}", f"az_{t}"] if seg == 0 else []))
                    war["pvacc"] = f"aa_{t}_{blk}"

            for kc in range(nk):
                c256, hh = divmod(kc, 2)
                uu, r = divmod(c256, NCORE)
                kci = kc % 8
                if hh == 0:
                    ks = c256 % 4
                    S.dma("sync", f"akt_{t}_{c256}",
                          ktile[ks][:].rearrange("p (c q) -> p c q", c=NCH),
                          kout[b * NU + uu][r * NCH:(r + 1) * NCH].rearrange("c p q -> p c q"),
                          waits=[f"ccK_{b * NU + uu}", war.get(f"kt_{ks}")])
                    S.dma("gpsimd", f"avt_{t}_{c256}",
                          vtile[ks][:].rearrange("p (s d) -> p s d", s=2),
                          vout[b * NU + uu][r * 2:(r + 1) * 2].rearrange("s p d -> p s d"),
                          waits=[f"ccV_{b * NU + uu}", war.get(f"vt_{ks}")])
                sh = kc % 2
                stp = psX[:, sh * 256:(sh + 1) * 256]
                ks = c256 % 4
                for oc in range(NCH):
                    MM(f"qk_{t}_{kc}" if oc == NCH - 1 else None, stp,
                       ktile[ks][:, oc * 256 + hh * 128: oc * 256 + hh * 128 + 128],
                       qbuf[:, oc * 256:(oc + 1) * 256], oc == 0, oc == NCH - 1,
                       waits=([f"akt_{t}_{c256}", f"aq_{t}", war.get(f"stp_{sh}")] if oc == 0 else []))
                pk2 = []
                for qs in range(2):
                    ACT(f"ex_{t}_{kc}_{qs}", pb[sh][:, qs * 128:(qs + 1) * 128],
                        psX[:, sh * 256 + qs * 128: sh * 256 + qs * 128 + 128], AF.Exp,
                        bias=negc[:, t:t + 1], scale=f32(1.0 / 32.0),
                        waits=[f"qk_{t}_{kc}", f"ncc_{t}", war.get(f"pb_{sh}")])
                    k2 = f"ex_{t}_{kc}_{qs}"
                    if kc >= u * 16:
                        mi = kc - u * 16
                        TT(f"mk_{t}_{kc}_{qs}", pb[sh][:, qs * 128:(qs + 1) * 128],
                           pb[sh][:, qs * 128:(qs + 1) * 128],
                           masks[:, mi * 256 + qs * 128: mi * 256 + qs * 128 + 128],
                           ALU.mult, waits=[k2, "c_masks"])
                        k2 = f"mk_{t}_{kc}_{qs}"
                    pk2.append(k2)
                war[f"stp_{sh}"] = pk2[1]
                pbkey[(t, kc)] = tuple(pk2)
                war[f"kt_{ks}"] = f"qk_{t}_{kc}"
                emit_lpv(kc)
            war["qbuf"] = f"qk_{t}_{nk-1}"
            VCP(f"lr_{t}", lrow, psC[0:1, 0:256],
                waits=[f"lm_{t}", war.get("lrow"), (ffn_sc_done if t == 0 else None)])
            war["psC"] = f"lr_{t}"
            for qs in range(2):
                MM(f"ltp_{t}_{qs}", psY[:, 0:1], lrow[0:1, qs * 128:(qs + 1) * 128],
                   orf[0:1, 0:1], True, True, waits=[f"lr_{t}", war.get("ps_1")])
                war["lrow"] = f"ltp_{t}_{qs}"
                S.op("vector", f"li_{t}_{qs}",
                     lambda e, o=linv[:, qs:qs + 1], i=psY[:, 0:1]: e.reciprocal(o, i),
                     waits=[f"ltp_{t}_{qs}"])
                war["ps_1"] = f"li_{t}_{qs}"
                TS(f"an_{t}_{qs}", accv[:, qs * 1024:(qs + 1) * 1024],
                   accv[:, qs * 1024:(qs + 1) * 1024], linv[:, qs:qs + 1], None, ALU.mult,
                   waits=[f"li_{t}_{qs}", f"aa_{t}_{nk // 4 - 1}"])
            if DEBUG:
                S.dma("sync", f"dacc_{t}", d_acc[t].rearrange("(s p) d -> p s d", s=2),
                      accv[:].rearrange("p (s d) -> p s d", s=2), waits=[f"an_{t}_{1}"])
                S.dma("sync", f"dl_{t}", d_l[t], lrow, waits=[f"lr_{t}"])
                S.dma("sync", f"dh_{t}", d_h[t], h_scr[t], waits=[f"hsp_{u}_{1}_{b}"])
                S.dma("sync", f"dxnT_{t}", d_xnT[t], xnT_scr[t], waits=[f"xnTs_{t}"])
                S.dma("sync", f"dko_{t}", d_kout[t], kout[t], waits=[f"ccK_{t}"])
                S.dma("sync", f"dvo_{t}", d_vout[t], vout[t], waits=[f"ccV_{t}"])
            S.dma("gpsimd", f"ax_{t}", xrt[:].rearrange("p (s d) -> p s d", s=2),
                  x_own[t].rearrange("(s p) d -> p s d", s=2),
                  waits=[war.get("xrt") or ffn_ve_done, war.get("xrt_pe") or ffn_pe_done])
            S.dma("gpsimd", f"ah_{t}", hrt[:].rearrange("p (s d) -> p s d", s=2),
                  h_scr[t].rearrange("(s p) d -> p s d", s=2),
                  waits=[f"hsp_{u}_{1}_{b}", war.get("hrt") or ffn_ve_done,
                         (ffn_pe_done if t == 0 else None)])
            TT(f"ao1_{t}", accv, accv, xrt, ALU.add,
               waits=[f"an_{t}_{1}", f"ax_{t}"] + ([f"dacc_{t}"] if DEBUG else []))
            TT(f"ao2_{t}", accv, accv, hrt, ALU.add, waits=[f"ao1_{t}", f"ah_{t}"])
            war["xrt"] = f"ao1_{t}"
            war["hrt"] = f"ao2_{t}"
            war["xrt_pe"] = f"ao1_{t}"
            S.dma("gpsimd", f"out_{t}", y_own[t].rearrange("(s p) d -> p s d", s=2),
                  accv[:].rearrange("p (s d) -> p s d", s=2), waits=[f"ao2_{t}"])
            war["accv"] = f"out_{t}"
        if DEBUG:
            S.dma("sync", "dnegc", d_negc, negc[:], waits=[f"ncc_{NT - 1}"])
            S.dma("sync", "dstats", d_stats, stats[:], waits=[f"ncc_{NT - 1}"])
            S.dma("sync", "drows", d_rows, pf[0:1, 11264:11440], waits=[f"ncc_{NT - 1}"])
        S.op("sync", None, lambda e: e.nop(), waits=[f"out_{NT - 1}"] + (["dnegc"] if DEBUG else []))

        S.finalize(nc, stack, serial=SERIAL)
    return nc


def kernel(L0_attn_x, L0_attn_Wk, L0_attn_bk, L0_attn_Wq, L0_attn_bq, L0_attn_Wv,
           L0_attn_bv, L0_ffn_wf, L0_ffn_uf, L0_ffn_bf, L0_ffn_wi, L0_ffn_ui,
           L0_ffn_bi, L0_ffn_wc, L0_ffn_uc, L0_ffn_bc, L0_ffn_wo, L0_ffn_uo,
           L0_ffn_bo, L0_ffn_in_c):
    x = np.asarray(L0_attn_x, dtype=np.float32)
    B, Sq, Dm = x.shape
    assert B == 2 and Dm == D and Sq % (NCORE * CH) == 0
    NU = Sq // (NCORE * CH)

    def wt(a):
        return np.ascontiguousarray(np.asarray(a, dtype=np.float32).T)

    weights = {"w_kT": wt(L0_attn_Wk), "w_qT": wt(L0_attn_Wq), "w_vT": wt(L0_attn_Wv),
               "w_fT": wt(L0_ffn_wf), "w_iT": wt(L0_ffn_wi), "w_cT": wt(L0_ffn_wc),
               "w_ufT": wt(L0_ffn_uf), "w_uiT": wt(L0_ffn_ui), "w_ucT": wt(L0_ffn_uc),
               "w_oT": wt(L0_ffn_wo), "w_uoT": wt(L0_ffn_uo)}
    bk = np.asarray(L0_attn_bk, np.float32).reshape(NCH, 128).T
    bq = np.asarray(L0_attn_bq, np.float32).reshape(NCH, 128).T
    bkq = np.ascontiguousarray(np.concatenate([bk, bq], axis=1))
    brows = np.ascontiguousarray(np.stack([
        np.asarray(L0_attn_bv, np.float32), np.asarray(L0_ffn_bf, np.float32),
        np.asarray(L0_ffn_bi, np.float32), np.asarray(L0_ffn_bc, np.float32),
        np.asarray(L0_ffn_bo, np.float32)]))
    incrow = np.ascontiguousarray(np.asarray(L0_ffn_in_c, np.float32).reshape(1, D))
    incol = np.ascontiguousarray(np.asarray(L0_ffn_in_c, np.float32).reshape(NCH, 128).T)
    ident = np.eye(128, dtype=BFDT)
    ocolv = np.ones((128, 1), dtype=BFDT)
    orow_f = np.ones((1, 128), dtype=np.float32)
    orow_b = np.ones((1, 128), dtype=BFDT)

    xr = x.reshape(B, Sq // CH, CH, D)
    kk, qq = np.meshgrid(np.arange(128), np.arange(CH), indexing="ij")
    in_maps = []
    for j in range(NCORE):
        idx = np.arange(NU) * NCORE + j
        x_ownj = np.ascontiguousarray(xr[:, idx].reshape(2 * NU, CH, D))
        mt = np.zeros((16, 128, CH), dtype=BFDT)
        for mi in range(16):
            m, hh2 = divmod(mi, 2)
            if m < j:
                mt[mi] = 1
            elif m == j:
                mt[mi] = ((hh2 * 128 + kk) <= qq).astype(BFDT)
        im = {"x_own": x_ownj, "bkq": bkq, "brows": brows, "incrow": incrow,
              "incol": incol, "masks": mt, "ident": ident, "ocol": ocolv,
              "orow_f": orow_f, "orow_b": orow_b}
        im.update(weights)
        in_maps.append(im)

    nc = build(NU)
    res = run_bass_kernel_spmd(nc, in_maps, core_ids=list(range(NCORE)))
    out = np.empty((B, Sq, D), dtype=np.float32)
    for j in range(NCORE):
        yj = np.asarray(res.results[j]["y_own"])
        for b in range(B):
            for i in range(NU):
                c = i * NCORE + j
                out[b, c * CH:(c + 1) * CH] = yj[b * NU + i]
    return out


# version 06
# speedup vs baseline: 1.0138x; 1.0138x over previous
import contextlib
import numpy as np
import ml_dtypes
import concourse.bass as bass
import concourse.mybir as mybir
from concourse.bass_utils import run_bass_kernel_spmd

F32 = mybir.dt.float32
BF16 = mybir.dt.bfloat16
AF = mybir.ActivationFunctionType
ALU = mybir.AluOpType
AX = mybir.AxisListType
BFDT = ml_dtypes.bfloat16

DEBUG = False
SERIAL = False
NCORE = 8
D = 1024
NCH = D // 128
CH = 256
RG = [list(range(NCORE))]
f32 = np.float32


class Sched:
    ENGS = ("sync", "gpsimd", "vector", "scalar", "tensor")
    ESEM = {"sync": "sy", "gpsimd": "gp", "vector": "ve", "scalar": "sc", "tensor": "pe"}
    DSEM = {"sync": "dsy", "gpsimd": "dgp", "vector": "dve", "scalar": "dsc"}

    def __init__(self):
        self.ops = []
        self.keys = {}

    def op(self, eng, key, fn, waits=()):
        self.ops.append((eng, fn, tuple(w for w in waits if w), key, 1, self.ESEM[eng]))

    def dma(self, q, key, out, in_, waits=()):
        self.ops.append((q, (lambda e, o=out, i=in_: e.dma_start(out=o, in_=i)),
                         tuple(w for w in waits if w), key, 16, self.DSEM[q]))

    def cc(self, key, fn, waits=()):
        self.ops.append(("gpsimd", fn, tuple(w for w in waits if w), key, 1, "cc"))

    def finalize(self, nc, stack, serial=False):
        if serial:
            ops2 = []
            prev = None
            for n, (eng, fn, waits, key, inc, sem) in enumerate(self.ops):
                if key is None:
                    key = f"__auto{n}"
                waits = tuple(waits) + ((prev,) if prev else ())
                ops2.append((eng, fn, waits, key, inc, sem))
                prev = key
            self.ops = ops2
        sems = {s: stack.enter_context(nc.semaphore("s_" + s))
                for s in set(list(self.ESEM.values()) + list(self.DSEM.values()) + ["cc"])}
        ctr = {s: 0 for s in sems}
        chained = []
        for eng, fn, waits, key, inc, sem in self.ops:
            cw = None
            if key is not None:
                if sem not in self.ESEM.values() and ctr[sem] > 0:
                    cw = (sem, ctr[sem])
                ctr[sem] += inc
                assert key not in self.keys, f"dup key {key}"
                self.keys[key] = (sem, ctr[sem])
            chained.append(cw)
        self.chainw = chained
        for eng2, fn2, waits2, key2, inc2, sem2 in self.ops:
            for w in waits2:
                assert w in self.keys, f"wait on undefined key {w}"
        block = stack.enter_context(nc.Block())

        def mk_stream(eng):
            myops = [o for o in self.ops if o[0] == eng]
            mysem = self.ESEM[eng]

            mychain = [self.chainw[n] for n, o in enumerate(self.ops) if o[0] == eng]

            def stream(e):
                hw = {s: 0 for s in sems}
                for (_, fn, waits, key, inc, sem), cw in zip(myops, mychain):
                    for w in waits:
                        ws, wv = self.keys[w]
                        if wv <= hw[ws]:
                            continue
                        e.wait_ge(sems[ws], wv)
                        hw[ws] = wv
                    if cw is not None and cw[1] > hw[cw[0]]:
                        e.wait_ge(sems[cw[0]], cw[1])
                        hw[cw[0]] = cw[1]
                    instr = fn(e)
                    if key is not None:
                        instr.then_inc(sems[sem], inc)
            return stream

        for eng in self.ENGS:
            getattr(block, eng)(mk_stream(eng))


def build(NU):
    nc = bass.Bass("TRN2", target_bir_lowering=False)
    NT = 2 * NU

    def din(name, shape, dt):
        return nc.dram_tensor(name, shape, dt, kind="ExternalInput").ap()

    x_own = din("x_own", [NT, CH, D], F32)
    WNAMES = ("kT", "qT", "vT", "fT", "iT", "cT", "ufT", "uiT", "ucT", "oT", "uoT")
    wT = {m: din("w_" + m, [D, D], F32) for m in WNAMES}
    bkq_in = din("bkq", [128, 2 * NCH], F32)
    brows_in = din("brows", [5, D], F32)
    incrow_in = din("incrow", [1, D], F32)
    incol_in = din("incol", [128, NCH], F32)
    masks_in = din("masks", [16, 128, CH], BF16)
    ident_in = din("ident", [128, 128], BF16)
    ocol_in = din("ocol", [128, 1], BF16)
    orf_in = din("orow_f", [1, 128], F32)
    orb_in = din("orow_b", [1, 128], BF16)

    y_own = nc.dram_tensor("y_own", [NT, CH, D], F32, kind="ExternalOutput").ap()
    if DEBUG:
        d_h = nc.dram_tensor("d_h", [NT, CH, D], F32, kind="ExternalOutput").ap()
        d_acc = nc.dram_tensor("d_acc", [NT, CH, D], F32, kind="ExternalOutput").ap()
        d_xnT = nc.dram_tensor("d_xnT", [NT, NCH, 128, CH], BF16, kind="ExternalOutput").ap()
        d_l = nc.dram_tensor("d_l", [NT, 1, CH], F32, kind="ExternalOutput").ap()
        d_negc = nc.dram_tensor("d_negc", [128, 32], F32, kind="ExternalOutput").ap()
        d_stats = nc.dram_tensor("d_stats", [128, 64], F32, kind="ExternalOutput").ap()
        d_rows = nc.dram_tensor("d_rows", [1, 176], F32, kind="ExternalOutput").ap()
        d_kout = nc.dram_tensor("d_kout", [NT, NCORE * NCH, 128, CH], BF16, kind="ExternalOutput").ap()
        d_vout = nc.dram_tensor("d_vout", [NT, NCORE * 2, 128, D], BF16, kind="ExternalOutput").ap()

    xnT_scr = nc.dram_tensor("xnT_scr", [NT, NCH, 128, CH], BF16, kind="Internal").ap()
    q_scr = nc.dram_tensor("q_scr", [NT, NCH, 128, CH], BF16, kind="Internal").ap()
    h_scr = nc.dram_tensor("h_scr", [NT, CH, D], F32, kind="Internal").ap()
    kin = [nc.dram_tensor(f"kin_{t}", [NCH, 128, CH], BF16, kind="Internal").ap() for t in range(NT)]
    kout = [nc.dram_tensor(f"kout_{t}", [NCORE * NCH, 128, CH], BF16, kind="Internal").ap() for t in range(NT)]
    vin = [nc.dram_tensor(f"vin_{t}", [2, 128, D], BF16, kind="Internal").ap() for t in range(NT)]
    vout = [nc.dram_tensor(f"vout_{t}", [NCORE * 2, 128, D], BF16, kind="Internal").ap() for t in range(NT)]
    mxin = nc.dram_tensor("mxin", [1, 16], F32, kind="Internal").ap()
    mxout = nc.dram_tensor("mxout", [1, 16], F32, kind="Internal").ap()

    with contextlib.ExitStack() as stack:
        E = stack.enter_context
        WCOL = 8 * NCH * 1024
        pbf = E(nc.sbuf_tensor("pbf", [128, WCOL + 11264], BF16))
        pf = E(nc.sbuf_tensor("pf", [128, 11520], F32))
        masks = E(nc.sbuf_tensor("masks_sb", [128, 16 * CH], BF16))
        stats = E(nc.sbuf_tensor("stats", [128, 64], F32))
        negc = E(nc.sbuf_tensor("negc", [128, 32], F32))
        linv = E(nc.sbuf_tensor("linv", [128, 4], F32))
        bkq = E(nc.sbuf_tensor("bkq_sb", [128, 2 * NCH], F32))
        incol = E(nc.sbuf_tensor("incol_sb", [128, NCH], F32))
        incb = E(nc.sbuf_tensor("incb", [128, NCH], BF16))
        ident = E(nc.sbuf_tensor("ident_sb", [128, 128], BF16))
        ocol = E(nc.sbuf_tensor("ocol_sb", [128, 1], BF16))
        orf = E(nc.sbuf_tensor("orf", [1, 128], F32))
        orb = E(nc.sbuf_tensor("orb", [1, 128], BF16))
        psA = E(nc.psum_tensor("psA", [128, 1024], F32))
        psB = E(nc.psum_tensor("psB", [128, 1024], F32))
        psC = E(nc.psum_tensor("psC", [128, 1024], F32))
        psX = E(nc.psum_tensor("psX", [128, 512], F32))
        psY = E(nc.psum_tensor("psY", [128, 512], F32))

        def wslot(i):
            return pbf[:, i * NCH * 1024:(i + 1) * NCH * 1024]
        WK, WQ, WV = wslot(0), wslot(1), wslot(2)
        FW = {"fT": wslot(0), "iT": wslot(1), "cT": wslot(2), "ufT": wslot(3),
              "uiT": wslot(4), "ucT": wslot(5), "oT": wslot(6), "uoT": wslot(7)}
        qbuf = pbf[:, 0:2048]
        ktile = [pbf[:, 4096 + k * 2048: 4096 + (k + 1) * 2048] for k in range(4)]
        vtile = [pbf[:, 12288 + k * 2048: 12288 + (k + 1) * 2048] for k in range(4)]
        pb = [pbf[:, 20480 + k * CH: 20480 + (k + 1) * CH] for k in range(2)]
        xn0T = pbf[:, WCOL + 0: WCOL + 2048]
        xn1T = pbf[:, WCOL + 2048: WCOL + 4096]
        xn_tok = xn1T
        sqb = xn1T
        c1T = pbf[:, WCOL + 4096: WCOL + 5120]
        c2T = pbf[:, WCOL + 5120: WCOL + 6144]
        c1bf = pbf[:, WCOL + 6144: WCOL + 7168]
        keb = [pbf[:, WCOL + 4096 + i * CH: WCOL + 4096 + (i + 1) * CH] for i in range(2)]
        veb = [pbf[:, WCOL + 4608 + i * 512: WCOL + 4608 + (i + 1) * 512] for i in range(2)]
        sqpp = [pbf[:, WCOL + 5632 + i * 256: WCOL + 5632 + (i + 1) * 256] for i in range(2)]
        bvrow = pbf[0:1, WCOL + 7168: WCOL + 8192]
        bprow = {"f": pbf[0:1, WCOL + 7168: WCOL + 8192],
                 "i": pbf[0:1, WCOL + 8192: WCOL + 9216],
                 "c": pbf[0:1, WCOL + 9216: WCOL + 10240]}
        borow = pbf[0:1, WCOL + 10240: WCOL + 11264]

        xa = pf[:, 0:2048]
        xa2 = pf[:, 7168:9216]
        gate = [pf[:, 0:1024], pf[:, 1024:2048], pf[:, 2048:3072]]
        sqd = pf[:, 3072:4096]
        c1f = pf[:, 3072:4096]
        c2f = pf[:, 4096:5120]
        stg = pf[:, 5120:6144]
        stgs = [pf[:, 5120:6144], pf[:, 4096:5120]]
        c0b = pf[:, 6144:7168]
        h0f = pf[:, 7168:8192]
        h1f = pf[:, 8192:9216]
        increg = pf[0:1, 9216:10240]
        brow_f = pf[0:1, 10240:11264]
        accv = pf[:, 0:2048]
        xrt = pf[:, 2048:4096]
        hrt = pf[:, 4096:6144]
        lrow = pf[0:1, 6144:6400]
        mxsb = pf[0:1, 11264:11280]
        garow = pf[0:1, 11280:11296]
        sqga = pf[0:1, 11296:11312]
        gaexp = pf[0:1, 11312:11344]
        prodr = pf[0:1, 11344:11376]
        negrow = pf[0:1, 11376:11408]
        sq1r = pf[0:1, 11408:11440]

        S = Sched()
        war = {}

        def MM(key, out, lhsT, rhs, start, stop, waits=()):
            S.op("tensor", key,
                 lambda e, o=out, l=lhsT, r=rhs, a=start, b=stop:
                 e.matmul(o, l, r, start=a, stop=b, skip_group_check=True), waits)

        def ACT(eng_key, out, in_, func, bias=None, scale=None, accum_out=None, waits=()):
            kw = {}
            if bias is not None:
                kw["bias"] = bias
            if scale is not None:
                kw["scale"] = scale
            if accum_out is not None:
                kw["accum_out"] = accum_out
            S.op("scalar", eng_key,
                 lambda e, o=out, i=in_, f=func, k=kw: e.activation(out=o, in_=i, func=f, **k), waits)

        def CP(key, out, in_, waits=()):
            S.op("scalar", key, lambda e, o=out, i=in_: e.copy(o, i), waits)

        def VCP(key, out, in_, waits=()):
            S.op("vector", key, lambda e, o=out, i=in_: e.tensor_copy(o, i), waits)

        def TT(key, out, in0, in1, op, waits=()):
            S.op("vector", key, lambda e, o=out, a=in0, b=in1, p=op: e.tensor_tensor(out=o, in0=a, in1=b, op=p), waits)

        def TS(key, out, in0, s1, s2, op0, op1=None, waits=()):
            s1 = float(s1) if isinstance(s1, np.floating) else s1
            s2 = float(s2) if isinstance(s2, np.floating) else s2
            if op1 is None:
                S.op("vector", key, lambda e, o=out, i=in0, a=s1: e.tensor_scalar(out=o, in0=i, scalar1=a, scalar2=None, op0=ALU.mult), waits)
            else:
                S.op("vector", key, lambda e, o=out, i=in0, a=s1, b=s2, p0=op0, p1=op1: e.tensor_scalar(out=o, in0=i, scalar1=a, scalar2=b, op0=p0, op1=p1), waits)

        S.dma("sync", "c_ident", ident[:], ident_in)
        S.dma("sync", "c_ocol", ocol[:], ocol_in)
        S.dma("sync", "c_orf", orf[:], orf_in)
        S.dma("sync", "c_orb", orb[:], orb_in)
        S.dma("sync", "c_masks", masks[:].rearrange("p (t q) -> p t q", t=16),
              masks_in.rearrange("t p q -> p t q"))
        S.dma("sync", "c_bkq", bkq[:], bkq_in)
        S.dma("sync", "c_incol", incol[:], incol_in)
        S.dma("sync", "c_bvf", brow_f, brows_in[0:1, :])
        VCP("c_bv", bvrow, brow_f, waits=["c_bvf"])
        prev_cast = None
        for m in ("kT", "qT", "vT"):
            slot = {"kT": WK, "qT": WQ, "vT": WV}[m]
            for ic in range(NCH):
                S.dma("gpsimd", f"wl_{m}_{ic}", stg[:, 0:1024], wT[m][ic * 128:(ic + 1) * 128, :],
                      waits=[prev_cast])
                CP(f"wc_{m}_{ic}", slot[:, ic * 1024:(ic + 1) * 1024], stg[:, 0:1024],
                   waits=[f"wl_{m}_{ic}"])
                prev_cast = f"wc_{m}_{ic}"

        tp_ps = [psX, psY]
        prev_cc = None
        for t in range(NT):
            b, u = divmod(t, NU)
            xau = [xa, xa2][t % 2]
            S.dma("sync", f"x_{t}", xau[:].rearrange("p (s d) -> p s d", s=2),
                  x_own[t].rearrange("(s p) d -> p s d", s=2),
                  waits=[war.get(f"xa_sc_{t % 2}"), war.get(f"xa_ve_{t % 2}")])
            for st in range(2):
                base = (t % 2) * 16 + st * 8
                s_sum, s_sq = stats[:, base:base + 1], stats[:, base + 1:base + 2]
                s_m, s_2d = stats[:, base + 2:base + 3], stats[:, base + 3:base + 4]
                s_nv, s_rs = stats[:, base + 4:base + 5], stats[:, base + 5:base + 6]
                s_mn = stats[:, base + 6:base + 7]
                xs = xau[:, st * 1024:(st + 1) * 1024]
                S.op("vector", f"lns_{t}_{st}",
                     lambda e, o=s_sum, i=xs: e.reduce_sum(o, i, axis=AX.X),
                     waits=[f"x_{t}", war.get(f"st_{(t % 2)}_{st}")])
                ACT(f"lnq_{t}_{st}", sqd, xs, AF.Square, accum_out=s_sq,
                    waits=[f"x_{t}", war.get(f"st_{(t % 2)}_{st}")])
                TS(f"lnm_{t}_{st}", s_m, s_sum, f32(1.0 / D), None, ALU.mult,
                   waits=[f"lns_{t}_{st}"])
                TS(f"ln2_{t}_{st}", s_2d, s_sq, f32(1.0 / D), None, ALU.mult,
                   waits=[f"lnq_{t}_{st}"])
                S.op("vector", f"lnv_{t}_{st}",
                     lambda e, o=s_nv, a=s_m, c=s_2d:
                     e.scalar_tensor_tensor(out=o, in0=a, scalar=a, in1=c, op0=ALU.mult, op1=ALU.subtract),
                     waits=[f"lnm_{t}_{st}", f"ln2_{t}_{st}"])
                s_sd = stats[:, base + 7:base + 8]
                ACT(f"lsd_{t}_{st}", s_sd, s_nv, AF.Sqrt, bias=f32(1e-5), scale=f32(-1.0),
                    waits=[f"lnv_{t}_{st}"])
                S.op("vector", f"lnr_{t}_{st}",
                     lambda e, o=s_rs, i=s_sd: e.reciprocal(o, i),
                     waits=[f"lsd_{t}_{st}"])
                TS(f"lnn_{t}_{st}", s_mn, s_m, s_rs, None, ALU.mult,
                   waits=[f"lnr_{t}_{st}"])
                S.op("vector", f"xn_{t}_{st}",
                     lambda e, o=xn_tok[:, st * 1024:(st + 1) * 1024], i=xs, a=s_rs, b=s_mn:
                     e.tensor_scalar(out=o, in0=i, scalar1=a, scalar2=b, op0=ALU.mult, op1=ALU.subtract),
                     waits=[f"lnn_{t}_{st}", war.get("xntok")])
                war[f"st_{(t % 2)}_{st}"] = f"xn_{t}_{st}"
            war[f"xa_sc_{t % 2}"] = f"xn_{t}_{1}"
            war[f"xa_ve_{t % 2}"] = f"lns_{t}_{1}"
            for idx in range(16):
                st, ic = divmod(idx, NCH)
                ps = tp_ps[idx % 2]
                MM(f"tp_{t}_{idx}", ps[:, 0:128], xn_tok[:, st * 1024 + ic * 128: st * 1024 + ic * 128 + 128],
                   ident[:], True, True,
                   waits=[f"xn_{t}_{st}", "c_ident", war.get(f"ps_{idx % 2}")])
                CP(f"tpe_{t}_{idx}", xn0T[:, ic * 256 + st * 128: ic * 256 + st * 128 + 128],
                   ps[:, 0:128], waits=[f"tp_{t}_{idx}", war.get("xnTu"), war.get("xnTu_spill")])
                war[f"ps_{idx % 2}"] = f"tpe_{t}_{idx}"
            war["xntok_pe"] = f"tp_{t}_{15}"
            war["sqb_pe"] = f"tp_{t}_{15}"
            S.dma("sync", f"xnTs_{t}", xnT_scr[t].rearrange("c p q -> p c q"),
                  xn0T[:].rearrange("p (c q) -> p c q", c=NCH),
                  waits=[f"tpe_{t}_{15}"])
            war["xnTu_spill"] = f"xnTs_{t}"
            for pj, (W, bcol, wkey) in enumerate(((WK, 0, "kT"), (WQ, 1, "qT"))):
                pn = "k" if pj == 0 else "q"
                for oc in range(NCH):
                    pi = oc % 2
                    ps = tp_ps[pi]
                    for ic in range(NCH):
                        MM(f"{pn}mm_{t}_{oc}" if ic == NCH - 1 else None,
                           ps[:, 0:256], W[:, ic * 1024 + oc * 128: ic * 1024 + oc * 128 + 128],
                           xn0T[:, ic * 256:(ic + 1) * 256], ic == 0, ic == NCH - 1,
                           waits=([f"tpe_{t}_{15}", f"wc_{wkey}_{NCH-1}", war.get(f"ps_{pi}")] if ic == 0 else []))
                    ACT(f"{pn}ev_{t}_{oc}", keb[pi], ps[:, 0:256], AF.Identity,
                        bias=bkq[:, bcol * NCH + oc: bcol * NCH + oc + 1],
                        waits=[f"{pn}mm_{t}_{oc}", "c_bkq", war.get(f"keb_{pi}")])
                    war[f"ps_{pi}"] = f"{pn}ev_{t}_{oc}"
                    dst = kin[t][oc] if pj == 0 else q_scr[t][oc]
                    S.dma("sync", f"{pn}sp_{t}_{oc}", dst, keb[pi][:, 0:256],
                          waits=[f"{pn}ev_{t}_{oc}"])
                    ACT(f"{pn}sq_{t}_{oc}", sqpp[pi], keb[pi], AF.Square,
                        waits=[f"{pn}ev_{t}_{oc}", war.get(f"sqpp_{pi}")])
                    war[f"keb_{pi}"] = f"{pn}sp_{t}_{oc}"
                    MM(f"{pn}nmm_{t}" if oc == NCH - 1 else f"{pn}nm_{t}_{oc}",
                       psC[0:1, 0:256], ocol[:], sqpp[pi],
                       oc == 0, oc == NCH - 1,
                       waits=[f"{pn}sq_{t}_{oc}"] + (["c_ocol", war.get("psC")] if oc == 0 else []))
                    war[f"sqpp_{pi}"] = f"{pn}nmm_{t}" if oc == NCH - 1 else f"{pn}nm_{t}_{oc}"
                scol = (32 + t) if pj == 1 else (48 + t)
                S.op("vector", f"{pn}nmx_{t}",
                     lambda e, o=stats[0:1, scol:scol + 1], i=psC[0:1, 0:256]:
                     e.reduce_max(o, i, axis=AX.X),
                     waits=[f"{pn}nmm_{t}"])
                war["psC"] = f"{pn}nmx_{t}"
                war["sqb_pe"] = f"{pn}nmm_{t}"
            war["xntok"] = f"tp_{t}_{15}"
            for st in range(2):
                for oh in range(2):
                    pi = (st * 2 + oh) % 2
                    ps = tp_ps[pi]
                    MM(None, ps[:, 0:512], orb[:], bvrow[0:1, oh * 512:(oh + 1) * 512],
                       True, False, waits=["c_bv", "c_orb", war.get(f"ps_{pi}")])
                    for ic in range(NCH):
                        MM(f"vmm_{t}_{st}_{oh}" if ic == NCH - 1 else None,
                           ps[:, 0:512], xn0T[:, ic * 256 + st * 128: ic * 256 + st * 128 + 128],
                           WV[:, ic * 1024 + oh * 512: ic * 1024 + oh * 512 + 512],
                           False, ic == NCH - 1,
                           waits=([f"wc_vT_{NCH-1}"] if ic == 0 else []))
                    CP(f"vev_{t}_{st}_{oh}", veb[pi][:, 0:512], ps[:, 0:512],
                       waits=[f"vmm_{t}_{st}_{oh}", war.get(f"veb_{pi}")])
                    war[f"ps_{pi}"] = f"vev_{t}_{st}_{oh}"
                    S.dma("sync", f"vsp_{t}_{st}_{oh}", vin[t][st][:, oh * 512:(oh + 1) * 512],
                          veb[pi][:, 0:512], waits=[f"vev_{t}_{st}_{oh}"])
                    war[f"veb_{pi}"] = f"vsp_{t}_{st}_{oh}"
            war["xnTu"] = f"vmm_{t}_{1}_{1}"
            S.cc(f"ccK_{t}", lambda e, a=kin[t], o=kout[t]: e.collective_compute(
                "AllGather", ALU.bypass, replica_groups=RG, ins=[a.opt()], outs=[o.opt()]),
                waits=[f"ksp_{t}_{NCH-1}", prev_cc])
            S.cc(f"ccV_{t}", lambda e, a=vin[t], o=vout[t]: e.collective_compute(
                "AllGather", ALU.bypass, replica_groups=RG, ins=[a.opt()], outs=[o.opt()]),
                waits=[f"vsp_{t}_{1}_{1}", f"ccK_{t}"])
            prev_cc = f"ccV_{t}"

        a_last = NT - 1
        S.op("vector", "mx0", lambda e: e.memset(mxsb, 0.0))
        for b in range(2):
            S.op("vector", f"mxb_{b}",
                 lambda e, o=mxsb[0:1, b:b + 1], i=stats[0:1, 48 + b * NU: 48 + (b + 1) * NU]:
                 e.reduce_max(o, i, axis=AX.X),
                 waits=["mx0", f"knmx_{b * NU + NU - 1}"])
        S.dma("sync", "mxsp", mxin, mxsb, waits=["mxb_0", "mxb_1"])
        S.cc("ccAR", lambda e: e.collective_compute(
            "AllReduce", ALU.max, replica_groups=RG, ins=[mxin.opt()], outs=[mxout.opt()]),
            waits=["mxsp", prev_cc])
        S.dma("sync", "gald", garow, mxout, waits=["ccAR"])

        FORDER = ("ufT", "uiT", "ucT", "fT", "iT", "cT", "oT", "uoT")
        a_pe_done = {"fT": f"kmm_{a_last}_{NCH-1}", "iT": f"qmm_{a_last}_{NCH-1}",
                     "cT": f"vmm_{a_last}_{1}_{1}"}
        prev_fcast = {0: "wc_vT_%d" % (NCH - 1), 1: None}
        nchunk = 0
        last_odd_cast = None
        for m in FORDER:
            for ic in range(NCH):
                par = (nchunk % 2) if nchunk < 40 else 0
                S.dma("sync", f"fwl_{m}_{ic}", stgs[par][:, 0:1024], wT[m][ic * 128:(ic + 1) * 128, :],
                      waits=[prev_fcast[par]])
                VCP(f"fwc_{m}_{ic}", FW[m][:, ic * 1024:(ic + 1) * 1024], stgs[par][:, 0:1024],
                    waits=[f"fwl_{m}_{ic}"] + ([a_pe_done[m]] if (m in a_pe_done and ic == 0) else []))
                prev_fcast[par] = f"fwc_{m}_{ic}"
                if par == 1:
                    last_odd_cast = f"fwc_{m}_{ic}"
                nchunk += 1
        CP("incb", incb[:], incol[:], waits=["c_incol"])
        S.dma("sync", "incld", increg, incrow_in)
        for mi, m in enumerate(("f", "i", "c")):
            um = ("ufT", "uiT", "ucT")[mi]
            S.dma("sync", f"brl_{m}", brow_f, brows_in[1 + mi:2 + mi, :],
                  waits=["c_bv" if mi == 0 else f"bp_{('f','i','c')[mi-1]}_{1}"])
            for oh in range(2):
                for ic in range(NCH):
                    MM(f"zmm_{m}_{oh}" if ic == NCH - 1 else None,
                       psX[0:1, 0:512], incb[:, ic:ic + 1],
                       FW[um][:, ic * 1024 + oh * 512: ic * 1024 + oh * 512 + 512],
                       ic == 0, ic == NCH - 1,
                       waits=(["incb", f"fwc_{um}_{NCH-1}", war.get("ps_0"),
                               (f"bp_{m}_{0}" if oh == 1 else None)] if ic == 0 else []))
                TT(f"bp_{m}_{oh}", bprow[m][0:1, oh * 512:(oh + 1) * 512],
                   psX[0:1, 0:512], brow_f[0:1, oh * 512:(oh + 1) * 512], ALU.add,
                   waits=[f"zmm_{m}_{oh}", f"brl_{m}",
                          (war.get("bvrow_pe") if m == "f" and oh == 0 else None)])
            war["ps_0"] = f"bp_{m}_{1}"
        S.dma("sync", "brl_o", brow_f, brows_in[4:5, :], waits=[f"bp_c_{1}"])
        VCP("c_bo", borow, brow_f, waits=["brl_o"])
        for oh in range(2):
            MM(f"c0mm_{oh}", psY[:, 0:512], orf[:], increg[0:1, oh * 512:(oh + 1) * 512],
               True, True, waits=["incld", "c_orf", war.get("ps_1"), (f"c0b_{0}" if oh else None)])
            CP(f"c0b_{oh}", c0b[:, oh * 512:(oh + 1) * 512], psY[:, 0:512], waits=[f"c0mm_{oh}"])
            war["ps_1"] = f"c0b_{oh}"
        war["increg"] = f"c0mm_{1}"

        GPS = [psA, psB, psC]
        a_sc_done = f"qsq_{a_last}_{NCH-1}"
        a_ve_done = f"qnmx_{a_last}"
        war["h0f"] = a_sc_done
        war["c1T_pe"] = f"qsp_{a_last}_{NCH-1}"
        war["c2T_pe"] = f"vsp_{a_last}_{1}_{1}"
        war["bvrow_pe"] = f"vmm_{a_last}_{1}_{1}"
        war["xn1T_war"] = f"qnmm_{a_last}"
        for u in range(NU):
            S.dma("sync", f"fx0_{u}", xn0T[:].rearrange("p (c q) -> p c q", c=NCH),
                  xnT_scr[u].rearrange("c p q -> p c q"),
                  waits=[war["xnTu"], f"xnTs_{u}"])
            S.dma("sync", f"fx1_{u}", xn1T[:].rearrange("p (c q) -> p c q", c=NCH),
                  xnT_scr[NU + u].rearrange("c p q -> p c q"),
                  waits=[war["xn1T_war"], a_sc_done, f"xnTs_{NU + u}"])
            for st in range(2):
                sb = f"{u}_{st}"
                for g in range(3):
                    for oh in range(2):
                        MM(None, GPS[g][:, oh * 512:(oh + 1) * 512], orb[:],
                           bprow["fic"[g]][0:1, oh * 512:(oh + 1) * 512], True, False,
                           waits=[f"bp_{'fic'[g]}_{oh}", war.get(f"gps_{g}"),
                                  (a_ve_done if (u == 0 and st == 0 and g == 2) else None)])
                for ic in range(NCH):
                    lh = xn0T[:, ic * 256 + st * 128: ic * 256 + st * 128 + 128]
                    for g in range(3):
                        wm = FW[("fT", "iT", "cT")[g]]
                        for oh in range(2):
                            MM(f"g1m_{sb}_{g}" if (ic == NCH - 1 and oh == 1) else None,
                               GPS[g][:, oh * 512:(oh + 1) * 512], lh,
                               wm[:, ic * 1024 + oh * 512: ic * 1024 + oh * 512 + 512],
                               False, ic == NCH - 1,
                               waits=([f"fx0_{u}", f"fwc_{('fT','iT','cT')[g]}_{NCH-1}"]
                                      if (ic == 0 and oh == 0) else []))
                for g in range(3):
                    ACT(f"g1e_{sb}_{g}", gate[g], GPS[g][:],
                        AF.Sigmoid if g < 2 else AF.Tanh,
                        waits=[f"g1m_{sb}_{g}", war.get(f"gate_{g}"),
                               (a_ve_done if (u == 0 and st == 0) else None)])
                    war[f"gps_{g}"] = f"g1e_{sb}_{g}"
                TT(f"ct1_{sb}", c1f, gate[0], c0b, ALU.mult,
                   waits=[f"g1e_{sb}_{0}", f"c0b_{1}", war.get("c1f"), (a_sc_done if (u == 0 and st == 0) else None)])
                TT(f"ct2_{sb}", h0f, gate[1], gate[2], ALU.mult,
                   waits=[f"g1e_{sb}_{1}", f"g1e_{sb}_{2}", war.get("h0f")])
                TT(f"c1_{sb}", c1f, c1f, h0f, ALU.add, waits=[f"ct1_{sb}", f"ct2_{sb}"])
                CP(f"c1b_{sb}", c1bf[:], c1f, waits=[f"c1_{sb}", war.get("c1bf")])
                for ic in range(NCH):
                    ps = tp_ps[ic % 2]
                    MM(f"c1tp_{sb}_{ic}", ps[:, 0:128], c1bf[:, ic * 128:(ic + 1) * 128],
                       ident[:], True, True, waits=[f"c1b_{sb}", war.get(f"ps_{ic % 2}")])
                    CP(f"c1T_{sb}_{ic}", c1T[:, ic * 128:(ic + 1) * 128], ps[:, 0:128],
                       waits=[f"c1tp_{sb}_{ic}", war.get("c1T_pe")])
                    war[f"ps_{ic % 2}"] = f"c1T_{sb}_{ic}"
                for g in range(3):
                    for oh in range(2):
                        MM(None, GPS[g][:, oh * 512:(oh + 1) * 512], orb[:],
                           bprow["fic"[g]][0:1, oh * 512:(oh + 1) * 512], True, False,
                           waits=[war.get(f"gps_{g}")])
                for ic in range(NCH):
                    lh = xn1T[:, ic * 256 + st * 128: ic * 256 + st * 128 + 128]
                    for g in range(3):
                        wm = FW[("fT", "iT", "cT")[g]]
                        for oh in range(2):
                            MM(None, GPS[g][:, oh * 512:(oh + 1) * 512], lh,
                               wm[:, ic * 1024 + oh * 512: ic * 1024 + oh * 512 + 512],
                               False, False,
                               waits=([f"fx1_{u}"] if (ic == 0 and g == 0 and oh == 0) else []))
                for mc in range(NCH):
                    lh = c1T[:, mc * 128:(mc + 1) * 128]
                    for g in range(3):
                        um = FW[("ufT", "uiT", "ucT")[g]]
                        for oh in range(2):
                            MM(f"g2m_{sb}_{g}" if (mc == NCH - 1 and oh == 1) else None,
                               GPS[g][:, oh * 512:(oh + 1) * 512], lh,
                               um[:, mc * 1024 + oh * 512: mc * 1024 + oh * 512 + 512],
                               False, mc == NCH - 1,
                               waits=([f"c1T_{sb}_{NCH-1}"] if (mc == 0 and g == 0 and oh == 0) else []))
                for g in range(3):
                    ACT(f"g2e_{sb}_{g}", gate[g], GPS[g][:],
                        AF.Sigmoid if g < 2 else AF.Tanh,
                        waits=[f"g2m_{sb}_{g}", f"c1_{sb}" if g == 0 else None,
                               f"ct2_{sb}" if g > 0 else None])
                    war[f"gps_{g}"] = f"g2e_{sb}_{g}"
                TT(f"dt1_{sb}", c2f, gate[0], c1f, ALU.mult,
                   waits=[f"g2e_{sb}_{0}", war.get("c2f"),
                          (last_odd_cast if (u == 0 and st == 0) else None)])
                TT(f"dt2_{sb}", h1f, gate[1], gate[2], ALU.mult,
                   waits=[f"g2e_{sb}_{1}", f"g2e_{sb}_{2}", war.get("h1f")])
                TT(f"c2_{sb}", c2f, c2f, h1f, ALU.add, waits=[f"dt1_{sb}", f"dt2_{sb}"])
                war["c1f"] = f"dt1_{sb}"
                CP(f"c2b_{sb}", c1bf[:], c2f, waits=[f"c2_{sb}", f"c1tp_{sb}_{NCH-1}"])
                war["c1bf"] = f"c2b_{sb}"
                for ic in range(NCH):
                    ps = tp_ps[ic % 2]
                    MM(f"c2tp_{sb}_{ic}", ps[:, 0:128], c1bf[:, ic * 128:(ic + 1) * 128],
                       ident[:], True, True, waits=[f"c2b_{sb}", war.get(f"ps_{ic % 2}")])
                    CP(f"c2T_{sb}_{ic}", c2T[:, ic * 128:(ic + 1) * 128], ps[:, 0:128],
                       waits=[f"c2tp_{sb}_{ic}", war.get("c2T_pe")])
                    war[f"ps_{ic % 2}"] = f"c2T_{sb}_{ic}"
                war["c1bf"] = f"c2tp_{sb}_{NCH-1}"
                war["c1T_pe"] = f"g2m_{sb}_{2}"
                for hb in range(2):
                    ps = GPS[hb]
                    xnsrc = xn0T if hb == 0 else xn1T
                    hslot = h0f if hb == 0 else h1f
                    for oh in range(2):
                        MM(None, ps[:, oh * 512:(oh + 1) * 512], orb[:],
                           borow[0:1, oh * 512:(oh + 1) * 512], True, False,
                           waits=["c_bo", war.get(f"gps_{hb}")])
                        for ic in range(NCH):
                            MM(None, ps[:, oh * 512:(oh + 1) * 512],
                               xnsrc[:, ic * 256 + st * 128: ic * 256 + st * 128 + 128],
                               FW["oT"][:, ic * 1024 + oh * 512: ic * 1024 + oh * 512 + 512],
                               False, False,
                               waits=([f"fwc_oT_{NCH-1}"] if (oh == 0 and ic == 0) else []))
                        for mc in range(NCH):
                            MM(f"hm_{sb}_{hb}" if (oh == 1 and mc == NCH - 1) else None,
                               ps[:, oh * 512:(oh + 1) * 512], c2T[:, mc * 128:(mc + 1) * 128],
                               FW["uoT"][:, mc * 1024 + oh * 512: mc * 1024 + oh * 512 + 512],
                               False, mc == NCH - 1,
                               waits=([f"c2T_{sb}_{NCH-1}", f"fwc_uoT_{NCH-1}"] if (oh == 0 and mc == 0) else []))
                    ACT(f"he_{sb}_{hb}", hslot, ps[:], AF.Tanh,
                        waits=[f"hm_{sb}_{hb}", (f"c2_{sb}" if hb == 1 else f"c1_{sb}"),
                               war.get("h0sp" if hb == 0 else "h1sp")])
                    war[f"gps_{hb}"] = f"he_{sb}_{hb}"
                    S.dma("sync", f"hsp_{sb}_{hb}", h_scr[hb * NU + u][st * 128:(st + 1) * 128, :],
                          hslot, waits=[f"he_{sb}_{hb}"])
                    war["h0sp" if hb == 0 else "h1sp"] = f"hsp_{sb}_{hb}"
                war["h0f"] = f"hsp_{sb}_{0}"
                war["h1f"] = f"hsp_{sb}_{1}"
                war["c2f"] = f"hm_{sb}_{1}"
                war["c2T_pe"] = f"hm_{sb}_{1}"
                war["gate_0"] = f"dt1_{sb}"
                war["gate_1"] = f"dt2_{sb}"
                war["gate_2"] = f"dt2_{sb}"
            war["xnTu"] = f"hm_{u}_{1}_{0}"
            war["xn1T_war"] = f"hm_{u}_{1}_{1}"
        ffn_pe_done = f"hm_{NU-1}_{1}_{1}"
        ffn_sc_done = f"he_{NU-1}_{1}_{1}"
        ffn_ve_done = f"c2_{NU-1}_{1}"

        ACT("sq1", sq1r[0:1, 0:NT], stats[0:1, 32:32 + NT], AF.Sqrt,
            waits=[f"qnmx_{a_last}"])
        ACT("sqga", sqga[0:1, 0:2], garow[0:1, 0:2], AF.Sqrt, waits=["gald"])
        for b in range(2):
            S.op("vector", f"gax_{b}",
                 lambda e, o=gaexp[0:1, b * NU:(b + 1) * NU], i=orf[0:1, 0:NU], s=sqga[0:1, b:b + 1]:
                 e.tensor_scalar(out=o, in0=i, scalar1=s, scalar2=None, op0=ALU.mult),
                 waits=["sqga", "c_orf"])
        TT("prod", prodr[0:1, 0:NT], sq1r[0:1, 0:NT], gaexp[0:1, 0:NT], ALU.mult,
           waits=["gax_0", "gax_1", "sq1"])
        TS("negrow", negrow[0:1, 0:NT], prodr[0:1, 0:NT], f32(-1.0 / 32.0), None, ALU.mult,
           waits=["prod"])
        for t in range(NT):
            MM(f"ncb_{t}", psY[:, 0:1], orf[:], negrow[0:1, t:t + 1], True, True,
               waits=["negrow", war.get("ps_1"), ffn_pe_done if t == 0 else None])
            CP(f"ncc_{t}", negc[:, t:t + 1], psY[:, 0:1], waits=[f"ncb_{t}"])
            war["ps_1"] = f"ncc_{t}"

        war["stp_0"] = war.get("ps_0")
        war["stp_1"] = war.get("ps_0")
        war["pvacc"] = ffn_sc_done
        war["psC"] = war.get("gps_2")
        for _k in range(4):
            war[f"kt_{_k}"] = ffn_pe_done
        for _k in range(4):
            war[f"vt_{_k}"] = ffn_pe_done
        war["pb_0"] = ffn_pe_done
        war["pb_1"] = ffn_pe_done
        pbkey = {}
        for t in range(NT):
            b, u = divmod(t, NU)
            nk = (u + 1) * 16
            S.dma("sync", f"aq_{t}", qbuf[:].rearrange("p (c q) -> p c q", c=NCH),
                  q_scr[t].rearrange("c p q -> p c q"),
                  waits=[war.get("qbuf") or ffn_pe_done])
            S.op("vector", f"az_{t}", lambda e: e.memset(accv, 0.0),
                 waits=[war.get("accv") or ffn_ve_done,
                        (ffn_sc_done if t == 0 else None)])
            def emit_lpv(kc, t=t, u=u, nk=nk):
                sh = kc % 2
                kci = (kc // 2) % 4
                pk = pbkey[(t, kc)]
                MM(f"lm_{t}" if kc == nk - 1 else None, psC[0:1, 0:256], ocol[:], pb[sh],
                   kc == 0, kc == nk - 1,
                   waits=[pk] + ([war.get("psC"), "c_ocol"] if kc == 0 else []))
                for qs in range(2):
                    for oh in range(2):
                        MM(f"pv_{t}_{kc}" if (qs == 1 and oh == 1) else None,
                           GPS[qs][:, oh * 512:(oh + 1) * 512],
                           pb[sh][:, qs * 128:(qs + 1) * 128],
                           vtile[kci][:, (kc % 2) * 1024 + oh * 512:(kc % 2) * 1024 + oh * 512 + 512],
                           kc % 4 == 0, kc % 4 == 3,
                           waits=([f"avt_{t}_{kc // 2}"] if (qs == 0 and oh == 0) else []) +
                                 ([war.get("pvacc")] if (qs == 0 and oh == 0 and kc % 4 == 0) else []))
                war[f"pb_{sh}"] = f"pv_{t}_{kc}"
                war[f"vt_{kci}"] = f"pv_{t}_{kc}"
                if kc % 4 == 3:
                    blk = kc // 4
                    for seg in range(4):
                        TT(f"aa_{t}_{blk}" if seg == 3 else None,
                           accv[:, seg * 512:(seg + 1) * 512], accv[:, seg * 512:(seg + 1) * 512],
                           GPS[seg // 2][:, (seg % 2) * 512:(seg % 2 + 1) * 512], ALU.add,
                           waits=([f"pv_{t}_{kc}", f"az_{t}"] if seg == 0 else []))
                    war["pvacc"] = f"aa_{t}_{blk}"

            for kc in range(nk):
                c256, hh = divmod(kc, 2)
                uu, r = divmod(c256, NCORE)
                kci = kc % 8
                if hh == 0:
                    ks = c256 % 4
                    S.dma("sync", f"akt_{t}_{c256}",
                          ktile[ks][:].rearrange("p (c q) -> p c q", c=NCH),
                          kout[b * NU + uu][r * NCH:(r + 1) * NCH].rearrange("c p q -> p c q"),
                          waits=[f"ccK_{b * NU + uu}", war.get(f"kt_{ks}")])
                    S.dma("gpsimd", f"avt_{t}_{c256}",
                          vtile[ks][:].rearrange("p (s d) -> p s d", s=2),
                          vout[b * NU + uu][r * 2:(r + 1) * 2].rearrange("s p d -> p s d"),
                          waits=[f"ccV_{b * NU + uu}", war.get(f"vt_{ks}")])
                sh = kc % 2
                stp = psX[:, sh * 256:(sh + 1) * 256]
                ks = c256 % 4
                for oc in range(NCH):
                    MM(f"qk_{t}_{kc}" if oc == NCH - 1 else None, stp,
                       ktile[ks][:, oc * 256 + hh * 128: oc * 256 + hh * 128 + 128],
                       qbuf[:, oc * 256:(oc + 1) * 256], oc == 0, oc == NCH - 1,
                       waits=([f"akt_{t}_{c256}", f"aq_{t}", war.get(f"stp_{sh}")] if oc == 0 else []))
                ACT(f"ex_{t}_{kc}", pb[sh], stp, AF.Exp, bias=negc[:, t:t + 1],
                    scale=f32(1.0 / 32.0),
                    waits=[f"qk_{t}_{kc}", f"ncc_{t}", war.get(f"pb_{sh}")])
                war[f"stp_{sh}"] = f"ex_{t}_{kc}"
                pkey = f"ex_{t}_{kc}"
                if kc >= u * 16:
                    mi = kc - u * 16
                    TT(f"mk_{t}_{kc}", pb[sh], pb[sh], masks[:, mi * 256:(mi + 1) * 256],
                       ALU.mult, waits=[pkey, "c_masks"])
                    pkey = f"mk_{t}_{kc}"
                pbkey[(t, kc)] = pkey
                war[f"kt_{ks}"] = f"qk_{t}_{kc}"
                emit_lpv(kc)
            war["qbuf"] = f"qk_{t}_{nk-1}"
            VCP(f"lr_{t}", lrow, psC[0:1, 0:256],
                waits=[f"lm_{t}", war.get("lrow"), (ffn_sc_done if t == 0 else None)])
            war["psC"] = f"lr_{t}"
            for qs in range(2):
                MM(f"ltp_{t}_{qs}", psY[:, 0:1], lrow[0:1, qs * 128:(qs + 1) * 128],
                   orf[0:1, 0:1], True, True, waits=[f"lr_{t}", war.get("ps_1")])
                war["lrow"] = f"ltp_{t}_{qs}"
                S.op("vector", f"li_{t}_{qs}",
                     lambda e, o=linv[:, qs:qs + 1], i=psY[:, 0:1]: e.reciprocal(o, i),
                     waits=[f"ltp_{t}_{qs}"])
                war["ps_1"] = f"li_{t}_{qs}"
                TS(f"an_{t}_{qs}", accv[:, qs * 1024:(qs + 1) * 1024],
                   accv[:, qs * 1024:(qs + 1) * 1024], linv[:, qs:qs + 1], None, ALU.mult,
                   waits=[f"li_{t}_{qs}", f"aa_{t}_{nk // 4 - 1}"])
            if DEBUG:
                S.dma("sync", f"dacc_{t}", d_acc[t].rearrange("(s p) d -> p s d", s=2),
                      accv[:].rearrange("p (s d) -> p s d", s=2), waits=[f"an_{t}_{1}"])
                S.dma("sync", f"dl_{t}", d_l[t], lrow, waits=[f"lr_{t}"])
                S.dma("sync", f"dh_{t}", d_h[t], h_scr[t], waits=[f"hsp_{u}_{1}_{b}"])
                S.dma("sync", f"dxnT_{t}", d_xnT[t], xnT_scr[t], waits=[f"xnTs_{t}"])
                S.dma("sync", f"dko_{t}", d_kout[t], kout[t], waits=[f"ccK_{t}"])
                S.dma("sync", f"dvo_{t}", d_vout[t], vout[t], waits=[f"ccV_{t}"])
            S.dma("gpsimd", f"ax_{t}", xrt[:].rearrange("p (s d) -> p s d", s=2),
                  x_own[t].rearrange("(s p) d -> p s d", s=2),
                  waits=[war.get("xrt") or ffn_ve_done, war.get("xrt_pe") or ffn_pe_done])
            S.dma("gpsimd", f"ah_{t}", hrt[:].rearrange("p (s d) -> p s d", s=2),
                  h_scr[t].rearrange("(s p) d -> p s d", s=2),
                  waits=[f"hsp_{u}_{1}_{b}", war.get("hrt") or ffn_ve_done,
                         (ffn_pe_done if t == 0 else None)])
            TT(f"ao1_{t}", accv, accv, xrt, ALU.add,
               waits=[f"an_{t}_{1}", f"ax_{t}"] + ([f"dacc_{t}"] if DEBUG else []))
            TT(f"ao2_{t}", accv, accv, hrt, ALU.add, waits=[f"ao1_{t}", f"ah_{t}"])
            war["xrt"] = f"ao1_{t}"
            war["hrt"] = f"ao2_{t}"
            war["xrt_pe"] = f"ao1_{t}"
            S.dma("gpsimd", f"out_{t}", y_own[t].rearrange("(s p) d -> p s d", s=2),
                  accv[:].rearrange("p (s d) -> p s d", s=2), waits=[f"ao2_{t}"])
            war["accv"] = f"out_{t}"
        if DEBUG:
            S.dma("sync", "dnegc", d_negc, negc[:], waits=[f"ncc_{NT - 1}"])
            S.dma("sync", "dstats", d_stats, stats[:], waits=[f"ncc_{NT - 1}"])
            S.dma("sync", "drows", d_rows, pf[0:1, 11264:11440], waits=[f"ncc_{NT - 1}"])
        S.op("sync", None, lambda e: e.nop(), waits=[f"out_{NT - 1}"] + (["dnegc"] if DEBUG else []))

        S.finalize(nc, stack, serial=SERIAL)
    return nc


def kernel(L0_attn_x, L0_attn_Wk, L0_attn_bk, L0_attn_Wq, L0_attn_bq, L0_attn_Wv,
           L0_attn_bv, L0_ffn_wf, L0_ffn_uf, L0_ffn_bf, L0_ffn_wi, L0_ffn_ui,
           L0_ffn_bi, L0_ffn_wc, L0_ffn_uc, L0_ffn_bc, L0_ffn_wo, L0_ffn_uo,
           L0_ffn_bo, L0_ffn_in_c):
    x = np.asarray(L0_attn_x, dtype=np.float32)
    B, Sq, Dm = x.shape
    assert B == 2 and Dm == D and Sq % (NCORE * CH) == 0
    NU = Sq // (NCORE * CH)

    def wt(a):
        return np.ascontiguousarray(np.asarray(a, dtype=np.float32).T)

    weights = {"w_kT": wt(L0_attn_Wk), "w_qT": wt(L0_attn_Wq), "w_vT": wt(L0_attn_Wv),
               "w_fT": wt(L0_ffn_wf), "w_iT": wt(L0_ffn_wi), "w_cT": wt(L0_ffn_wc),
               "w_ufT": wt(L0_ffn_uf), "w_uiT": wt(L0_ffn_ui), "w_ucT": wt(L0_ffn_uc),
               "w_oT": wt(L0_ffn_wo), "w_uoT": wt(L0_ffn_uo)}
    bk = np.asarray(L0_attn_bk, np.float32).reshape(NCH, 128).T
    bq = np.asarray(L0_attn_bq, np.float32).reshape(NCH, 128).T
    bkq = np.ascontiguousarray(np.concatenate([bk, bq], axis=1))
    brows = np.ascontiguousarray(np.stack([
        np.asarray(L0_attn_bv, np.float32), np.asarray(L0_ffn_bf, np.float32),
        np.asarray(L0_ffn_bi, np.float32), np.asarray(L0_ffn_bc, np.float32),
        np.asarray(L0_ffn_bo, np.float32)]))
    incrow = np.ascontiguousarray(np.asarray(L0_ffn_in_c, np.float32).reshape(1, D))
    incol = np.ascontiguousarray(np.asarray(L0_ffn_in_c, np.float32).reshape(NCH, 128).T)
    ident = np.eye(128, dtype=BFDT)
    ocolv = np.ones((128, 1), dtype=BFDT)
    orow_f = np.ones((1, 128), dtype=np.float32)
    orow_b = np.ones((1, 128), dtype=BFDT)

    xr = x.reshape(B, Sq // CH, CH, D)
    kk, qq = np.meshgrid(np.arange(128), np.arange(CH), indexing="ij")
    in_maps = []
    for j in range(NCORE):
        idx = np.arange(NU) * NCORE + j
        x_ownj = np.ascontiguousarray(xr[:, idx].reshape(2 * NU, CH, D))
        mt = np.zeros((16, 128, CH), dtype=BFDT)
        for mi in range(16):
            m, hh2 = divmod(mi, 2)
            if m < j:
                mt[mi] = 1
            elif m == j:
                mt[mi] = ((hh2 * 128 + kk) <= qq).astype(BFDT)
        im = {"x_own": x_ownj, "bkq": bkq, "brows": brows, "incrow": incrow,
              "incol": incol, "masks": mt, "ident": ident, "ocol": ocolv,
              "orow_f": orow_f, "orow_b": orow_b}
        im.update(weights)
        in_maps.append(im)

    nc = build(NU)
    res = run_bass_kernel_spmd(nc, in_maps, core_ids=list(range(NCORE)))
    out = np.empty((B, Sq, D), dtype=np.float32)
    for j in range(NCORE):
        yj = np.asarray(res.results[j]["y_own"])
        for b in range(B):
            for i in range(NU):
                c = i * NCORE + j
                out[b, c * CH:(c + 1) * CH] = yj[b * NU + i]
    return out


# version 07
# speedup vs baseline: 1.0398x; 1.0256x over previous
import contextlib
import numpy as np
import ml_dtypes
import concourse.bass as bass
import concourse.mybir as mybir
from concourse.bass_utils import run_bass_kernel_spmd

F32 = mybir.dt.float32
BF16 = mybir.dt.bfloat16
AF = mybir.ActivationFunctionType
ALU = mybir.AluOpType
AX = mybir.AxisListType
BFDT = ml_dtypes.bfloat16

DEBUG = False
SERIAL = False
NCORE = 8
D = 1024
NCH = D // 128
CH = 256
RG = [list(range(NCORE))]
f32 = np.float32


class Sched:
    ENGS = ("sync", "gpsimd", "vector", "scalar", "tensor")
    ESEM = {"sync": "sy", "gpsimd": "gp", "vector": "ve", "scalar": "sc", "tensor": "pe"}
    DSEM = {"sync": "dsy", "gpsimd": "dgp", "vector": "dve", "scalar": "dsc"}

    def __init__(self):
        self.ops = []
        self.keys = {}

    def op(self, eng, key, fn, waits=()):
        self.ops.append((eng, fn, tuple(w for w in waits if w), key, 1, self.ESEM[eng]))

    def dma(self, q, key, out, in_, waits=()):
        self.ops.append((q, (lambda e, o=out, i=in_: e.dma_start(out=o, in_=i)),
                         tuple(w for w in waits if w), key, 16, self.DSEM[q]))

    def cc(self, key, fn, waits=()):
        self.ops.append(("gpsimd", fn, tuple(w for w in waits if w), key, 1, "cc"))

    def finalize(self, nc, stack, serial=False):
        if serial:
            ops2 = []
            prev = None
            for n, (eng, fn, waits, key, inc, sem) in enumerate(self.ops):
                if key is None:
                    key = f"__auto{n}"
                waits = tuple(waits) + ((prev,) if prev else ())
                ops2.append((eng, fn, waits, key, inc, sem))
                prev = key
            self.ops = ops2
        sems = {s: stack.enter_context(nc.semaphore("s_" + s))
                for s in set(list(self.ESEM.values()) + list(self.DSEM.values()) + ["cc"])}
        ctr = {s: 0 for s in sems}
        chained = []
        for eng, fn, waits, key, inc, sem in self.ops:
            cw = None
            if key is not None:
                if sem not in self.ESEM.values() and ctr[sem] > 0:
                    cw = (sem, ctr[sem])
                ctr[sem] += inc
                assert key not in self.keys, f"dup key {key}"
                self.keys[key] = (sem, ctr[sem])
            chained.append(cw)
        self.chainw = chained
        for eng2, fn2, waits2, key2, inc2, sem2 in self.ops:
            for w in waits2:
                assert w in self.keys, f"wait on undefined key {w}"
        block = stack.enter_context(nc.Block())

        def mk_stream(eng):
            myops = [o for o in self.ops if o[0] == eng]
            mysem = self.ESEM[eng]

            mychain = [self.chainw[n] for n, o in enumerate(self.ops) if o[0] == eng]

            def stream(e):
                hw = {s: 0 for s in sems}
                for (_, fn, waits, key, inc, sem), cw in zip(myops, mychain):
                    for w in waits:
                        ws, wv = self.keys[w]
                        if wv <= hw[ws]:
                            continue
                        e.wait_ge(sems[ws], wv)
                        hw[ws] = wv
                    if cw is not None and cw[1] > hw[cw[0]]:
                        e.wait_ge(sems[cw[0]], cw[1])
                        hw[cw[0]] = cw[1]
                    instr = fn(e)
                    if key is not None:
                        instr.then_inc(sems[sem], inc)
            return stream

        for eng in self.ENGS:
            getattr(block, eng)(mk_stream(eng))


def build(NU):
    nc = bass.Bass("TRN2", target_bir_lowering=False)
    NT = 2 * NU

    def din(name, shape, dt):
        return nc.dram_tensor(name, shape, dt, kind="ExternalInput").ap()

    x_own = din("x_own", [NT, CH, D], F32)
    WNAMES = ("kT", "qT", "vT", "fT", "iT", "cT", "ufT", "uiT", "ucT", "oT", "uoT")
    wT = {m: din("w_" + m, [D, D], F32) for m in WNAMES}
    bkq_in = din("bkq", [128, 2 * NCH], F32)
    brows_in = din("brows", [5, D], F32)
    incrow_in = din("incrow", [1, D], F32)
    incol_in = din("incol", [128, NCH], F32)
    masks_in = din("masks", [16, 128, CH], BF16)
    ident_in = din("ident", [128, 128], BF16)
    ocol_in = din("ocol", [128, 1], BF16)
    orf_in = din("orow_f", [1, 128], F32)
    orb_in = din("orow_b", [1, 128], BF16)

    y_own = nc.dram_tensor("y_own", [NT, CH, D], F32, kind="ExternalOutput").ap()
    if DEBUG:
        d_h = nc.dram_tensor("d_h", [NT, CH, D], F32, kind="ExternalOutput").ap()
        d_acc = nc.dram_tensor("d_acc", [NT, CH, D], F32, kind="ExternalOutput").ap()
        d_xnT = nc.dram_tensor("d_xnT", [NT, NCH, 128, CH], BF16, kind="ExternalOutput").ap()
        d_l = nc.dram_tensor("d_l", [NT, 1, CH], F32, kind="ExternalOutput").ap()
        d_negc = nc.dram_tensor("d_negc", [128, 32], F32, kind="ExternalOutput").ap()
        d_stats = nc.dram_tensor("d_stats", [128, 64], F32, kind="ExternalOutput").ap()
        d_rows = nc.dram_tensor("d_rows", [1, 176], F32, kind="ExternalOutput").ap()
        d_kout = nc.dram_tensor("d_kout", [NT, NCORE * NCH, 128, CH], BF16, kind="ExternalOutput").ap()
        d_vout = nc.dram_tensor("d_vout", [NT, NCORE * 2, 128, D], BF16, kind="ExternalOutput").ap()

    xnT_scr = nc.dram_tensor("xnT_scr", [NT, NCH, 128, CH], BF16, kind="Internal").ap()
    q_scr = nc.dram_tensor("q_scr", [NT, NCH, 128, CH], BF16, kind="Internal").ap()
    h_scr = nc.dram_tensor("h_scr", [NT, CH, D], F32, kind="Internal").ap()
    kin = [nc.dram_tensor(f"kin_{t}", [NCH, 128, CH], BF16, kind="Internal").ap() for t in range(NT)]
    kout = [nc.dram_tensor(f"kout_{t}", [NCORE * NCH, 128, CH], BF16, kind="Internal").ap() for t in range(NT)]
    vin = [nc.dram_tensor(f"vin_{t}", [2, 128, D], BF16, kind="Internal").ap() for t in range(NT)]
    vout = [nc.dram_tensor(f"vout_{t}", [NCORE * 2, 128, D], BF16, kind="Internal").ap() for t in range(NT)]
    mxin = nc.dram_tensor("mxin", [1, 16], F32, kind="Internal").ap()
    mxout = nc.dram_tensor("mxout", [1, 16], F32, kind="Internal").ap()

    with contextlib.ExitStack() as stack:
        E = stack.enter_context
        WCOL = 8 * NCH * 1024
        pbf = E(nc.sbuf_tensor("pbf", [128, WCOL + 11264], BF16))
        pf = E(nc.sbuf_tensor("pf", [128, 11520], F32))
        masks = E(nc.sbuf_tensor("masks_sb", [128, 16 * CH], BF16))
        stats = E(nc.sbuf_tensor("stats", [128, 64], F32))
        negc = E(nc.sbuf_tensor("negc", [128, 32], F32))
        linv = E(nc.sbuf_tensor("linv", [128, 4], F32))
        bkq = E(nc.sbuf_tensor("bkq_sb", [128, 2 * NCH], F32))
        incol = E(nc.sbuf_tensor("incol_sb", [128, NCH], F32))
        incb = E(nc.sbuf_tensor("incb", [128, NCH], BF16))
        ident = E(nc.sbuf_tensor("ident_sb", [128, 128], BF16))
        ocol = E(nc.sbuf_tensor("ocol_sb", [128, 1], BF16))
        orf = E(nc.sbuf_tensor("orf", [1, 128], F32))
        orb = E(nc.sbuf_tensor("orb", [1, 128], BF16))
        psA = E(nc.psum_tensor("psA", [128, 1024], F32))
        psB = E(nc.psum_tensor("psB", [128, 1024], F32))
        psC = E(nc.psum_tensor("psC", [128, 1024], F32))
        psX = E(nc.psum_tensor("psX", [128, 512], F32))
        psY = E(nc.psum_tensor("psY", [128, 512], F32))

        def wslot(i):
            return pbf[:, i * NCH * 1024:(i + 1) * NCH * 1024]
        WK, WQ, WV = wslot(0), wslot(1), wslot(2)
        FW = {"ufT": wslot(0), "uiT": wslot(1), "ucT": wslot(2), "fT": wslot(3),
              "iT": wslot(4), "cT": wslot(5), "oT": wslot(6), "uoT": wslot(7)}
        qbuf = pbf[:, 0:2048]
        ktile = [pbf[:, 4096 + k * 2048: 4096 + (k + 1) * 2048] for k in range(4)]
        vtile = [pbf[:, 12288 + k * 2048: 12288 + (k + 1) * 2048] for k in range(4)]
        pb = [pbf[:, 20480 + k * CH: 20480 + (k + 1) * CH] for k in range(2)]
        xn0T = pbf[:, WCOL + 0: WCOL + 2048]
        xn1T = pbf[:, WCOL + 2048: WCOL + 4096]
        xn_tok = xn1T
        sqb = xn1T
        c1T = pbf[:, WCOL + 4096: WCOL + 5120]
        c2T = pbf[:, WCOL + 5120: WCOL + 6144]
        c1bf = pbf[:, WCOL + 6144: WCOL + 7168]
        keb = [pbf[:, WCOL + 4096 + i * CH: WCOL + 4096 + (i + 1) * CH] for i in range(2)]
        veb = [pbf[:, WCOL + 4608 + i * 512: WCOL + 4608 + (i + 1) * 512] for i in range(2)]
        bvrow = pbf[0:1, WCOL + 7168: WCOL + 8192]
        bprow = {"f": pbf[0:1, WCOL + 7168: WCOL + 8192],
                 "i": pbf[0:1, WCOL + 8192: WCOL + 9216],
                 "c": pbf[0:1, WCOL + 9216: WCOL + 10240]}
        borow = pbf[0:1, WCOL + 10240: WCOL + 11264]

        xa = pf[:, 0:2048]
        xa2 = pf[:, 7168:9216]
        gate = [pf[:, 0:1024], pf[:, 1024:2048], pf[:, 2048:3072]]
        sqd = pf[:, 3072:4096]
        c1f = pf[:, 3072:4096]
        c2f = pf[:, 4096:5120]
        stg = pf[:, 5120:6144]
        stgs = [pf[:, 5120:6144], pf[:, 4096:5120]]
        c0b = pf[:, 6144:7168]
        h0f = pf[:, 7168:8192]
        h1f = pf[:, 8192:9216]
        increg = pf[0:1, 9216:10240]
        brow_f = pf[0:1, 10240:11264]
        accv = pf[:, 0:2048]
        xrt = pf[:, 2048:4096]
        hrt = pf[:, 4096:6144]
        lrow = pf[0:1, 6144:6400]
        mxsb = pf[0:1, 11264:11280]
        garow = pf[0:1, 11280:11296]
        sqga = pf[0:1, 11296:11312]
        gaexp = pf[0:1, 11312:11344]
        prodr = pf[0:1, 11344:11376]
        negrow = pf[0:1, 11376:11408]
        sq1r = pf[0:1, 11408:11440]

        S = Sched()
        war = {}

        def MM(key, out, lhsT, rhs, start, stop, waits=()):
            S.op("tensor", key,
                 lambda e, o=out, l=lhsT, r=rhs, a=start, b=stop:
                 e.matmul(o, l, r, start=a, stop=b, skip_group_check=True), waits)

        def ACT(eng_key, out, in_, func, bias=None, scale=None, accum_out=None, waits=()):
            kw = {}
            if bias is not None:
                kw["bias"] = bias
            if scale is not None:
                kw["scale"] = scale
            if accum_out is not None:
                kw["accum_out"] = accum_out
            S.op("scalar", eng_key,
                 lambda e, o=out, i=in_, f=func, k=kw: e.activation(out=o, in_=i, func=f, **k), waits)

        def CP(key, out, in_, waits=()):
            S.op("scalar", key, lambda e, o=out, i=in_: e.copy(o, i), waits)

        def VCP(key, out, in_, waits=()):
            S.op("vector", key, lambda e, o=out, i=in_: e.tensor_copy(o, i), waits)

        def TT(key, out, in0, in1, op, waits=()):
            S.op("vector", key, lambda e, o=out, a=in0, b=in1, p=op: e.tensor_tensor(out=o, in0=a, in1=b, op=p), waits)

        def TS(key, out, in0, s1, s2, op0, op1=None, waits=()):
            s1 = float(s1) if isinstance(s1, np.floating) else s1
            s2 = float(s2) if isinstance(s2, np.floating) else s2
            if op1 is None:
                S.op("vector", key, lambda e, o=out, i=in0, a=s1: e.tensor_scalar(out=o, in0=i, scalar1=a, scalar2=None, op0=ALU.mult), waits)
            else:
                S.op("vector", key, lambda e, o=out, i=in0, a=s1, b=s2, p0=op0, p1=op1: e.tensor_scalar(out=o, in0=i, scalar1=a, scalar2=b, op0=p0, op1=p1), waits)

        S.dma("sync", "c_ident", ident[:], ident_in)
        S.dma("sync", "c_ocol", ocol[:], ocol_in)
        S.dma("sync", "c_orf", orf[:], orf_in)
        S.dma("sync", "c_orb", orb[:], orb_in)
        S.dma("sync", "c_masks", masks[:].rearrange("p (t q) -> p t q", t=16),
              masks_in.rearrange("t p q -> p t q"))
        S.dma("sync", "c_bkq", bkq[:], bkq_in)
        S.dma("sync", "c_incol", incol[:], incol_in)
        S.dma("sync", "c_bvf", brow_f, brows_in[0:1, :])
        VCP("c_bv", bvrow, brow_f, waits=["c_bvf"])
        prev_cast = None
        for m in ("kT", "qT", "vT"):
            slot = {"kT": WK, "qT": WQ, "vT": WV}[m]
            for ic in range(NCH):
                S.dma("gpsimd", f"wl_{m}_{ic}", stg[:, 0:1024], wT[m][ic * 128:(ic + 1) * 128, :],
                      waits=[prev_cast])
                CP(f"wc_{m}_{ic}", slot[:, ic * 1024:(ic + 1) * 1024], stg[:, 0:1024],
                   waits=[f"wl_{m}_{ic}"])
                prev_cast = f"wc_{m}_{ic}"

        tp_ps = [psX, psY]
        prev_cc = None
        for t in range(NT):
            b, u = divmod(t, NU)
            xau = [xa, xa2][t % 2]
            S.dma("sync", f"x_{t}", xau[:].rearrange("p (s d) -> p s d", s=2),
                  x_own[t].rearrange("(s p) d -> p s d", s=2),
                  waits=[war.get(f"xa_sc_{t % 2}"), war.get(f"xa_ve_{t % 2}")])
            for st in range(2):
                base = (t % 2) * 16 + st * 8
                s_sum, s_sq = stats[:, base:base + 1], stats[:, base + 1:base + 2]
                s_m, s_2d = stats[:, base + 2:base + 3], stats[:, base + 3:base + 4]
                s_nv, s_rs = stats[:, base + 4:base + 5], stats[:, base + 5:base + 6]
                s_mn = stats[:, base + 6:base + 7]
                xs = xau[:, st * 1024:(st + 1) * 1024]
                S.op("vector", f"lns_{t}_{st}",
                     lambda e, o=s_sum, i=xs: e.reduce_sum(o, i, axis=AX.X),
                     waits=[f"x_{t}", war.get(f"st_{(t % 2)}_{st}")])
                ACT(f"lnq_{t}_{st}", sqd, xs, AF.Square, accum_out=s_sq,
                    waits=[f"x_{t}", war.get(f"st_{(t % 2)}_{st}")])
                TS(f"lnm_{t}_{st}", s_m, s_sum, f32(1.0 / D), None, ALU.mult,
                   waits=[f"lns_{t}_{st}"])
                TS(f"ln2_{t}_{st}", s_2d, s_sq, f32(1.0 / D), None, ALU.mult,
                   waits=[f"lnq_{t}_{st}"])
                S.op("vector", f"lnv_{t}_{st}",
                     lambda e, o=s_nv, a=s_m, c=s_2d:
                     e.scalar_tensor_tensor(out=o, in0=a, scalar=a, in1=c, op0=ALU.mult, op1=ALU.subtract),
                     waits=[f"lnm_{t}_{st}", f"ln2_{t}_{st}"])
                s_sd = stats[:, base + 7:base + 8]
                ACT(f"lsd_{t}_{st}", s_sd, s_nv, AF.Sqrt, bias=f32(1e-5), scale=f32(-1.0),
                    waits=[f"lnv_{t}_{st}"])
                S.op("vector", f"lnr_{t}_{st}",
                     lambda e, o=s_rs, i=s_sd: e.reciprocal(o, i),
                     waits=[f"lsd_{t}_{st}"])
                TS(f"lnn_{t}_{st}", s_mn, s_m, s_rs, None, ALU.mult,
                   waits=[f"lnr_{t}_{st}"])
                S.op("vector", f"xn_{t}_{st}",
                     lambda e, o=xn_tok[:, st * 1024:(st + 1) * 1024], i=xs, a=s_rs, b=s_mn:
                     e.tensor_scalar(out=o, in0=i, scalar1=a, scalar2=b, op0=ALU.mult, op1=ALU.subtract),
                     waits=[f"lnn_{t}_{st}", war.get("xntok")])
                war[f"st_{(t % 2)}_{st}"] = f"xn_{t}_{st}"
            war[f"xa_sc_{t % 2}"] = f"xn_{t}_{1}"
            war[f"xa_ve_{t % 2}"] = f"lns_{t}_{1}"
            for idx in range(16):
                st, ic = divmod(idx, NCH)
                ps = tp_ps[idx % 2]
                MM(f"tp_{t}_{idx}", ps[:, 0:128], xn_tok[:, st * 1024 + ic * 128: st * 1024 + ic * 128 + 128],
                   ident[:], True, True,
                   waits=[f"xn_{t}_{st}", "c_ident", war.get(f"ps_{idx % 2}")])
                CP(f"tpe_{t}_{idx}", xn0T[:, ic * 256 + st * 128: ic * 256 + st * 128 + 128],
                   ps[:, 0:128], waits=[f"tp_{t}_{idx}", war.get("xnTu"), war.get("xnTu_spill")])
                war[f"ps_{idx % 2}"] = f"tpe_{t}_{idx}"
            war["xntok_pe"] = f"tp_{t}_{15}"
            war["sqb_pe"] = f"tp_{t}_{15}"
            S.dma("sync", f"xnTs_{t}", xnT_scr[t].rearrange("c p q -> p c q"),
                  xn0T[:].rearrange("p (c q) -> p c q", c=NCH),
                  waits=[f"tpe_{t}_{15}"])
            war["xnTu_spill"] = f"xnTs_{t}"
            for pj, (W, bcol, wkey) in enumerate(((WK, 0, "kT"), (WQ, 1, "qT"))):
                pn = "k" if pj == 0 else "q"
                for oc in range(NCH):
                    pi = oc % 2
                    ps = tp_ps[pi]
                    for ic in range(NCH):
                        MM(f"{pn}mm_{t}_{oc}" if ic == NCH - 1 else None,
                           ps[:, 0:256], W[:, ic * 1024 + oc * 128: ic * 1024 + oc * 128 + 128],
                           xn0T[:, ic * 256:(ic + 1) * 256], ic == 0, ic == NCH - 1,
                           waits=([f"tpe_{t}_{15}", f"wc_{wkey}_{NCH-1}", war.get(f"ps_{pi}")] if ic == 0 else []))
                    ACT(f"{pn}ev_{t}_{oc}", keb[pi], ps[:, 0:256], AF.Identity,
                        bias=bkq[:, bcol * NCH + oc: bcol * NCH + oc + 1],
                        waits=[f"{pn}mm_{t}_{oc}", "c_bkq", war.get(f"keb_{pi}")])
                    war[f"ps_{pi}"] = f"{pn}ev_{t}_{oc}"
                    dst = kin[t][oc] if pj == 0 else q_scr[t][oc]
                    S.dma("sync", f"{pn}sp_{t}_{oc}", dst, keb[pi][:, 0:256],
                          waits=[f"{pn}ev_{t}_{oc}"])
                    ACT(f"{pn}sq_{t}_{oc}", sqb[:, oc * 256:(oc + 1) * 256], keb[pi], AF.Square,
                        waits=[f"{pn}ev_{t}_{oc}", (war.get("sqb_pe") if oc == 0 else None)])
                    war[f"keb_{pi}"] = f"{pn}sp_{t}_{oc}"
                    MM(f"{pn}nmm_{t}" if oc == NCH - 1 else None,
                       psC[0:1, 0:256], ocol[:], sqb[:, oc * 256:(oc + 1) * 256],
                       oc == 0, oc == NCH - 1,
                       waits=[f"{pn}sq_{t}_{oc}"] + (["c_ocol", war.get("psC")] if oc == 0 else []))
                scol = (32 + t) if pj == 1 else (48 + t)
                S.op("vector", f"{pn}nmx_{t}",
                     lambda e, o=stats[0:1, scol:scol + 1], i=psC[0:1, 0:256]:
                     e.reduce_max(o, i, axis=AX.X),
                     waits=[f"{pn}nmm_{t}"])
                war["psC"] = f"{pn}nmx_{t}"
                war["sqb_pe"] = f"{pn}nmm_{t}"
            war["xntok"] = f"qnmm_{t}"
            for st in range(2):
                for oh in range(2):
                    pi = (st * 2 + oh) % 2
                    ps = tp_ps[pi]
                    MM(None, ps[:, 0:512], orb[:], bvrow[0:1, oh * 512:(oh + 1) * 512],
                       True, False, waits=["c_bv", "c_orb", war.get(f"ps_{pi}")])
                    for ic in range(NCH):
                        MM(f"vmm_{t}_{st}_{oh}" if ic == NCH - 1 else None,
                           ps[:, 0:512], xn0T[:, ic * 256 + st * 128: ic * 256 + st * 128 + 128],
                           WV[:, ic * 1024 + oh * 512: ic * 1024 + oh * 512 + 512],
                           False, ic == NCH - 1,
                           waits=([f"wc_vT_{NCH-1}"] if ic == 0 else []))
                    CP(f"vev_{t}_{st}_{oh}", veb[pi][:, 0:512], ps[:, 0:512],
                       waits=[f"vmm_{t}_{st}_{oh}", war.get(f"veb_{pi}")])
                    war[f"ps_{pi}"] = f"vev_{t}_{st}_{oh}"
                    S.dma("sync", f"vsp_{t}_{st}_{oh}", vin[t][st][:, oh * 512:(oh + 1) * 512],
                          veb[pi][:, 0:512], waits=[f"vev_{t}_{st}_{oh}"])
                    war[f"veb_{pi}"] = f"vsp_{t}_{st}_{oh}"
            war["xnTu"] = f"vmm_{t}_{1}_{1}"
            S.cc(f"ccK_{t}", lambda e, a=kin[t], o=kout[t]: e.collective_compute(
                "AllGather", ALU.bypass, replica_groups=RG, ins=[a.opt()], outs=[o.opt()]),
                waits=[f"ksp_{t}_{NCH-1}", prev_cc])
            S.cc(f"ccV_{t}", lambda e, a=vin[t], o=vout[t]: e.collective_compute(
                "AllGather", ALU.bypass, replica_groups=RG, ins=[a.opt()], outs=[o.opt()]),
                waits=[f"vsp_{t}_{1}_{1}", f"ccK_{t}"])
            prev_cc = f"ccV_{t}"

        a_last = NT - 1
        S.op("vector", "mx0", lambda e: e.memset(mxsb, 0.0))
        for b in range(2):
            S.op("vector", f"mxb_{b}",
                 lambda e, o=mxsb[0:1, b:b + 1], i=stats[0:1, 48 + b * NU: 48 + (b + 1) * NU]:
                 e.reduce_max(o, i, axis=AX.X),
                 waits=["mx0", f"knmx_{b * NU + NU - 1}"])
        S.dma("sync", "mxsp", mxin, mxsb, waits=["mxb_0", "mxb_1"])
        S.cc("ccAR", lambda e: e.collective_compute(
            "AllReduce", ALU.max, replica_groups=RG, ins=[mxin.opt()], outs=[mxout.opt()]),
            waits=["mxsp", prev_cc])
        S.dma("sync", "gald", garow, mxout, waits=["ccAR"])

        FORDER = ("fT", "iT", "cT", "ufT", "uiT", "ucT", "oT", "uoT")
        a_pe_done = {"ufT": f"kmm_{a_last}_{NCH-1}", "uiT": f"qmm_{a_last}_{NCH-1}",
                     "ucT": f"vmm_{a_last}_{1}_{1}"}
        prev_fcast = {0: "wc_vT_%d" % (NCH - 1), 1: None}
        nchunk = 0
        last_odd_cast = None
        for m in FORDER:
            for ic in range(NCH):
                par = (nchunk % 2) if nchunk < 40 else 0
                S.dma("sync", f"fwl_{m}_{ic}", stgs[par][:, 0:1024], wT[m][ic * 128:(ic + 1) * 128, :],
                      waits=[prev_fcast[par]])
                VCP(f"fwc_{m}_{ic}", FW[m][:, ic * 1024:(ic + 1) * 1024], stgs[par][:, 0:1024],
                    waits=[f"fwl_{m}_{ic}"] + ([a_pe_done[m]] if (m in a_pe_done and ic == 0) else []))
                prev_fcast[par] = f"fwc_{m}_{ic}"
                if par == 1:
                    last_odd_cast = f"fwc_{m}_{ic}"
                nchunk += 1
        CP("incb", incb[:], incol[:], waits=["c_incol"])
        S.dma("sync", "incld", increg, incrow_in)
        for mi, m in enumerate(("f", "i", "c")):
            um = ("ufT", "uiT", "ucT")[mi]
            S.dma("sync", f"brl_{m}", brow_f, brows_in[1 + mi:2 + mi, :],
                  waits=["c_bv" if mi == 0 else f"bp_{('f','i','c')[mi-1]}_{1}"])
            for oh in range(2):
                for ic in range(NCH):
                    MM(f"zmm_{m}_{oh}" if ic == NCH - 1 else None,
                       psX[0:1, 0:512], incb[:, ic:ic + 1],
                       FW[um][:, ic * 1024 + oh * 512: ic * 1024 + oh * 512 + 512],
                       ic == 0, ic == NCH - 1,
                       waits=(["incb", f"fwc_{um}_{NCH-1}", war.get("ps_0"),
                               (f"bp_{m}_{0}" if oh == 1 else None)] if ic == 0 else []))
                TT(f"bp_{m}_{oh}", bprow[m][0:1, oh * 512:(oh + 1) * 512],
                   psX[0:1, 0:512], brow_f[0:1, oh * 512:(oh + 1) * 512], ALU.add,
                   waits=[f"zmm_{m}_{oh}", f"brl_{m}",
                          (war.get("bvrow_pe") if m == "f" and oh == 0 else None)])
            war["ps_0"] = f"bp_{m}_{1}"
        S.dma("sync", "brl_o", brow_f, brows_in[4:5, :], waits=[f"bp_c_{1}"])
        VCP("c_bo", borow, brow_f, waits=["brl_o"])
        for oh in range(2):
            MM(f"c0mm_{oh}", psY[:, 0:512], orf[:], increg[0:1, oh * 512:(oh + 1) * 512],
               True, True, waits=["incld", "c_orf", war.get("ps_1"), (f"c0b_{0}" if oh else None)])
            CP(f"c0b_{oh}", c0b[:, oh * 512:(oh + 1) * 512], psY[:, 0:512], waits=[f"c0mm_{oh}"])
            war["ps_1"] = f"c0b_{oh}"
        war["increg"] = f"c0mm_{1}"

        GPS = [psA, psB, psC]
        a_sc_done = f"qsq_{a_last}_{NCH-1}"
        a_ve_done = f"qnmx_{a_last}"
        war["h0f"] = a_sc_done
        war["c1T_pe"] = f"qsp_{a_last}_{NCH-1}"
        war["c2T_pe"] = f"vsp_{a_last}_{1}_{1}"
        war["bvrow_pe"] = f"vmm_{a_last}_{1}_{1}"
        war["xn1T_war"] = f"qnmm_{a_last}"
        for u in range(NU):
            S.dma("sync", f"fx0_{u}", xn0T[:].rearrange("p (c q) -> p c q", c=NCH),
                  xnT_scr[u].rearrange("c p q -> p c q"),
                  waits=[war["xnTu"], f"xnTs_{u}"])
            S.dma("sync", f"fx1_{u}", xn1T[:].rearrange("p (c q) -> p c q", c=NCH),
                  xnT_scr[NU + u].rearrange("c p q -> p c q"),
                  waits=[war["xn1T_war"], a_sc_done, f"xnTs_{NU + u}"])
            for st in range(2):
                sb = f"{u}_{st}"
                for ic in range(NCH):
                    lh = xn0T[:, ic * 256 + st * 128: ic * 256 + st * 128 + 128]
                    for g in range(3):
                        wm = FW[("fT", "iT", "cT")[g]]
                        for oh in range(2):
                            MM(None, GPS[g][:, oh * 512:(oh + 1) * 512], lh,
                               wm[:, ic * 1024 + oh * 512: ic * 1024 + oh * 512 + 512],
                               ic == 0, False,
                               waits=([f"fx0_{u}", f"fwc_{('fT','iT','cT')[g]}_{NCH-1}",
                                       war.get(f"gps_{g}"),
                                       (a_ve_done if (u == 0 and st == 0 and g == 2) else None)]
                                      if (ic == 0) else []))
                for g in range(3):
                    for oh in range(2):
                        MM(f"g1m_{sb}_{g}" if oh == 1 else None,
                           GPS[g][:, oh * 512:(oh + 1) * 512], orb[:],
                           bprow["fic"[g]][0:1, oh * 512:(oh + 1) * 512], False, True,
                           waits=[f"bp_{'fic'[g]}_{oh}"])
                for g in range(3):
                    ACT(f"g1e_{sb}_{g}", gate[g], GPS[g][:],
                        AF.Sigmoid if g < 2 else AF.Tanh,
                        waits=[f"g1m_{sb}_{g}", war.get(f"gate_{g}"),
                               (a_ve_done if (u == 0 and st == 0) else None)])
                    war[f"gps_{g}"] = f"g1e_{sb}_{g}"
                TT(f"ct1_{sb}", c1f, gate[0], c0b, ALU.mult,
                   waits=[f"g1e_{sb}_{0}", f"c0b_{1}", war.get("c1f"), (a_sc_done if (u == 0 and st == 0) else None)])
                TT(f"ct2_{sb}", h0f, gate[1], gate[2], ALU.mult,
                   waits=[f"g1e_{sb}_{1}", f"g1e_{sb}_{2}", war.get("h0f")])
                TT(f"c1_{sb}", c1f, c1f, h0f, ALU.add, waits=[f"ct1_{sb}", f"ct2_{sb}"])
                CP(f"c1b_{sb}", c1bf[:], c1f, waits=[f"c1_{sb}", war.get("c1bf")])
                for ic in range(NCH):
                    ps = tp_ps[ic % 2]
                    MM(f"c1tp_{sb}_{ic}", ps[:, 0:128], c1bf[:, ic * 128:(ic + 1) * 128],
                       ident[:], True, True, waits=[f"c1b_{sb}", war.get(f"ps_{ic % 2}")])
                    CP(f"c1T_{sb}_{ic}", c1T[:, ic * 128:(ic + 1) * 128], ps[:, 0:128],
                       waits=[f"c1tp_{sb}_{ic}", war.get("c1T_pe")])
                    war[f"ps_{ic % 2}"] = f"c1T_{sb}_{ic}"
                for ic in range(NCH):
                    lh = xn1T[:, ic * 256 + st * 128: ic * 256 + st * 128 + 128]
                    for g in range(3):
                        wm = FW[("fT", "iT", "cT")[g]]
                        for oh in range(2):
                            MM(None, GPS[g][:, oh * 512:(oh + 1) * 512], lh,
                               wm[:, ic * 1024 + oh * 512: ic * 1024 + oh * 512 + 512],
                               ic == 0, False,
                               waits=([f"fx1_{u}", war.get(f"gps_{g}")] if ic == 0 else []))
                for mc in range(NCH):
                    lh = c1T[:, mc * 128:(mc + 1) * 128]
                    for g in range(3):
                        um = FW[("ufT", "uiT", "ucT")[g]]
                        for oh in range(2):
                            MM(None, GPS[g][:, oh * 512:(oh + 1) * 512], lh,
                               um[:, mc * 1024 + oh * 512: mc * 1024 + oh * 512 + 512],
                               False, False,
                               waits=([f"c1T_{sb}_{NCH-1}", f"fwc_ucT_{NCH-1}"]
                                      if (mc == 0 and g == 0 and oh == 0) else []))
                for g in range(3):
                    for oh in range(2):
                        MM(f"g2m_{sb}_{g}" if oh == 1 else None,
                           GPS[g][:, oh * 512:(oh + 1) * 512], orb[:],
                           bprow["fic"[g]][0:1, oh * 512:(oh + 1) * 512], False, True,
                           waits=[])
                for g in range(3):
                    ACT(f"g2e_{sb}_{g}", gate[g], GPS[g][:],
                        AF.Sigmoid if g < 2 else AF.Tanh,
                        waits=[f"g2m_{sb}_{g}", f"c1_{sb}" if g == 0 else None,
                               f"ct2_{sb}" if g > 0 else None])
                    war[f"gps_{g}"] = f"g2e_{sb}_{g}"
                TT(f"dt1_{sb}", c2f, gate[0], c1f, ALU.mult,
                   waits=[f"g2e_{sb}_{0}", war.get("c2f"),
                          (last_odd_cast if (u == 0 and st == 0) else None)])
                TT(f"dt2_{sb}", h1f, gate[1], gate[2], ALU.mult,
                   waits=[f"g2e_{sb}_{1}", f"g2e_{sb}_{2}", war.get("h1f")])
                TT(f"c2_{sb}", c2f, c2f, h1f, ALU.add, waits=[f"dt1_{sb}", f"dt2_{sb}"])
                war["c1f"] = f"dt1_{sb}"
                CP(f"c2b_{sb}", c1bf[:], c2f, waits=[f"c2_{sb}", f"c1tp_{sb}_{NCH-1}"])
                war["c1bf"] = f"c2b_{sb}"
                for ic in range(NCH):
                    ps = tp_ps[ic % 2]
                    MM(f"c2tp_{sb}_{ic}", ps[:, 0:128], c1bf[:, ic * 128:(ic + 1) * 128],
                       ident[:], True, True, waits=[f"c2b_{sb}", war.get(f"ps_{ic % 2}")])
                    CP(f"c2T_{sb}_{ic}", c2T[:, ic * 128:(ic + 1) * 128], ps[:, 0:128],
                       waits=[f"c2tp_{sb}_{ic}", war.get("c2T_pe")])
                    war[f"ps_{ic % 2}"] = f"c2T_{sb}_{ic}"
                war["c1bf"] = f"c2tp_{sb}_{NCH-1}"
                war["c1T_pe"] = f"g2m_{sb}_{2}"
                for hb in range(2):
                    ps = GPS[hb]
                    xnsrc = xn0T if hb == 0 else xn1T
                    hslot = h0f if hb == 0 else h1f
                    for oh in range(2):
                        for ic in range(NCH):
                            MM(None, ps[:, oh * 512:(oh + 1) * 512],
                               xnsrc[:, ic * 256 + st * 128: ic * 256 + st * 128 + 128],
                               FW["oT"][:, ic * 1024 + oh * 512: ic * 1024 + oh * 512 + 512],
                               ic == 0, False,
                               waits=([f"fwc_oT_{NCH-1}", war.get(f"gps_{hb}")]
                                      if (oh == 0 and ic == 0) else []))
                        for mc in range(NCH):
                            MM(None,
                               ps[:, oh * 512:(oh + 1) * 512], c2T[:, mc * 128:(mc + 1) * 128],
                               FW["uoT"][:, mc * 1024 + oh * 512: mc * 1024 + oh * 512 + 512],
                               False, False,
                               waits=([f"c2T_{sb}_{NCH-1}", f"fwc_uoT_{NCH-1}"] if (oh == 0 and mc == 0) else []))
                        MM(f"hm_{sb}_{hb}" if oh == 1 else None,
                           ps[:, oh * 512:(oh + 1) * 512], orb[:],
                           borow[0:1, oh * 512:(oh + 1) * 512], False, True,
                           waits=["c_bo"])
                    ACT(f"he_{sb}_{hb}", hslot, ps[:], AF.Tanh,
                        waits=[f"hm_{sb}_{hb}", (f"c2_{sb}" if hb == 1 else f"c1_{sb}"),
                               war.get("h0sp" if hb == 0 else "h1sp")])
                    war[f"gps_{hb}"] = f"he_{sb}_{hb}"
                    S.dma("sync", f"hsp_{sb}_{hb}", h_scr[hb * NU + u][st * 128:(st + 1) * 128, :],
                          hslot, waits=[f"he_{sb}_{hb}"])
                    war["h0sp" if hb == 0 else "h1sp"] = f"hsp_{sb}_{hb}"
                war["h0f"] = f"hsp_{sb}_{0}"
                war["h1f"] = f"hsp_{sb}_{1}"
                war["c2f"] = f"hm_{sb}_{1}"
                war["c2T_pe"] = f"hm_{sb}_{1}"
                war["gate_0"] = f"dt1_{sb}"
                war["gate_1"] = f"dt2_{sb}"
                war["gate_2"] = f"dt2_{sb}"
            war["xnTu"] = f"hm_{u}_{1}_{0}"
            war["xn1T_war"] = f"hm_{u}_{1}_{1}"
        ffn_pe_done = f"hm_{NU-1}_{1}_{1}"
        ffn_sc_done = f"he_{NU-1}_{1}_{1}"
        ffn_ve_done = f"c2_{NU-1}_{1}"

        ACT("sq1", sq1r[0:1, 0:NT], stats[0:1, 32:32 + NT], AF.Sqrt,
            waits=[f"qnmx_{a_last}"])
        ACT("sqga", sqga[0:1, 0:2], garow[0:1, 0:2], AF.Sqrt, waits=["gald"])
        for b in range(2):
            S.op("vector", f"gax_{b}",
                 lambda e, o=gaexp[0:1, b * NU:(b + 1) * NU], i=orf[0:1, 0:NU], s=sqga[0:1, b:b + 1]:
                 e.tensor_scalar(out=o, in0=i, scalar1=s, scalar2=None, op0=ALU.mult),
                 waits=["sqga", "c_orf"])
        TT("prod", prodr[0:1, 0:NT], sq1r[0:1, 0:NT], gaexp[0:1, 0:NT], ALU.mult,
           waits=["gax_0", "gax_1", "sq1"])
        TS("negrow", negrow[0:1, 0:NT], prodr[0:1, 0:NT], f32(-1.0 / 32.0), None, ALU.mult,
           waits=["prod"])
        for t in range(NT):
            MM(f"ncb_{t}", psY[:, 0:1], orf[:], negrow[0:1, t:t + 1], True, True,
               waits=["negrow", war.get("ps_1"), ffn_pe_done if t == 0 else None])
            CP(f"ncc_{t}", negc[:, t:t + 1], psY[:, 0:1], waits=[f"ncb_{t}"])
            war["ps_1"] = f"ncc_{t}"

        war["stp_0"] = war.get("ps_0")
        war["stp_1"] = war.get("ps_0")
        war["pvacc"] = ffn_sc_done
        war["psC"] = war.get("gps_2")
        for _k in range(4):
            war[f"kt_{_k}"] = ffn_pe_done
        for _k in range(4):
            war[f"vt_{_k}"] = ffn_pe_done
        war["pb_0"] = ffn_pe_done
        war["pb_1"] = ffn_pe_done
        pbkey = {}
        for t in range(NT):
            b, u = divmod(t, NU)
            nk = (u + 1) * 16
            S.dma("sync", f"aq_{t}", qbuf[:].rearrange("p (c q) -> p c q", c=NCH),
                  q_scr[t].rearrange("c p q -> p c q"),
                  waits=[war.get("qbuf") or ffn_pe_done])
            S.op("vector", f"az_{t}", lambda e: e.memset(accv, 0.0),
                 waits=[war.get("accv") or ffn_ve_done,
                        (ffn_sc_done if t == 0 else None)])
            def emit_lpv(kc, t=t, u=u, nk=nk):
                sh = kc % 2
                kci = (kc // 2) % 4
                pk = pbkey[(t, kc)]
                MM(f"lm_{t}" if kc == nk - 1 else None, psC[0:1, 0:256], ocol[:], pb[sh],
                   kc == 0, kc == nk - 1,
                   waits=[pk] + ([war.get("psC"), "c_ocol"] if kc == 0 else []))
                for qs in range(2):
                    for oh in range(2):
                        MM(f"pv_{t}_{kc}" if (qs == 1 and oh == 1) else None,
                           GPS[qs][:, oh * 512:(oh + 1) * 512],
                           pb[sh][:, qs * 128:(qs + 1) * 128],
                           vtile[kci][:, (kc % 2) * 1024 + oh * 512:(kc % 2) * 1024 + oh * 512 + 512],
                           kc % 4 == 0, kc % 4 == 3,
                           waits=([f"avt_{t}_{kc // 2}"] if (qs == 0 and oh == 0) else []) +
                                 ([war.get("pvacc")] if (qs == 0 and oh == 0 and kc % 4 == 0) else []))
                war[f"pb_{sh}"] = f"pv_{t}_{kc}"
                war[f"vt_{kci}"] = f"pv_{t}_{kc}"
                if kc % 4 == 3:
                    blk = kc // 4
                    for seg in range(4):
                        TT(f"aa_{t}_{blk}" if seg == 3 else None,
                           accv[:, seg * 512:(seg + 1) * 512], accv[:, seg * 512:(seg + 1) * 512],
                           GPS[seg // 2][:, (seg % 2) * 512:(seg % 2 + 1) * 512], ALU.add,
                           waits=([f"pv_{t}_{kc}", f"az_{t}"] if seg == 0 else []))
                    war["pvacc"] = f"aa_{t}_{blk}"

            for kc in range(nk):
                c256, hh = divmod(kc, 2)
                uu, r = divmod(c256, NCORE)
                kci = kc % 8
                if hh == 0:
                    ks = c256 % 4
                    S.dma("sync", f"akt_{t}_{c256}",
                          ktile[ks][:].rearrange("p (c q) -> p c q", c=NCH),
                          kout[b * NU + uu][r * NCH:(r + 1) * NCH].rearrange("c p q -> p c q"),
                          waits=[f"ccK_{b * NU + uu}", war.get(f"kt_{ks}")])
                    S.dma("gpsimd", f"avt_{t}_{c256}",
                          vtile[ks][:].rearrange("p (s d) -> p s d", s=2),
                          vout[b * NU + uu][r * 2:(r + 1) * 2].rearrange("s p d -> p s d"),
                          waits=[f"ccV_{b * NU + uu}", war.get(f"vt_{ks}")])
                sh = kc % 2
                stp = psX[:, sh * 256:(sh + 1) * 256]
                ks = c256 % 4
                for oc in range(NCH):
                    MM(f"qk_{t}_{kc}" if oc == NCH - 1 else None, stp,
                       ktile[ks][:, oc * 256 + hh * 128: oc * 256 + hh * 128 + 128],
                       qbuf[:, oc * 256:(oc + 1) * 256], oc == 0, oc == NCH - 1,
                       waits=([f"akt_{t}_{c256}", f"aq_{t}", war.get(f"stp_{sh}")] if oc == 0 else []))
                ACT(f"ex_{t}_{kc}", pb[sh], stp, AF.Exp, bias=negc[:, t:t + 1],
                    scale=f32(1.0 / 32.0),
                    waits=[f"qk_{t}_{kc}", f"ncc_{t}", war.get(f"pb_{sh}")])
                war[f"stp_{sh}"] = f"ex_{t}_{kc}"
                pkey = f"ex_{t}_{kc}"
                if kc >= u * 16:
                    mi = kc - u * 16
                    TT(f"mk_{t}_{kc}", pb[sh], pb[sh], masks[:, mi * 256:(mi + 1) * 256],
                       ALU.mult, waits=[pkey, "c_masks"])
                    pkey = f"mk_{t}_{kc}"
                pbkey[(t, kc)] = pkey
                war[f"kt_{ks}"] = f"qk_{t}_{kc}"
                emit_lpv(kc)
            war["qbuf"] = f"qk_{t}_{nk-1}"
            VCP(f"lr_{t}", lrow, psC[0:1, 0:256],
                waits=[f"lm_{t}", war.get("lrow"), (ffn_sc_done if t == 0 else None)])
            war["psC"] = f"lr_{t}"
            for qs in range(2):
                MM(f"ltp_{t}_{qs}", psY[:, 0:1], lrow[0:1, qs * 128:(qs + 1) * 128],
                   orf[0:1, 0:1], True, True, waits=[f"lr_{t}", war.get("ps_1")])
                war["lrow"] = f"ltp_{t}_{qs}"
                S.op("vector", f"li_{t}_{qs}",
                     lambda e, o=linv[:, qs:qs + 1], i=psY[:, 0:1]: e.reciprocal(o, i),
                     waits=[f"ltp_{t}_{qs}"])
                war["ps_1"] = f"li_{t}_{qs}"
                TS(f"an_{t}_{qs}", accv[:, qs * 1024:(qs + 1) * 1024],
                   accv[:, qs * 1024:(qs + 1) * 1024], linv[:, qs:qs + 1], None, ALU.mult,
                   waits=[f"li_{t}_{qs}", f"aa_{t}_{nk // 4 - 1}"])
            if DEBUG:
                S.dma("sync", f"dacc_{t}", d_acc[t].rearrange("(s p) d -> p s d", s=2),
                      accv[:].rearrange("p (s d) -> p s d", s=2), waits=[f"an_{t}_{1}"])
                S.dma("sync", f"dl_{t}", d_l[t], lrow, waits=[f"lr_{t}"])
                S.dma("sync", f"dh_{t}", d_h[t], h_scr[t], waits=[f"hsp_{u}_{1}_{b}"])
                S.dma("sync", f"dxnT_{t}", d_xnT[t], xnT_scr[t], waits=[f"xnTs_{t}"])
                S.dma("sync", f"dko_{t}", d_kout[t], kout[t], waits=[f"ccK_{t}"])
                S.dma("sync", f"dvo_{t}", d_vout[t], vout[t], waits=[f"ccV_{t}"])
            S.dma("gpsimd", f"ax_{t}", xrt[:].rearrange("p (s d) -> p s d", s=2),
                  x_own[t].rearrange("(s p) d -> p s d", s=2),
                  waits=[war.get("xrt") or ffn_ve_done, war.get("xrt_pe") or ffn_pe_done])
            S.dma("gpsimd", f"ah_{t}", hrt[:].rearrange("p (s d) -> p s d", s=2),
                  h_scr[t].rearrange("(s p) d -> p s d", s=2),
                  waits=[f"hsp_{u}_{1}_{b}", war.get("hrt") or ffn_ve_done,
                         (ffn_pe_done if t == 0 else None)])
            TT(f"ao1_{t}", accv, accv, xrt, ALU.add,
               waits=[f"an_{t}_{1}", f"ax_{t}"] + ([f"dacc_{t}"] if DEBUG else []))
            TT(f"ao2_{t}", accv, accv, hrt, ALU.add, waits=[f"ao1_{t}", f"ah_{t}"])
            war["xrt"] = f"ao1_{t}"
            war["hrt"] = f"ao2_{t}"
            war["xrt_pe"] = f"ao1_{t}"
            S.dma("gpsimd", f"out_{t}", y_own[t].rearrange("(s p) d -> p s d", s=2),
                  accv[:].rearrange("p (s d) -> p s d", s=2), waits=[f"ao2_{t}"])
            war["accv"] = f"out_{t}"
        if DEBUG:
            S.dma("sync", "dnegc", d_negc, negc[:], waits=[f"ncc_{NT - 1}"])
            S.dma("sync", "dstats", d_stats, stats[:], waits=[f"ncc_{NT - 1}"])
            S.dma("sync", "drows", d_rows, pf[0:1, 11264:11440], waits=[f"ncc_{NT - 1}"])
        S.op("sync", None, lambda e: e.nop(), waits=[f"out_{NT - 1}"] + (["dnegc"] if DEBUG else []))

        S.finalize(nc, stack, serial=SERIAL)
    return nc


def kernel(L0_attn_x, L0_attn_Wk, L0_attn_bk, L0_attn_Wq, L0_attn_bq, L0_attn_Wv,
           L0_attn_bv, L0_ffn_wf, L0_ffn_uf, L0_ffn_bf, L0_ffn_wi, L0_ffn_ui,
           L0_ffn_bi, L0_ffn_wc, L0_ffn_uc, L0_ffn_bc, L0_ffn_wo, L0_ffn_uo,
           L0_ffn_bo, L0_ffn_in_c):
    x = np.asarray(L0_attn_x, dtype=np.float32)
    B, Sq, Dm = x.shape
    assert B == 2 and Dm == D and Sq % (NCORE * CH) == 0
    NU = Sq // (NCORE * CH)

    def wt(a):
        return np.ascontiguousarray(np.asarray(a, dtype=np.float32).T)

    weights = {"w_kT": wt(L0_attn_Wk), "w_qT": wt(L0_attn_Wq), "w_vT": wt(L0_attn_Wv),
               "w_fT": wt(L0_ffn_wf), "w_iT": wt(L0_ffn_wi), "w_cT": wt(L0_ffn_wc),
               "w_ufT": wt(L0_ffn_uf), "w_uiT": wt(L0_ffn_ui), "w_ucT": wt(L0_ffn_uc),
               "w_oT": wt(L0_ffn_wo), "w_uoT": wt(L0_ffn_uo)}
    bk = np.asarray(L0_attn_bk, np.float32).reshape(NCH, 128).T
    bq = np.asarray(L0_attn_bq, np.float32).reshape(NCH, 128).T
    bkq = np.ascontiguousarray(np.concatenate([bk, bq], axis=1))
    brows = np.ascontiguousarray(np.stack([
        np.asarray(L0_attn_bv, np.float32), np.asarray(L0_ffn_bf, np.float32),
        np.asarray(L0_ffn_bi, np.float32), np.asarray(L0_ffn_bc, np.float32),
        np.asarray(L0_ffn_bo, np.float32)]))
    incrow = np.ascontiguousarray(np.asarray(L0_ffn_in_c, np.float32).reshape(1, D))
    incol = np.ascontiguousarray(np.asarray(L0_ffn_in_c, np.float32).reshape(NCH, 128).T)
    ident = np.eye(128, dtype=BFDT)
    ocolv = np.ones((128, 1), dtype=BFDT)
    orow_f = np.ones((1, 128), dtype=np.float32)
    orow_b = np.ones((1, 128), dtype=BFDT)

    xr = x.reshape(B, Sq // CH, CH, D)
    kk, qq = np.meshgrid(np.arange(128), np.arange(CH), indexing="ij")
    in_maps = []
    for j in range(NCORE):
        idx = np.arange(NU) * NCORE + j
        x_ownj = np.ascontiguousarray(xr[:, idx].reshape(2 * NU, CH, D))
        mt = np.zeros((16, 128, CH), dtype=BFDT)
        for mi in range(16):
            m, hh2 = divmod(mi, 2)
            if m < j:
                mt[mi] = 1
            elif m == j:
                mt[mi] = ((hh2 * 128 + kk) <= qq).astype(BFDT)
        im = {"x_own": x_ownj, "bkq": bkq, "brows": brows, "incrow": incrow,
              "incol": incol, "masks": mt, "ident": ident, "ocol": ocolv,
              "orow_f": orow_f, "orow_b": orow_b}
        im.update(weights)
        in_maps.append(im)

    nc = build(NU)
    res = run_bass_kernel_spmd(nc, in_maps, core_ids=list(range(NCORE)))
    out = np.empty((B, Sq, D), dtype=np.float32)
    for j in range(NCORE):
        yj = np.asarray(res.results[j]["y_own"])
        for b in range(B):
            for i in range(NU):
                c = i * NCORE + j
                out[b, c * CH:(c + 1) * CH] = yj[b * NU + i]
    return out


# version 08 (final)
# speedup vs baseline: 1.0456x; 1.0056x over previous
import contextlib
import numpy as np
import ml_dtypes
import concourse.bass as bass
import concourse.mybir as mybir
from concourse.bass_utils import run_bass_kernel_spmd

F32 = mybir.dt.float32
BF16 = mybir.dt.bfloat16
AF = mybir.ActivationFunctionType
ALU = mybir.AluOpType
AX = mybir.AxisListType
BFDT = ml_dtypes.bfloat16

DEBUG = False
SERIAL = False
NCORE = 8
D = 1024
NCH = D // 128
CH = 256
RG = [list(range(NCORE))]
f32 = np.float32


class Sched:
    ENGS = ("sync", "gpsimd", "vector", "scalar", "tensor")
    ESEM = {"sync": "sy", "gpsimd": "gp", "vector": "ve", "scalar": "sc", "tensor": "pe"}
    DSEM = {"sync": "dsy", "gpsimd": "dgp", "vector": "dve", "scalar": "dsc"}

    def __init__(self):
        self.ops = []
        self.keys = {}

    def op(self, eng, key, fn, waits=()):
        self.ops.append((eng, fn, tuple(w for w in waits if w), key, 1, self.ESEM[eng]))

    def dma(self, q, key, out, in_, waits=()):
        self.ops.append((q, (lambda e, o=out, i=in_: e.dma_start(out=o, in_=i)),
                         tuple(w for w in waits if w), key, 16, self.DSEM[q]))

    def cc(self, key, fn, waits=()):
        self.ops.append(("gpsimd", fn, tuple(w for w in waits if w), key, 1, "cc"))

    def finalize(self, nc, stack, serial=False):
        if serial:
            ops2 = []
            prev = None
            for n, (eng, fn, waits, key, inc, sem) in enumerate(self.ops):
                if key is None:
                    key = f"__auto{n}"
                waits = tuple(waits) + ((prev,) if prev else ())
                ops2.append((eng, fn, waits, key, inc, sem))
                prev = key
            self.ops = ops2
        sems = {s: stack.enter_context(nc.semaphore("s_" + s))
                for s in set(list(self.ESEM.values()) + list(self.DSEM.values()) + ["cc"])}
        ctr = {s: 0 for s in sems}
        chained = []
        for eng, fn, waits, key, inc, sem in self.ops:
            cw = None
            if key is not None:
                if sem not in self.ESEM.values() and ctr[sem] > 0:
                    cw = (sem, ctr[sem])
                ctr[sem] += inc
                assert key not in self.keys, f"dup key {key}"
                self.keys[key] = (sem, ctr[sem])
            chained.append(cw)
        self.chainw = chained
        for eng2, fn2, waits2, key2, inc2, sem2 in self.ops:
            for w in waits2:
                assert w in self.keys, f"wait on undefined key {w}"
        block = stack.enter_context(nc.Block())

        def mk_stream(eng):
            myops = [o for o in self.ops if o[0] == eng]
            mysem = self.ESEM[eng]

            mychain = [self.chainw[n] for n, o in enumerate(self.ops) if o[0] == eng]

            def stream(e):
                hw = {s: 0 for s in sems}
                for (_, fn, waits, key, inc, sem), cw in zip(myops, mychain):
                    for w in waits:
                        ws, wv = self.keys[w]
                        if wv <= hw[ws]:
                            continue
                        e.wait_ge(sems[ws], wv)
                        hw[ws] = wv
                    if cw is not None and cw[1] > hw[cw[0]]:
                        e.wait_ge(sems[cw[0]], cw[1])
                        hw[cw[0]] = cw[1]
                    instr = fn(e)
                    if key is not None:
                        instr.then_inc(sems[sem], inc)
            return stream

        for eng in self.ENGS:
            getattr(block, eng)(mk_stream(eng))


def build(NU):
    nc = bass.Bass("TRN2", target_bir_lowering=False)
    NT = 2 * NU

    def din(name, shape, dt):
        return nc.dram_tensor(name, shape, dt, kind="ExternalInput").ap()

    x_own = din("x_own", [NT, CH, D], F32)
    WNAMES = ("kT", "qT", "vT", "fT", "iT", "cT", "ufT", "uiT", "ucT", "oT", "uoT")
    wT = {m: din("w_" + m, [D, D], F32) for m in WNAMES}
    bkq_in = din("bkq", [128, 2 * NCH], F32)
    brows_in = din("brows", [5, D], F32)
    incrow_in = din("incrow", [1, D], F32)
    incol_in = din("incol", [128, NCH], F32)
    masks_in = din("masks", [16, 128, CH], BF16)
    ident_in = din("ident", [128, 128], BF16)
    ocol_in = din("ocol", [128, 1], BF16)
    orf_in = din("orow_f", [1, 128], F32)
    orb_in = din("orow_b", [1, 128], BF16)

    y_own = nc.dram_tensor("y_own", [NT, CH, D], F32, kind="ExternalOutput").ap()
    if DEBUG:
        d_h = nc.dram_tensor("d_h", [NT, CH, D], F32, kind="ExternalOutput").ap()
        d_acc = nc.dram_tensor("d_acc", [NT, CH, D], F32, kind="ExternalOutput").ap()
        d_xnT = nc.dram_tensor("d_xnT", [NT, NCH, 128, CH], BF16, kind="ExternalOutput").ap()
        d_l = nc.dram_tensor("d_l", [NT, 1, CH], F32, kind="ExternalOutput").ap()
        d_negc = nc.dram_tensor("d_negc", [128, 32], F32, kind="ExternalOutput").ap()
        d_stats = nc.dram_tensor("d_stats", [128, 64], F32, kind="ExternalOutput").ap()
        d_rows = nc.dram_tensor("d_rows", [1, 176], F32, kind="ExternalOutput").ap()
        d_kout = nc.dram_tensor("d_kout", [NT, NCORE * NCH, 128, CH], BF16, kind="ExternalOutput").ap()
        d_vout = nc.dram_tensor("d_vout", [NT, NCORE * 2, 128, D], BF16, kind="ExternalOutput").ap()

    xnT_scr = nc.dram_tensor("xnT_scr", [NT, NCH, 128, CH], BF16, kind="Internal").ap()
    q_scr = nc.dram_tensor("q_scr", [NT, NCH, 128, CH], BF16, kind="Internal").ap()
    h_scr = nc.dram_tensor("h_scr", [NT, CH, D], F32, kind="Internal").ap()
    kin = [nc.dram_tensor(f"kin_{t}", [NCH, 128, CH], BF16, kind="Internal").ap() for t in range(NT)]
    kout = [nc.dram_tensor(f"kout_{t}", [NCORE * NCH, 128, CH], BF16, kind="Internal").ap() for t in range(NT)]
    vin = [nc.dram_tensor(f"vin_{t}", [2, 128, D], BF16, kind="Internal").ap() for t in range(NT)]
    vout = [nc.dram_tensor(f"vout_{t}", [NCORE * 2, 128, D], BF16, kind="Internal").ap() for t in range(NT)]
    mxin = nc.dram_tensor("mxin", [1, 16], F32, kind="Internal").ap()
    mxout = nc.dram_tensor("mxout", [1, 16], F32, kind="Internal").ap()

    with contextlib.ExitStack() as stack:
        E = stack.enter_context
        WCOL = 8 * NCH * 1024
        pbf = E(nc.sbuf_tensor("pbf", [128, WCOL + 11264], BF16))
        pf = E(nc.sbuf_tensor("pf", [128, 11520], F32))
        masks = E(nc.sbuf_tensor("masks_sb", [128, 16 * CH], BF16))
        stats = E(nc.sbuf_tensor("stats", [128, 64], F32))
        negc = E(nc.sbuf_tensor("negc", [128, 32], F32))
        linv = E(nc.sbuf_tensor("linv", [128, 4], F32))
        bkq = E(nc.sbuf_tensor("bkq_sb", [128, 2 * NCH], F32))
        incol = E(nc.sbuf_tensor("incol_sb", [128, NCH], F32))
        incb = E(nc.sbuf_tensor("incb", [128, NCH], BF16))
        ident = E(nc.sbuf_tensor("ident_sb", [128, 128], BF16))
        ocol = E(nc.sbuf_tensor("ocol_sb", [128, 1], BF16))
        orf = E(nc.sbuf_tensor("orf", [1, 128], F32))
        orb = E(nc.sbuf_tensor("orb", [1, 128], BF16))
        psA = E(nc.psum_tensor("psA", [128, 1024], F32))
        psB = E(nc.psum_tensor("psB", [128, 1024], F32))
        psC = E(nc.psum_tensor("psC", [128, 1024], F32))
        psX = E(nc.psum_tensor("psX", [128, 512], F32))
        psY = E(nc.psum_tensor("psY", [128, 512], F32))

        def wslot(i):
            return pbf[:, i * NCH * 1024:(i + 1) * NCH * 1024]
        WK, WQ, WV = wslot(0), wslot(1), wslot(2)
        FW = {"fT": wslot(0), "iT": wslot(1), "cT": wslot(2), "ufT": wslot(3),
              "uiT": wslot(4), "ucT": wslot(5), "oT": wslot(6), "uoT": wslot(7)}
        qbufs = [pbf[:, 0:2048], pbf[:, 2048:4096]]
        ktile = [pbf[:, 4096 + k * 2048: 4096 + (k + 1) * 2048] for k in range(4)]
        vtile = [pbf[:, 12288 + k * 2048: 12288 + (k + 1) * 2048] for k in range(4)]
        pb = [pbf[:, 20480 + k * CH: 20480 + (k + 1) * CH] for k in range(2)]
        xn0T = pbf[:, WCOL + 0: WCOL + 2048]
        xn1T = pbf[:, WCOL + 2048: WCOL + 4096]
        xn_tok = xn1T
        sqb = xn1T
        c1T = pbf[:, WCOL + 4096: WCOL + 5120]
        c2T = pbf[:, WCOL + 5120: WCOL + 6144]
        c1bf = pbf[:, WCOL + 6144: WCOL + 7168]
        keb = [pbf[:, WCOL + 4096 + i * CH: WCOL + 4096 + (i + 1) * CH] for i in range(2)]
        veb = [pbf[:, WCOL + 4608 + i * 512: WCOL + 4608 + (i + 1) * 512] for i in range(2)]
        bvrow = pbf[0:1, WCOL + 7168: WCOL + 8192]
        bprow = {"f": pbf[0:1, WCOL + 7168: WCOL + 8192],
                 "i": pbf[0:1, WCOL + 8192: WCOL + 9216],
                 "c": pbf[0:1, WCOL + 9216: WCOL + 10240]}
        borow = pbf[0:1, WCOL + 10240: WCOL + 11264]

        xa = pf[:, 0:2048]
        xa2 = pf[:, 7168:9216]
        gate = [pf[:, 0:1024], pf[:, 1024:2048], pf[:, 2048:3072]]
        sqd = pf[:, 3072:4096]
        c1f = pf[:, 3072:4096]
        c2f = pf[:, 4096:5120]
        stg = pf[:, 5120:6144]
        stgs = [pf[:, 5120:6144], pf[:, 4096:5120]]
        c0b = pf[:, 6144:7168]
        h0f = pf[:, 7168:8192]
        h1f = pf[:, 8192:9216]
        increg = pf[0:1, 9216:10240]
        brow_f = pf[0:1, 10240:11264]
        accv = pf[:, 0:2048]
        xrt = pf[:, 2048:4096]
        hrt = pf[:, 4096:6144]
        lrow = pf[0:1, 6144:6400]
        mxsb = pf[0:1, 11264:11280]
        garow = pf[0:1, 11280:11296]
        sqga = pf[0:1, 11296:11312]
        gaexp = pf[0:1, 11312:11344]
        prodr = pf[0:1, 11344:11376]
        negrow = pf[0:1, 11376:11408]
        sq1r = pf[0:1, 11408:11440]

        S = Sched()
        war = {}

        def MM(key, out, lhsT, rhs, start, stop, waits=()):
            S.op("tensor", key,
                 lambda e, o=out, l=lhsT, r=rhs, a=start, b=stop:
                 e.matmul(o, l, r, start=a, stop=b, skip_group_check=True), waits)

        def ACT(eng_key, out, in_, func, bias=None, scale=None, accum_out=None, waits=()):
            kw = {}
            if bias is not None:
                kw["bias"] = bias
            if scale is not None:
                kw["scale"] = scale
            if accum_out is not None:
                kw["accum_out"] = accum_out
            S.op("scalar", eng_key,
                 lambda e, o=out, i=in_, f=func, k=kw: e.activation(out=o, in_=i, func=f, **k), waits)

        def CP(key, out, in_, waits=()):
            S.op("scalar", key, lambda e, o=out, i=in_: e.copy(o, i), waits)

        def VCP(key, out, in_, waits=()):
            S.op("vector", key, lambda e, o=out, i=in_: e.tensor_copy(o, i), waits)

        def TT(key, out, in0, in1, op, waits=()):
            S.op("vector", key, lambda e, o=out, a=in0, b=in1, p=op: e.tensor_tensor(out=o, in0=a, in1=b, op=p), waits)

        def TS(key, out, in0, s1, s2, op0, op1=None, waits=()):
            s1 = float(s1) if isinstance(s1, np.floating) else s1
            s2 = float(s2) if isinstance(s2, np.floating) else s2
            if op1 is None:
                S.op("vector", key, lambda e, o=out, i=in0, a=s1: e.tensor_scalar(out=o, in0=i, scalar1=a, scalar2=None, op0=ALU.mult), waits)
            else:
                S.op("vector", key, lambda e, o=out, i=in0, a=s1, b=s2, p0=op0, p1=op1: e.tensor_scalar(out=o, in0=i, scalar1=a, scalar2=b, op0=p0, op1=p1), waits)

        S.dma("sync", "c_ident", ident[:], ident_in)
        S.dma("sync", "c_ocol", ocol[:], ocol_in)
        S.dma("sync", "c_orf", orf[:], orf_in)
        S.dma("sync", "c_orb", orb[:], orb_in)
        S.dma("sync", "c_masks", masks[:].rearrange("p (t q) -> p t q", t=16),
              masks_in.rearrange("t p q -> p t q"))
        S.dma("sync", "c_bkq", bkq[:], bkq_in)
        S.dma("sync", "c_incol", incol[:], incol_in)
        S.dma("sync", "c_bvf", brow_f, brows_in[0:1, :])
        VCP("c_bv", bvrow, brow_f, waits=["c_bvf"])
        prev_cast = None
        for m in ("kT", "qT", "vT"):
            slot = {"kT": WK, "qT": WQ, "vT": WV}[m]
            for ic in range(NCH):
                S.dma("gpsimd", f"wl_{m}_{ic}", stg[:, 0:1024], wT[m][ic * 128:(ic + 1) * 128, :],
                      waits=[prev_cast])
                CP(f"wc_{m}_{ic}", slot[:, ic * 1024:(ic + 1) * 1024], stg[:, 0:1024],
                   waits=[f"wl_{m}_{ic}"])
                prev_cast = f"wc_{m}_{ic}"

        tp_ps = [psX, psY]
        prev_cc = None
        for t in range(NT):
            b, u = divmod(t, NU)
            xau = [xa, xa2][t % 2]
            S.dma("sync", f"x_{t}", xau[:].rearrange("p (s d) -> p s d", s=2),
                  x_own[t].rearrange("(s p) d -> p s d", s=2),
                  waits=[war.get(f"xa_sc_{t % 2}"), war.get(f"xa_ve_{t % 2}")])
            for st in range(2):
                base = (t % 2) * 16 + st * 8
                s_sum, s_sq = stats[:, base:base + 1], stats[:, base + 1:base + 2]
                s_m, s_2d = stats[:, base + 2:base + 3], stats[:, base + 3:base + 4]
                s_nv, s_rs = stats[:, base + 4:base + 5], stats[:, base + 5:base + 6]
                s_mn = stats[:, base + 6:base + 7]
                xs = xau[:, st * 1024:(st + 1) * 1024]
                S.op("vector", f"lns_{t}_{st}",
                     lambda e, o=s_sum, i=xs: e.reduce_sum(o, i, axis=AX.X),
                     waits=[f"x_{t}", war.get(f"st_{(t % 2)}_{st}")])
                ACT(f"lnq_{t}_{st}", sqd, xs, AF.Square, accum_out=s_sq,
                    waits=[f"x_{t}", war.get(f"st_{(t % 2)}_{st}")])
                TS(f"lnm_{t}_{st}", s_m, s_sum, f32(1.0 / D), None, ALU.mult,
                   waits=[f"lns_{t}_{st}"])
                TS(f"ln2_{t}_{st}", s_2d, s_sq, f32(1.0 / D), None, ALU.mult,
                   waits=[f"lnq_{t}_{st}"])
                S.op("vector", f"lnv_{t}_{st}",
                     lambda e, o=s_nv, a=s_m, c=s_2d:
                     e.scalar_tensor_tensor(out=o, in0=a, scalar=a, in1=c, op0=ALU.mult, op1=ALU.subtract),
                     waits=[f"lnm_{t}_{st}", f"ln2_{t}_{st}"])
                s_sd = stats[:, base + 7:base + 8]
                ACT(f"lsd_{t}_{st}", s_sd, s_nv, AF.Sqrt, bias=f32(1e-5), scale=f32(-1.0),
                    waits=[f"lnv_{t}_{st}"])
                S.op("vector", f"lnr_{t}_{st}",
                     lambda e, o=s_rs, i=s_sd: e.reciprocal(o, i),
                     waits=[f"lsd_{t}_{st}"])
                TS(f"lnn_{t}_{st}", s_mn, s_m, s_rs, None, ALU.mult,
                   waits=[f"lnr_{t}_{st}"])
                S.op("vector", f"xn_{t}_{st}",
                     lambda e, o=xn_tok[:, st * 1024:(st + 1) * 1024], i=xs, a=s_rs, b=s_mn:
                     e.tensor_scalar(out=o, in0=i, scalar1=a, scalar2=b, op0=ALU.mult, op1=ALU.subtract),
                     waits=[f"lnn_{t}_{st}", war.get("xntok")])
                war[f"st_{(t % 2)}_{st}"] = f"xn_{t}_{st}"
            war[f"xa_sc_{t % 2}"] = f"xn_{t}_{1}"
            war[f"xa_ve_{t % 2}"] = f"lns_{t}_{1}"
            for idx in range(16):
                st, ic = divmod(idx, NCH)
                ps = tp_ps[idx % 2]
                MM(f"tp_{t}_{idx}", ps[:, 0:128], xn_tok[:, st * 1024 + ic * 128: st * 1024 + ic * 128 + 128],
                   ident[:], True, True,
                   waits=[f"xn_{t}_{st}", "c_ident", war.get(f"ps_{idx % 2}")])
                CP(f"tpe_{t}_{idx}", xn0T[:, ic * 256 + st * 128: ic * 256 + st * 128 + 128],
                   ps[:, 0:128], waits=[f"tp_{t}_{idx}", war.get("xnTu"), war.get("xnTu_spill")])
                war[f"ps_{idx % 2}"] = f"tpe_{t}_{idx}"
            war["xntok_pe"] = f"tp_{t}_{15}"
            war["sqb_pe"] = f"tp_{t}_{15}"
            S.dma("sync", f"xnTs_{t}", xnT_scr[t].rearrange("c p q -> p c q"),
                  xn0T[:].rearrange("p (c q) -> p c q", c=NCH),
                  waits=[f"tpe_{t}_{15}"])
            war["xnTu_spill"] = f"xnTs_{t}"
            for pj, (W, bcol, wkey) in enumerate(((WK, 0, "kT"), (WQ, 1, "qT"))):
                pn = "k" if pj == 0 else "q"
                for oc in range(NCH):
                    pi = oc % 2
                    ps = tp_ps[pi]
                    for ic in range(NCH):
                        MM(f"{pn}mm_{t}_{oc}" if ic == NCH - 1 else None,
                           ps[:, 0:256], W[:, ic * 1024 + oc * 128: ic * 1024 + oc * 128 + 128],
                           xn0T[:, ic * 256:(ic + 1) * 256], ic == 0, ic == NCH - 1,
                           waits=([f"tpe_{t}_{15}", f"wc_{wkey}_{NCH-1}", war.get(f"ps_{pi}")] if ic == 0 else []))
                    ACT(f"{pn}ev_{t}_{oc}", keb[pi], ps[:, 0:256], AF.Identity,
                        bias=bkq[:, bcol * NCH + oc: bcol * NCH + oc + 1],
                        waits=[f"{pn}mm_{t}_{oc}", "c_bkq", war.get(f"keb_{pi}")])
                    war[f"ps_{pi}"] = f"{pn}ev_{t}_{oc}"
                    dst = kin[t][oc] if pj == 0 else q_scr[t][oc]
                    S.dma("sync", f"{pn}sp_{t}_{oc}", dst, keb[pi][:, 0:256],
                          waits=[f"{pn}ev_{t}_{oc}"])
                    ACT(f"{pn}sq_{t}_{oc}", sqb[:, oc * 256:(oc + 1) * 256], keb[pi], AF.Square,
                        waits=[f"{pn}ev_{t}_{oc}", (war.get("sqb_pe") if oc == 0 else None)])
                    war[f"keb_{pi}"] = f"{pn}sp_{t}_{oc}"
                    MM(f"{pn}nmm_{t}" if oc == NCH - 1 else None,
                       psC[0:1, 0:256], ocol[:], sqb[:, oc * 256:(oc + 1) * 256],
                       oc == 0, oc == NCH - 1,
                       waits=[f"{pn}sq_{t}_{oc}"] + (["c_ocol", war.get("psC")] if oc == 0 else []))
                scol = (32 + t) if pj == 1 else (48 + t)
                S.op("vector", f"{pn}nmx_{t}",
                     lambda e, o=stats[0:1, scol:scol + 1], i=psC[0:1, 0:256]:
                     e.reduce_max(o, i, axis=AX.X),
                     waits=[f"{pn}nmm_{t}"])
                war["psC"] = f"{pn}nmx_{t}"
                war["sqb_pe"] = f"{pn}nmm_{t}"
            war["xntok"] = f"qnmm_{t}"
            for st in range(2):
                for oh in range(2):
                    pi = (st * 2 + oh) % 2
                    ps = tp_ps[pi]
                    MM(None, ps[:, 0:512], orb[:], bvrow[0:1, oh * 512:(oh + 1) * 512],
                       True, False, waits=["c_bv", "c_orb", war.get(f"ps_{pi}")])
                    for ic in range(NCH):
                        MM(f"vmm_{t}_{st}_{oh}" if ic == NCH - 1 else None,
                           ps[:, 0:512], xn0T[:, ic * 256 + st * 128: ic * 256 + st * 128 + 128],
                           WV[:, ic * 1024 + oh * 512: ic * 1024 + oh * 512 + 512],
                           False, ic == NCH - 1,
                           waits=([f"wc_vT_{NCH-1}"] if ic == 0 else []))
                    CP(f"vev_{t}_{st}_{oh}", veb[pi][:, 0:512], ps[:, 0:512],
                       waits=[f"vmm_{t}_{st}_{oh}", war.get(f"veb_{pi}")])
                    war[f"ps_{pi}"] = f"vev_{t}_{st}_{oh}"
                    S.dma("sync", f"vsp_{t}_{st}_{oh}", vin[t][st][:, oh * 512:(oh + 1) * 512],
                          veb[pi][:, 0:512], waits=[f"vev_{t}_{st}_{oh}"])
                    war[f"veb_{pi}"] = f"vsp_{t}_{st}_{oh}"
            war["xnTu"] = f"vmm_{t}_{1}_{1}"
            S.cc(f"ccK_{t}", lambda e, a=kin[t], o=kout[t]: e.collective_compute(
                "AllGather", ALU.bypass, replica_groups=RG, ins=[a.opt()], outs=[o.opt()]),
                waits=[f"ksp_{t}_{NCH-1}", prev_cc])
            S.cc(f"ccV_{t}", lambda e, a=vin[t], o=vout[t]: e.collective_compute(
                "AllGather", ALU.bypass, replica_groups=RG, ins=[a.opt()], outs=[o.opt()]),
                waits=[f"vsp_{t}_{1}_{1}", f"ccK_{t}"])
            prev_cc = f"ccV_{t}"

        a_last = NT - 1
        S.op("vector", "mx0", lambda e: e.memset(mxsb, 0.0))
        for b in range(2):
            S.op("vector", f"mxb_{b}",
                 lambda e, o=mxsb[0:1, b:b + 1], i=stats[0:1, 48 + b * NU: 48 + (b + 1) * NU]:
                 e.reduce_max(o, i, axis=AX.X),
                 waits=["mx0", f"knmx_{b * NU + NU - 1}"])
        S.dma("sync", "mxsp", mxin, mxsb, waits=["mxb_0", "mxb_1"])
        S.cc("ccAR", lambda e: e.collective_compute(
            "AllReduce", ALU.max, replica_groups=RG, ins=[mxin.opt()], outs=[mxout.opt()]),
            waits=["mxsp", prev_cc])
        S.dma("sync", "gald", garow, mxout, waits=["ccAR"])

        FORDER = ("ufT", "uiT", "ucT", "fT", "iT", "cT", "oT", "uoT")
        a_pe_done = {"fT": f"kmm_{a_last}_{NCH-1}", "iT": f"qmm_{a_last}_{NCH-1}",
                     "cT": f"vmm_{a_last}_{1}_{1}"}
        prev_fcast = {0: "wc_vT_%d" % (NCH - 1), 1: None}
        nchunk = 0
        last_odd_cast = None
        for m in FORDER:
            for ic in range(NCH):
                par = (nchunk % 2) if nchunk < 40 else 0
                S.dma("sync", f"fwl_{m}_{ic}", stgs[par][:, 0:1024], wT[m][ic * 128:(ic + 1) * 128, :],
                      waits=[prev_fcast[par]])
                VCP(f"fwc_{m}_{ic}", FW[m][:, ic * 1024:(ic + 1) * 1024], stgs[par][:, 0:1024],
                    waits=[f"fwl_{m}_{ic}"] + ([a_pe_done[m]] if (m in a_pe_done and ic == 0) else []))
                prev_fcast[par] = f"fwc_{m}_{ic}"
                if par == 1:
                    last_odd_cast = f"fwc_{m}_{ic}"
                nchunk += 1
        CP("incb", incb[:], incol[:], waits=["c_incol"])
        S.dma("sync", "incld", increg, incrow_in)
        for mi, m in enumerate(("f", "i", "c")):
            um = ("ufT", "uiT", "ucT")[mi]
            S.dma("sync", f"brl_{m}", brow_f, brows_in[1 + mi:2 + mi, :],
                  waits=["c_bv" if mi == 0 else f"bp_{('f','i','c')[mi-1]}_{1}"])
            for oh in range(2):
                for ic in range(NCH):
                    MM(f"zmm_{m}_{oh}" if ic == NCH - 1 else None,
                       psX[0:1, 0:512], incb[:, ic:ic + 1],
                       FW[um][:, ic * 1024 + oh * 512: ic * 1024 + oh * 512 + 512],
                       ic == 0, ic == NCH - 1,
                       waits=(["incb", f"fwc_{um}_{NCH-1}", war.get("ps_0"),
                               (f"bp_{m}_{0}" if oh == 1 else None)] if ic == 0 else []))
                TT(f"bp_{m}_{oh}", bprow[m][0:1, oh * 512:(oh + 1) * 512],
                   psX[0:1, 0:512], brow_f[0:1, oh * 512:(oh + 1) * 512], ALU.add,
                   waits=[f"zmm_{m}_{oh}", f"brl_{m}",
                          (war.get("bvrow_pe") if m == "f" and oh == 0 else None)])
            war["ps_0"] = f"bp_{m}_{1}"
        S.dma("sync", "brl_o", brow_f, brows_in[4:5, :], waits=[f"bp_c_{1}"])
        VCP("c_bo", borow, brow_f, waits=["brl_o"])
        for oh in range(2):
            MM(f"c0mm_{oh}", psY[:, 0:512], orf[:], increg[0:1, oh * 512:(oh + 1) * 512],
               True, True, waits=["incld", "c_orf", war.get("ps_1"), (f"c0b_{0}" if oh else None)])
            CP(f"c0b_{oh}", c0b[:, oh * 512:(oh + 1) * 512], psY[:, 0:512], waits=[f"c0mm_{oh}"])
            war["ps_1"] = f"c0b_{oh}"
        war["increg"] = f"c0mm_{1}"

        GPS = [psA, psB, psC]
        a_sc_done = f"qsq_{a_last}_{NCH-1}"
        a_ve_done = f"qnmx_{a_last}"
        war["h0f"] = a_sc_done
        war["c1T_pe"] = f"qsp_{a_last}_{NCH-1}"
        war["c2T_pe"] = f"vsp_{a_last}_{1}_{1}"
        war["bvrow_pe"] = f"vmm_{a_last}_{1}_{1}"
        war["xn1T_war"] = f"qnmm_{a_last}"
        for u in range(NU):
            S.dma("sync", f"fx0_{u}", xn0T[:].rearrange("p (c q) -> p c q", c=NCH),
                  xnT_scr[u].rearrange("c p q -> p c q"),
                  waits=[war["xnTu"], f"xnTs_{u}"])
            S.dma("sync", f"fx1_{u}", xn1T[:].rearrange("p (c q) -> p c q", c=NCH),
                  xnT_scr[NU + u].rearrange("c p q -> p c q"),
                  waits=[war["xn1T_war"], a_sc_done, f"xnTs_{NU + u}"])
            for st in range(2):
                sb = f"{u}_{st}"
                for g in range(3):
                    for oh in range(2):
                        MM(None, GPS[g][:, oh * 512:(oh + 1) * 512], orb[:],
                           bprow["fic"[g]][0:1, oh * 512:(oh + 1) * 512], True, False,
                           waits=[f"bp_{'fic'[g]}_{oh}", war.get(f"gps_{g}"),
                                  (a_ve_done if (u == 0 and st == 0 and g == 2) else None)])
                for ic in range(NCH):
                    lh = xn0T[:, ic * 256 + st * 128: ic * 256 + st * 128 + 128]
                    for g in range(3):
                        wm = FW[("fT", "iT", "cT")[g]]
                        for oh in range(2):
                            MM(f"g1m_{sb}_{g}" if (ic == NCH - 1 and oh == 1) else None,
                               GPS[g][:, oh * 512:(oh + 1) * 512], lh,
                               wm[:, ic * 1024 + oh * 512: ic * 1024 + oh * 512 + 512],
                               False, ic == NCH - 1,
                               waits=([f"fx0_{u}", f"fwc_{('fT','iT','cT')[g]}_{NCH-1}"]
                                      if (ic == 0 and oh == 0) else []))
                for g in range(3):
                    ACT(f"g1e_{sb}_{g}", gate[g], GPS[g][:],
                        AF.Sigmoid if g < 2 else AF.Tanh,
                        waits=[f"g1m_{sb}_{g}", war.get(f"gate_{g}"),
                               (a_ve_done if (u == 0 and st == 0) else None)])
                    war[f"gps_{g}"] = f"g1e_{sb}_{g}"
                TT(f"ct1_{sb}", c1f, gate[0], c0b, ALU.mult,
                   waits=[f"g1e_{sb}_{0}", f"c0b_{1}", war.get("c1f"), (a_sc_done if (u == 0 and st == 0) else None)])
                TT(f"ct2_{sb}", h0f, gate[1], gate[2], ALU.mult,
                   waits=[f"g1e_{sb}_{1}", f"g1e_{sb}_{2}", war.get("h0f")])
                TT(f"c1_{sb}", c1f, c1f, h0f, ALU.add, waits=[f"ct1_{sb}", f"ct2_{sb}"])
                CP(f"c1b_{sb}", c1bf[:], c1f, waits=[f"c1_{sb}", war.get("c1bf")])
                for ic in range(NCH):
                    ps = tp_ps[ic % 2]
                    MM(f"c1tp_{sb}_{ic}", ps[:, 0:128], c1bf[:, ic * 128:(ic + 1) * 128],
                       ident[:], True, True, waits=[f"c1b_{sb}", war.get(f"ps_{ic % 2}")])
                    CP(f"c1T_{sb}_{ic}", c1T[:, ic * 128:(ic + 1) * 128], ps[:, 0:128],
                       waits=[f"c1tp_{sb}_{ic}", war.get("c1T_pe")])
                    war[f"ps_{ic % 2}"] = f"c1T_{sb}_{ic}"
                for g in range(3):
                    for oh in range(2):
                        MM(None, GPS[g][:, oh * 512:(oh + 1) * 512], orb[:],
                           bprow["fic"[g]][0:1, oh * 512:(oh + 1) * 512], True, False,
                           waits=[war.get(f"gps_{g}")])
                for ic in range(NCH):
                    lh = xn1T[:, ic * 256 + st * 128: ic * 256 + st * 128 + 128]
                    for g in range(3):
                        wm = FW[("fT", "iT", "cT")[g]]
                        for oh in range(2):
                            MM(None, GPS[g][:, oh * 512:(oh + 1) * 512], lh,
                               wm[:, ic * 1024 + oh * 512: ic * 1024 + oh * 512 + 512],
                               False, False,
                               waits=([f"fx1_{u}"] if (ic == 0 and g == 0 and oh == 0) else []))
                for mc in range(NCH):
                    lh = c1T[:, mc * 128:(mc + 1) * 128]
                    for g in range(3):
                        um = FW[("ufT", "uiT", "ucT")[g]]
                        for oh in range(2):
                            MM(f"g2m_{sb}_{g}" if (mc == NCH - 1 and oh == 1) else None,
                               GPS[g][:, oh * 512:(oh + 1) * 512], lh,
                               um[:, mc * 1024 + oh * 512: mc * 1024 + oh * 512 + 512],
                               False, mc == NCH - 1,
                               waits=([f"c1T_{sb}_{NCH-1}"] if (mc == 0 and g == 0 and oh == 0) else []))
                for g in range(3):
                    ACT(f"g2e_{sb}_{g}", gate[g], GPS[g][:],
                        AF.Sigmoid if g < 2 else AF.Tanh,
                        waits=[f"g2m_{sb}_{g}", f"c1_{sb}" if g == 0 else None,
                               f"ct2_{sb}" if g > 0 else None])
                    war[f"gps_{g}"] = f"g2e_{sb}_{g}"
                TT(f"dt1_{sb}", c2f, gate[0], c1f, ALU.mult,
                   waits=[f"g2e_{sb}_{0}", war.get("c2f"),
                          (last_odd_cast if (u == 0 and st == 0) else None)])
                TT(f"dt2_{sb}", h1f, gate[1], gate[2], ALU.mult,
                   waits=[f"g2e_{sb}_{1}", f"g2e_{sb}_{2}", war.get("h1f")])
                TT(f"c2_{sb}", c2f, c2f, h1f, ALU.add, waits=[f"dt1_{sb}", f"dt2_{sb}"])
                war["c1f"] = f"dt1_{sb}"
                CP(f"c2b_{sb}", c1bf[:], c2f, waits=[f"c2_{sb}", f"c1tp_{sb}_{NCH-1}"])
                war["c1bf"] = f"c2b_{sb}"
                for ic in range(NCH):
                    ps = tp_ps[ic % 2]
                    MM(f"c2tp_{sb}_{ic}", ps[:, 0:128], c1bf[:, ic * 128:(ic + 1) * 128],
                       ident[:], True, True, waits=[f"c2b_{sb}", war.get(f"ps_{ic % 2}")])
                    CP(f"c2T_{sb}_{ic}", c2T[:, ic * 128:(ic + 1) * 128], ps[:, 0:128],
                       waits=[f"c2tp_{sb}_{ic}", war.get("c2T_pe")])
                    war[f"ps_{ic % 2}"] = f"c2T_{sb}_{ic}"
                war["c1bf"] = f"c2tp_{sb}_{NCH-1}"
                war["c1T_pe"] = f"g2m_{sb}_{2}"
                for hb in range(2):
                    ps = GPS[hb]
                    xnsrc = xn0T if hb == 0 else xn1T
                    hslot = h0f if hb == 0 else h1f
                    for oh in range(2):
                        MM(None, ps[:, oh * 512:(oh + 1) * 512], orb[:],
                           borow[0:1, oh * 512:(oh + 1) * 512], True, False,
                           waits=["c_bo", war.get(f"gps_{hb}")])
                        for ic in range(NCH):
                            MM(None, ps[:, oh * 512:(oh + 1) * 512],
                               xnsrc[:, ic * 256 + st * 128: ic * 256 + st * 128 + 128],
                               FW["oT"][:, ic * 1024 + oh * 512: ic * 1024 + oh * 512 + 512],
                               False, False,
                               waits=([f"fwc_oT_{NCH-1}"] if (oh == 0 and ic == 0) else []))
                        for mc in range(NCH):
                            MM(f"hm_{sb}_{hb}" if (oh == 1 and mc == NCH - 1) else None,
                               ps[:, oh * 512:(oh + 1) * 512], c2T[:, mc * 128:(mc + 1) * 128],
                               FW["uoT"][:, mc * 1024 + oh * 512: mc * 1024 + oh * 512 + 512],
                               False, mc == NCH - 1,
                               waits=([f"c2T_{sb}_{NCH-1}", f"fwc_uoT_{NCH-1}"] if (oh == 0 and mc == 0) else []))
                    ACT(f"he_{sb}_{hb}", hslot, ps[:], AF.Tanh,
                        waits=[f"hm_{sb}_{hb}", (f"c2_{sb}" if hb == 1 else f"c1_{sb}"),
                               war.get("h0sp" if hb == 0 else "h1sp")])
                    war[f"gps_{hb}"] = f"he_{sb}_{hb}"
                    S.dma("sync", f"hsp_{sb}_{hb}", h_scr[hb * NU + u][st * 128:(st + 1) * 128, :],
                          hslot, waits=[f"he_{sb}_{hb}"])
                    war["h0sp" if hb == 0 else "h1sp"] = f"hsp_{sb}_{hb}"
                war["h0f"] = f"hsp_{sb}_{0}"
                war["h1f"] = f"hsp_{sb}_{1}"
                war["c2f"] = f"hm_{sb}_{1}"
                war["c2T_pe"] = f"hm_{sb}_{1}"
                war["gate_0"] = f"dt1_{sb}"
                war["gate_1"] = f"dt2_{sb}"
                war["gate_2"] = f"dt2_{sb}"
            war["xnTu"] = f"hm_{u}_{1}_{0}"
            war["xn1T_war"] = f"hm_{u}_{1}_{1}"
        ffn_pe_done = f"hm_{NU-1}_{1}_{1}"
        ffn_sc_done = f"he_{NU-1}_{1}_{1}"
        ffn_ve_done = f"c2_{NU-1}_{1}"

        ACT("sq1", sq1r[0:1, 0:NT], stats[0:1, 32:32 + NT], AF.Sqrt,
            waits=[f"qnmx_{a_last}"])
        ACT("sqga", sqga[0:1, 0:2], garow[0:1, 0:2], AF.Sqrt, waits=["gald"])
        for b in range(2):
            S.op("vector", f"gax_{b}",
                 lambda e, o=gaexp[0:1, b * NU:(b + 1) * NU], i=orf[0:1, 0:NU], s=sqga[0:1, b:b + 1]:
                 e.tensor_scalar(out=o, in0=i, scalar1=s, scalar2=None, op0=ALU.mult),
                 waits=["sqga", "c_orf"])
        TT("prod", prodr[0:1, 0:NT], sq1r[0:1, 0:NT], gaexp[0:1, 0:NT], ALU.mult,
           waits=["gax_0", "gax_1", "sq1"])
        TS("negrow", negrow[0:1, 0:NT], prodr[0:1, 0:NT], f32(-1.0 / 32.0), None, ALU.mult,
           waits=["prod"])
        for t in range(NT):
            MM(f"ncb_{t}", psY[:, 0:1], orf[:], negrow[0:1, t:t + 1], True, True,
               waits=["negrow", war.get("ps_1"), ffn_pe_done if t == 0 else None])
            CP(f"ncc_{t}", negc[:, t:t + 1], psY[:, 0:1], waits=[f"ncb_{t}"])
            war["ps_1"] = f"ncc_{t}"

        war["stp_0"] = war.get("ps_0")
        war["stp_1"] = war.get("ps_0")
        war["pvacc"] = ffn_sc_done
        war["psC"] = war.get("gps_2")
        for _k in range(4):
            war[f"kt_{_k}"] = ffn_pe_done
        for _k in range(4):
            war[f"vt_{_k}"] = ffn_pe_done
        war["pb_0"] = ffn_pe_done
        war["pb_1"] = ffn_pe_done
        war["qbuf_0"] = ffn_pe_done
        war["qbuf_1"] = ffn_pe_done
        pbkey = {}
        for t in range(NT):
            b, u = divmod(t, NU)
            nk = (u + 1) * 16
            qbuf = qbufs[t % 2]
            S.dma("sync", f"aq_{t}", qbuf[:].rearrange("p (c q) -> p c q", c=NCH),
                  q_scr[t].rearrange("c p q -> p c q"),
                  waits=[war[f"qbuf_{t % 2}"]])
            S.op("vector", f"az_{t}", lambda e: e.memset(accv, 0.0),
                 waits=[war.get("accv") or ffn_ve_done,
                        (ffn_sc_done if t == 0 else None)])
            def emit_lpv(kc, t=t, u=u, nk=nk):
                sh = kc % 2
                kci = (kc // 2) % 4
                pk = pbkey[(t, kc)]
                MM(f"lm_{t}" if kc == nk - 1 else None, psC[0:1, 0:256], ocol[:], pb[sh],
                   kc == 0, kc == nk - 1,
                   waits=[pk] + ([war.get("psC"), "c_ocol"] if kc == 0 else []))
                for qs in range(2):
                    for oh in range(2):
                        MM(f"pv_{t}_{kc}" if (qs == 1 and oh == 1) else None,
                           GPS[qs][:, oh * 512:(oh + 1) * 512],
                           pb[sh][:, qs * 128:(qs + 1) * 128],
                           vtile[kci][:, (kc % 2) * 1024 + oh * 512:(kc % 2) * 1024 + oh * 512 + 512],
                           kc % 4 == 0, kc % 4 == 3,
                           waits=([f"avt_{t}_{kc // 2}"] if (qs == 0 and oh == 0) else []) +
                                 ([war.get("pvacc")] if (qs == 0 and oh == 0 and kc % 4 == 0) else []))
                war[f"pb_{sh}"] = f"pv_{t}_{kc}"
                war[f"vt_{kci}"] = f"pv_{t}_{kc}"
                if kc % 4 == 3:
                    blk = kc // 4
                    for seg in range(4):
                        TT(f"aa_{t}_{blk}" if seg == 3 else None,
                           accv[:, seg * 512:(seg + 1) * 512], accv[:, seg * 512:(seg + 1) * 512],
                           GPS[seg // 2][:, (seg % 2) * 512:(seg % 2 + 1) * 512], ALU.add,
                           waits=([f"pv_{t}_{kc}", f"az_{t}"] if seg == 0 else []))
                    war["pvacc"] = f"aa_{t}_{blk}"

            for kc in range(nk):
                c256, hh = divmod(kc, 2)
                uu, r = divmod(c256, NCORE)
                kci = kc % 8
                if hh == 0:
                    ks = c256 % 4
                    S.dma("sync", f"akt_{t}_{c256}",
                          ktile[ks][:].rearrange("p (c q) -> p c q", c=NCH),
                          kout[b * NU + uu][r * NCH:(r + 1) * NCH].rearrange("c p q -> p c q"),
                          waits=[f"ccK_{b * NU + uu}", war.get(f"kt_{ks}")])
                    S.dma("gpsimd", f"avt_{t}_{c256}",
                          vtile[ks][:].rearrange("p (s d) -> p s d", s=2),
                          vout[b * NU + uu][r * 2:(r + 1) * 2].rearrange("s p d -> p s d"),
                          waits=[f"ccV_{b * NU + uu}", war.get(f"vt_{ks}")])
                sh = kc % 2
                stp = psX[:, sh * 256:(sh + 1) * 256]
                ks = c256 % 4
                for oc in range(NCH):
                    MM(f"qk_{t}_{kc}" if oc == NCH - 1 else None, stp,
                       ktile[ks][:, oc * 256 + hh * 128: oc * 256 + hh * 128 + 128],
                       qbuf[:, oc * 256:(oc + 1) * 256], oc == 0, oc == NCH - 1,
                       waits=([f"akt_{t}_{c256}", f"aq_{t}", war.get(f"stp_{sh}")] if oc == 0 else []))
                ACT(f"ex_{t}_{kc}", pb[sh], stp, AF.Exp, bias=negc[:, t:t + 1],
                    scale=f32(1.0 / 32.0),
                    waits=[f"qk_{t}_{kc}", f"ncc_{t}", war.get(f"pb_{sh}")])
                war[f"stp_{sh}"] = f"ex_{t}_{kc}"
                pkey = f"ex_{t}_{kc}"
                if kc >= u * 16:
                    mi = kc - u * 16
                    TT(f"mk_{t}_{kc}", pb[sh], pb[sh], masks[:, mi * 256:(mi + 1) * 256],
                       ALU.mult, waits=[pkey, "c_masks"])
                    pkey = f"mk_{t}_{kc}"
                pbkey[(t, kc)] = pkey
                war[f"kt_{ks}"] = f"qk_{t}_{kc}"
                emit_lpv(kc)
            war[f"qbuf_{t % 2}"] = f"qk_{t}_{nk-1}"
            VCP(f"lr_{t}", lrow, psC[0:1, 0:256],
                waits=[f"lm_{t}", war.get("lrow"), (ffn_sc_done if t == 0 else None)])
            war["psC"] = f"lr_{t}"
            for qs in range(2):
                MM(f"ltp_{t}_{qs}", psY[:, 0:1], lrow[0:1, qs * 128:(qs + 1) * 128],
                   orf[0:1, 0:1], True, True, waits=[f"lr_{t}", war.get("ps_1")])
                war["lrow"] = f"ltp_{t}_{qs}"
                S.op("vector", f"li_{t}_{qs}",
                     lambda e, o=linv[:, qs:qs + 1], i=psY[:, 0:1]: e.reciprocal(o, i),
                     waits=[f"ltp_{t}_{qs}"])
                war["ps_1"] = f"li_{t}_{qs}"
                TS(f"an_{t}_{qs}", accv[:, qs * 1024:(qs + 1) * 1024],
                   accv[:, qs * 1024:(qs + 1) * 1024], linv[:, qs:qs + 1], None, ALU.mult,
                   waits=[f"li_{t}_{qs}", f"aa_{t}_{nk // 4 - 1}"])
            if DEBUG:
                S.dma("sync", f"dacc_{t}", d_acc[t].rearrange("(s p) d -> p s d", s=2),
                      accv[:].rearrange("p (s d) -> p s d", s=2), waits=[f"an_{t}_{1}"])
                S.dma("sync", f"dl_{t}", d_l[t], lrow, waits=[f"lr_{t}"])
                S.dma("sync", f"dh_{t}", d_h[t], h_scr[t], waits=[f"hsp_{u}_{1}_{b}"])
                S.dma("sync", f"dxnT_{t}", d_xnT[t], xnT_scr[t], waits=[f"xnTs_{t}"])
                S.dma("sync", f"dko_{t}", d_kout[t], kout[t], waits=[f"ccK_{t}"])
                S.dma("sync", f"dvo_{t}", d_vout[t], vout[t], waits=[f"ccV_{t}"])
            S.dma("gpsimd", f"ax_{t}", xrt[:].rearrange("p (s d) -> p s d", s=2),
                  x_own[t].rearrange("(s p) d -> p s d", s=2),
                  waits=[war.get("xrt") or ffn_ve_done, war.get("xrt_pe") or ffn_pe_done])
            S.dma("gpsimd", f"ah_{t}", hrt[:].rearrange("p (s d) -> p s d", s=2),
                  h_scr[t].rearrange("(s p) d -> p s d", s=2),
                  waits=[f"hsp_{u}_{1}_{b}", war.get("hrt") or ffn_ve_done,
                         (ffn_pe_done if t == 0 else None)])
            TT(f"ao1_{t}", accv, accv, xrt, ALU.add,
               waits=[f"an_{t}_{1}", f"ax_{t}"] + ([f"dacc_{t}"] if DEBUG else []))
            TT(f"ao2_{t}", accv, accv, hrt, ALU.add, waits=[f"ao1_{t}", f"ah_{t}"])
            war["xrt"] = f"ao1_{t}"
            war["hrt"] = f"ao2_{t}"
            war["xrt_pe"] = f"ao1_{t}"
            S.dma("gpsimd", f"out_{t}", y_own[t].rearrange("(s p) d -> p s d", s=2),
                  accv[:].rearrange("p (s d) -> p s d", s=2), waits=[f"ao2_{t}"])
            war["accv"] = f"out_{t}"
        if DEBUG:
            S.dma("sync", "dnegc", d_negc, negc[:], waits=[f"ncc_{NT - 1}"])
            S.dma("sync", "dstats", d_stats, stats[:], waits=[f"ncc_{NT - 1}"])
            S.dma("sync", "drows", d_rows, pf[0:1, 11264:11440], waits=[f"ncc_{NT - 1}"])
        S.op("sync", None, lambda e: e.nop(), waits=[f"out_{NT - 1}"] + (["dnegc"] if DEBUG else []))

        S.finalize(nc, stack, serial=SERIAL)
    return nc


def kernel(L0_attn_x, L0_attn_Wk, L0_attn_bk, L0_attn_Wq, L0_attn_bq, L0_attn_Wv,
           L0_attn_bv, L0_ffn_wf, L0_ffn_uf, L0_ffn_bf, L0_ffn_wi, L0_ffn_ui,
           L0_ffn_bi, L0_ffn_wc, L0_ffn_uc, L0_ffn_bc, L0_ffn_wo, L0_ffn_uo,
           L0_ffn_bo, L0_ffn_in_c):
    x = np.asarray(L0_attn_x, dtype=np.float32)
    B, Sq, Dm = x.shape
    assert B == 2 and Dm == D and Sq % (NCORE * CH) == 0
    NU = Sq // (NCORE * CH)

    def wt(a):
        return np.ascontiguousarray(np.asarray(a, dtype=np.float32).T)

    weights = {"w_kT": wt(L0_attn_Wk), "w_qT": wt(L0_attn_Wq), "w_vT": wt(L0_attn_Wv),
               "w_fT": wt(L0_ffn_wf), "w_iT": wt(L0_ffn_wi), "w_cT": wt(L0_ffn_wc),
               "w_ufT": wt(L0_ffn_uf), "w_uiT": wt(L0_ffn_ui), "w_ucT": wt(L0_ffn_uc),
               "w_oT": wt(L0_ffn_wo), "w_uoT": wt(L0_ffn_uo)}
    bk = np.asarray(L0_attn_bk, np.float32).reshape(NCH, 128).T
    bq = np.asarray(L0_attn_bq, np.float32).reshape(NCH, 128).T
    bkq = np.ascontiguousarray(np.concatenate([bk, bq], axis=1))
    brows = np.ascontiguousarray(np.stack([
        np.asarray(L0_attn_bv, np.float32), np.asarray(L0_ffn_bf, np.float32),
        np.asarray(L0_ffn_bi, np.float32), np.asarray(L0_ffn_bc, np.float32),
        np.asarray(L0_ffn_bo, np.float32)]))
    incrow = np.ascontiguousarray(np.asarray(L0_ffn_in_c, np.float32).reshape(1, D))
    incol = np.ascontiguousarray(np.asarray(L0_ffn_in_c, np.float32).reshape(NCH, 128).T)
    ident = np.eye(128, dtype=BFDT)
    ocolv = np.ones((128, 1), dtype=BFDT)
    orow_f = np.ones((1, 128), dtype=np.float32)
    orow_b = np.ones((1, 128), dtype=BFDT)

    xr = x.reshape(B, Sq // CH, CH, D)
    kk, qq = np.meshgrid(np.arange(128), np.arange(CH), indexing="ij")
    in_maps = []
    for j in range(NCORE):
        idx = np.arange(NU) * NCORE + j
        x_ownj = np.ascontiguousarray(xr[:, idx].reshape(2 * NU, CH, D))
        mt = np.zeros((16, 128, CH), dtype=BFDT)
        for mi in range(16):
            m, hh2 = divmod(mi, 2)
            if m < j:
                mt[mi] = 1
            elif m == j:
                mt[mi] = ((hh2 * 128 + kk) <= qq).astype(BFDT)
        im = {"x_own": x_ownj, "bkq": bkq, "brows": brows, "incrow": incrow,
              "incol": incol, "masks": mt, "ident": ident, "ocol": ocolv,
              "orow_f": orow_f, "orow_b": orow_b}
        im.update(weights)
        in_maps.append(im)

    nc = build(NU)
    res = run_bass_kernel_spmd(nc, in_maps, core_ids=list(range(NCORE)))
    out = np.empty((B, Sq, D), dtype=np.float32)
    for j in range(NCORE):
        yj = np.asarray(res.results[j]["y_own"])
        for b in range(B):
            for i in range(NU):
                c = i * NCORE + j
                out[b, c * CH:(c + 1) * CH] = yj[b * NU + i]
    return out
